# version 05
# baseline (speedup 1.0000x reference)
import numpy as np
import ml_dtypes
import concourse.bass as bass
import concourse.mybir as mybir
from concourse.alu_op_type import AluOpType
from concourse.bass_utils import run_bass_kernel_spmd
from contextlib import ExitStack

F32 = mybir.dt.float32
BF16 = mybir.dt.bfloat16
F16 = mybir.dt.float16
I32 = mybir.dt.int32
U16 = mybir.dt.uint16
U32 = mybir.dt.uint32
AF = mybir.ActivationFunctionType
OP = AluOpType
BF = ml_dtypes.bfloat16

B, S, D, H, DH = 2, 4096, 2048, 16, 128
NT = S // 128
HPC = 4
QTOK = 1024
NB = 64
JL = 32
IFFN = 5632
SC_QK = float(1.0 / np.sqrt(128.0))
BCOEF = float(np.sqrt(128.0) * 1.0001)


def build_nc(debug=False):
    nc = bass.Bass("TRN2", target_bir_lowering=False, enable_partition_id=True)

    x_b = nc.dram_tensor("x_b", [S, D], F32, kind="ExternalInput").ap()
    x_q = nc.dram_tensor("x_q", [QTOK, D], F32, kind="ExternalInput").ap()
    wq_h = nc.dram_tensor("wq_h", [512, D], F32, kind="ExternalInput").ap()
    wk_h = nc.dram_tensor("wk_h", [512, D], F32, kind="ExternalInput").ap()
    wv_h = nc.dram_tensor("wv_h", [512, D], F32, kind="ExternalInput").ap()
    wo_s = nc.dram_tensor("wo_s", [D, 256], F32, kind="ExternalInput").ap()
    gup_s = nc.dram_tensor("gup_s", [2 * IFFN, 32], I32, kind="ExternalInput").ap()
    gus_s = nc.dram_tensor("gus_s", [2, 2 * IFFN], F16, kind="ExternalInput").ap()
    dnp_s = nc.dram_tensor("dnp_s", [D, 88], I32, kind="ExternalInput").ap()
    dns_s = nc.dram_tensor("dns_s", [11, D], F16, kind="ExternalInput").ap()
    gain_s = nc.dram_tensor("gain_s", [1, 4], F32, kind="ExternalInput").ap()
    cos_t = nc.dram_tensor("cos_t", [S, 64], F32, kind="ExternalInput").ap()
    sin_t = nc.dram_tensor("sin_t", [S, 64], F32, kind="ExternalInput").ap()
    proj_c = nc.dram_tensor("proj_c", [128, JL], BF16, kind="ExternalInput").ap()
    id_b = nc.dram_tensor("id_b", [128, 128], BF16, kind="ExternalInput").ap()
    id_f = nc.dram_tensor("id_f", [128, 128], F32, kind="ExternalInput").ap()

    woT_in = nc.dram_tensor("woT_in", [256, D], BF16)
    woT_all = nc.dram_tensor("woT_all", [D, D], BF16)
    wguT_in = nc.dram_tensor("wguT_in", [256, 2 * IFFN], BF16)
    wguT_all = nc.dram_tensor("wguT_all", [D, 2 * IFFN], BF16)
    wdT_in = nc.dram_tensor("wdT_in", [704, D], BF16)
    wdT_all = nc.dram_tensor("wdT_all", [IFFN, D], BF16)
    y_in = nc.dram_tensor("y_in", [512, S], BF16)
    y_all = nc.dram_tensor("y_all", [2048, S], BF16)
    attn_d = nc.dram_tensor("attn_d", [QTOK, D], F32)

    out_q = nc.dram_tensor("out_q", [QTOK, D], F32, kind="ExternalOutput").ap()
    dbg_t = {}
    if debug:
        for nm, shp, dt in [
            ("d_qfm", [128, HPC * S], BF16), ("d_kfm", [128, HPC * S], BF16),
            ("d_vres", [128, HPC * 136 * NT], BF16),
            ("d_kp", [JL, S], F32), ("d_kpn", [JL, S], F32),
            ("d_cent", [JL, NB], F32), ("d_scores", [1, NB], F32),
            ("d_pen", [128, HPC * 32], F32), ("d_yin", [512, S], BF16),
            ("d_yall", [2048, S], BF16), ("d_attn", [QTOK, D], F32),
            ("d_x2t", [128, 16 * QTOK], BF16), ("d_wg", [D, 2 * IFFN], BF16),
            ("d_wd", [IFFN, D], BF16), ("d_hfm", [128, 44 * QTOK], BF16),
            ("d_woT", [D, D], BF16),
            ("d_x1t0", [128, D], BF16), ("d_lnc", [128, 16], F32),
            ("d_knlast", [128, 1024], BF16), ("d_kclast", [128, 16], F32),
            ("d_ksblast", [128, 1024], F32),
            ("d_qn0", [128, 512], BF16), ("d_kn0", [128, 512], BF16),
            ("d_ksb0", [128, 512], F32), ("d_qcol", [128, 16], F32),
        ]:
            dbg_t[nm] = nc.dram_tensor(nm, shp, dt, kind="ExternalOutput").ap()

    ctx = ExitStack()
    with ctx:
        SEM, CNT = {}, {}
        for s_ in ["sp", "pld", "pe", "act", "dve", "pl",
                   "cwo", "cgu", "cwd", "cy"]:
            SEM[s_] = ctx.enter_context(nc.semaphore(s_))
            CNT[s_] = 0

        def inc(instr, s_, n=1):
            instr.then_inc(SEM[s_], n)
            CNT[s_] += n
            return CNT[s_]

        def now(s_):
            return CNT[s_]

        def w(eng, s_, c=None):
            c = CNT[s_] if c is None else c
            if c > 0:
                eng.wait_ge(SEM[s_], c)

        def bar(eng, instr, s_):
            inc(instr, s_)
            eng.wait_ge(SEM[s_], CNT[s_])

        def sbp(name, shape, dt):
            return ctx.enter_context(nc.sbuf_tensor(name, shape, dt))

        identb = sbp("identb", [128, 128], BF16)
        identf = sbp("identf", [128, 128], F32)
        projb = sbp("projb", [128, JL], BF16)
        gains = sbp("gains", [1, 4], F32)
        negb = sbp("negb", [1, 4], F32)
        gbc4 = sbp("gbc4", [128, 4], F32)
        onesr = sbp("onesr", [1, 128], F32)
        indu = sbp("indu", [1, 128], F32)
        indl = sbp("indl", [1, 128], F32)
        epsln = sbp("epsln", [128, 1], F32)
        epsrms = sbp("epsrms", [128, 1], F32)
        pen_tab = sbp("pen_tab", [128, HPC * 32], F32)
        pidc = sbp("pidc", [1, 1], U32)
        scg = sbp("scg", [128, 2 * 88], F32)
        scd = sbp("scd", [128, 11 * 16], F32)
        ones32_col = sbp("ones32_col", [JL, 1], F32)

        block = ctx.enter_context(nc.Block())

        def sec(name):
            def deco(fn):
                getattr(block, name)(fn)
            return deco

        marks = {}

        def _ones32():
            return ones32_col[0:JL, 0:1]

        pA = ExitStack()
        qfm = pA.enter_context(nc.sbuf_tensor("qfm", [128, HPC * S], BF16))
        kfm = pA.enter_context(nc.sbuf_tensor("kfm", [128, HPC * S], BF16))
        vres = pA.enter_context(nc.sbuf_tensor("vres", [128, HPC * 136 * NT], BF16))
        qfmv = qfm[:].rearrange("p (h s) -> p h s", h=HPC)
        kfmv = kfm[:].rearrange("p (h s) -> p h s", h=HPC)
        vview = vres[:].rearrange("p (h t c) -> p h t c", h=HPC, t=NT)

        p01 = ExitStack()
        sb1 = lambda n, s_, d: p01.enter_context(nc.sbuf_tensor(n, s_, d))
        ps1 = lambda n, s_, d: p01.enter_context(nc.psum_tensor(n, s_, d))
        xld = sb1("xld", [128, 2 * D], F32)
        x1r = sb1("x1r", [128, 2 * D], BF16)
        x1T = sb1("x1T", [128, 2 * D], BF16)
        dump = sb1("dump", [128, D], BF16)
        lncol = sb1("lncol", [128, 2 * 8], F32)
        cosld = sb1("cosld", [128, 2 * 64], F32)
        sinld = sb1("sinld", [128, 2 * 64], F32)
        wqkvT = sb1("wqkvT", [128, 3 * 16 * 512], BF16)
        pW = ps1("pW", [128, 512], F32)
        pQ = ps1("pQ", [128, 512], F32)
        pK = ps1("pK", [128, 512], F32)
        pV = ps1("pV", [128, 512], F32)
        pX1T = ps1("pX1T", [128, 2048], BF16)
        pQKT = ps1("pQKT", [128, 1024], BF16)
        pG4 = ps1("pG4", [128, 16], F32)

        p0t = ExitStack()
        sb0 = lambda n, s_, d: p0t.enter_context(nc.sbuf_tensor(n, s_, d))
        wsl = sb0("wsl", [128, D], F32)
        wstage = sb0("wstage", [128, 2 * 512], BF16)
        g88 = sb0("g88", [88, 256], F16)
        d16 = sb0("d16", [16, 11 * 128], F16)
        cscg = sb0("cscg", [88, 128], F32)
        cscd = sb0("cscd", [16, 128], F32)

        @sec("sync")
        def _(sy):
            for dst, src in [(identb[:], id_b), (identf[:], id_f),
                             (projb[:], proj_c), (gains[:], gain_s),
                             (pidc[:], nc.partition_id_tensor.ap())]:
                inc(sy.dma_start(out=dst, in_=src), "sp", 16)
            inc(sy.dma_start(out=g88[:].rearrange("t (g q) -> t g q", g=2),
                             in_=gus_s.rearrange("g (t q) -> t g q", q=128)), "sp", 16)
            inc(sy.dma_start(out=d16[:].rearrange("t (s q) -> t s q", s=11),
                             in_=dns_s.rearrange("s (t q) -> t s q", q=128)), "sp", 16)
        marks["consts"] = now("sp")

        @sec("vector")
        def _(v):
            w(v, "sp", marks["consts"])
            v.memset(onesr[:], 1.0)
            v.memset(ones32_col[:], 1.0)
            v.memset(indu[0:1, 0:64], 1.0)
            v.memset(indu[0:1, 64:128], 0.0)
            v.memset(indl[0:1, 0:64], 0.0)
            v.memset(indl[0:1, 64:128], 1.0)
            v.memset(epsln[:], 1e-5)
            v.memset(epsrms[:], 1e-6)
            v.memset(vres[:].rearrange("p (h t c) -> p h t c", h=HPC, t=NT)[:, :, :, 128:129], 1.0)
            bar(v, v.tensor_scalar(negb[:], gains[:], -1.0, None, OP.mult), "dve")
            bar(v, v.tensor_tensor(negb[:], negb[:], gains[:], OP.max), "dve")
            inc(v.tensor_scalar(negb[:], negb[:], -BCOEF, None, OP.mult), "dve")
        marks["p0v"] = now("dve")

        @sec("tensor")
        def _(t):
            w(t, "dve", marks["p0v"])
            inc(t.matmul(pG4[:, 0:4], onesr[:], gains[:], start=True, stop=True), "pe")
        marks["pG4"] = now("pe")

        @sec("scalar")
        def _(s_):
            w(s_, "pe", marks["pG4"])
            inc(s_.copy(gbc4[:], pG4[:, 0:4]), "act")
        marks["gbc4"] = now("act")

        marks["wTevlast"] = 0
        marks["wTplast"] = 0
        for wi, wsrc in enumerate([wq_h, wk_h, wv_h]):
            for rc in range(4):
                @sec("sync")
                def _(sy, wi=wi, rc=rc, wsrc=wsrc):
                    if wi == 0 and rc == 0:
                        w(sy, "sp", marks["consts"])
                    else:
                        w(sy, "pe", marks["wTplast"])
                    inc(sy.dma_start(out=wsl[:], in_=wsrc[128 * rc:128 * (rc + 1), :]), "sp", 16)
                marks["wld"] = now("sp")

                for jb in range(4):
                    @sec("tensor")
                    def _(t, wi=wi, rc=rc, jb=jb):
                        w(t, "sp", marks["wld"])
                        w(t, "act", marks["wTevlast"])
                        for u in range(4):
                            j = 4 * jb + u
                            ins = t.transpose(pW[:, 128 * u:128 * (u + 1)],
                                              wsl[:, 128 * j:128 * (j + 1)], identf[:])
                        inc(ins, "pe")
                    marks["wTplast"] = now("pe")

                    @sec("scalar")
                    def _(s_, wi=wi, rc=rc, jb=jb):
                        w(s_, "pe", marks["wTplast"])
                        dst = wqkvT[:].rearrange("p (w ic c) -> p w ic c", w=3, ic=16)
                        inc(s_.copy(dst[:, wi, 4 * jb:4 * (jb + 1), 128 * rc:128 * (rc + 1)],
                                    pW[:, 0:512].rearrange("p (j c) -> p j c", j=4)), "act")
                    marks["wTevlast"] = now("act")
        marks["wqkvT"] = now("act")

        marks["wost"] = 0
        for L in range(4):
            @sec("sync")
            def _(sy, L=L):
                w(sy, "pe", marks["wTplast"])
                inc(sy.dma_start(out=wsl[:, 0:1024].rearrange("p (r c) -> p r c", r=4),
                                 in_=wo_s.rearrange("(l r p) c -> p l r c", p=128, r=4)[:, L, :, :]), "sp", 16)
            marks["wold"] = now("sp")
            for ih in range(2):
                @sec("tensor")
                def _(t, L=L, ih=ih):
                    w(t, "sp", marks["wold"])
                    w(t, "act", marks["wTevlast"])
                    for rc in range(4):
                        ins = t.transpose(pW[:, 128 * rc:128 * (rc + 1)],
                                          wsl[:, 0:1024].rearrange("p (r c) -> p r c", r=4)[:, rc, 128 * ih:128 * (ih + 1)],
                                          identf[:])
                    inc(ins, "pe")
                marks["wTplast"] = now("pe")

                @sec("scalar")
                def _(s_, L=L, ih=ih):
                    w(s_, "pe", marks["wTplast"])
                    par = (2 * L + ih) % 2
                    w(s_, "pld", marks["wost"])
                    inc(s_.copy(wstage[:, 512 * par:512 * (par + 1)], pW[:, 0:512]), "act")
                marks["wTevlast"] = now("act")

                @sec("gpsimd")
                def _(g_, L=L, ih=ih):
                    w(g_, "act", marks["wTevlast"])
                    par = (2 * L + ih) % 2
                    dst = bass.AP(woT_in, ih * 128 * D + 512 * L, [[D, 128], [1, 512]])
                    inc(g_.dma_start(out=dst, in_=wstage[:, 512 * par:512 * (par + 1)]), "pld", 16)
                marks["wost"] = now("pld")
        marks["wo_all_st"] = now("pld")

        for g in range(2):
            @sec("vector")
            def _(v, g=g):
                if g > 0:
                    w(v, "pe", marks["wTplast"])
                gv = g88[:].rearrange("t (g q) -> t g q", g=2)
                inc(v.tensor_copy(cscg[:], gv[:, g, :]), "dve")
            marks["scc"] = now("dve")

            @sec("tensor")
            def _(t, g=g):
                w(t, "act", marks["wTevlast"])
                w(t, "dve", marks["scc"])
                inc(t.transpose(pW[:, 0:88], cscg[:], identf[0:88, 0:88]), "pe")
            marks["wTplast"] = now("pe")

            @sec("scalar")
            def _(s_, g=g):
                w(s_, "pe", marks["wTplast"])
                dst = scg[:].rearrange("p (g t) -> p g t", g=2)
                inc(s_.copy(dst[:, g, :], pW[:, 0:88]), "act")
            marks["wTevlast"] = now("act")
        for sg in range(11):
            @sec("vector")
            def _(v, sg=sg):
                w(v, "pe", marks["wTplast"])
                dv_ = d16[:].rearrange("t (s q) -> t s q", s=11)
                inc(v.tensor_copy(cscd[:], dv_[:, sg, :]), "dve")
            marks["scc"] = now("dve")

            @sec("tensor")
            def _(t, sg=sg):
                w(t, "act", marks["wTevlast"])
                w(t, "dve", marks["scc"])
                inc(t.transpose(pW[:, 0:16], cscd[:], identf[0:16, 0:16]), "pe")
            marks["wTplast"] = now("pe")

            @sec("scalar")
            def _(s_, sg=sg):
                w(s_, "pe", marks["wTplast"])
                dst = scd[:].rearrange("p (s t) -> p s t", s=11)
                inc(s_.copy(dst[:, sg, :], pW[:, 0:16]), "act")
            marks["wTevlast"] = now("act")
        marks["scales"] = now("act")

        @sec("gpsimd")
        def _(g_):
            w(g_, "pld", marks["wo_all_st"])
            inc(g_.collective_compute(
                "AllGather", OP.bypass, replica_groups=[list(range(8))],
                ins=[woT_in.ap().opt()], outs=[woT_all.ap().opt()]), "cwo")

        p0t.close()
        ksb = sb1("ksb", [128, 2 * 512], F32)
        qtmp = sb1("qtmp", [128, 512], F32)
        ktmp = sb1("ktmp", [128, 512], F32)
        qr_s = sb1("qr_s", [128, 512], F32)
        kr_s = sb1("kr_s", [128, 512], F32)
        qn_r = sb1("qn_r", [128, 2 * 512], BF16)
        kn_r = sb1("kn_r", [128, 2 * 512], BF16)
        qcol = sb1("qcol", [128, 2 * 8], F32)
        kcol = sb1("kcol", [128, 2 * 8], F32)

        lnc = lncol[:].rearrange("p (r c) -> p r c", r=2)
        qc = qcol[:].rearrange("p (r c) -> p r c", r=2)
        kc = kcol[:].rearrange("p (r c) -> p r c", r=2)

        for i in range(NT):
            pi = i % 2

            if debug and i == 2:
                @sec("vector")
                def _(v):
                    w(v, "pld", marks["dbg_t0"])
                    v.engine_nop() if False else None
                    inc(v.memset(qtmp[0:1, 0:1], 0.0), "dve")
                @sec("scalar")
                def _(s_):
                    w(s_, "pld", marks["dbg_t0"])
                    inc(s_.copy(epsln[0:1, 0:1], epsln[0:1, 0:1]), "act")
                @sec("gpsimd")
                def _(g_):
                    w(g_, "pld", marks["dbg_t0"])
                    inc(g_.memset(ktmp[0:1, 0:1], 0.0), "pl")

            @sec("sync")
            def _(sy, i=i, pi=pi):
                if i >= 2:
                    w(sy, "act", marks[("lnacc", i - 2)])
                    w(sy, "dve", marks[("qn", i - 2)])
                    w(sy, "pl", marks[("kn", i - 2)])
                inc(sy.dma_start(out=xld[:, D * pi:D * (pi + 1)],
                                 in_=x_b[128 * i:128 * (i + 1), :]), "sp", 16)
                inc(sy.dma_start(out=cosld[:, 64 * pi:64 * (pi + 1)],
                                 in_=cos_t[128 * i:128 * (i + 1), :]), "sp", 16)
                inc(sy.dma_start(out=sinld[:, 64 * pi:64 * (pi + 1)],
                                 in_=sin_t[128 * i:128 * (i + 1), :]), "sp", 16)
            marks[("xld", i)] = now("sp")

            @sec("scalar")
            def _(s_, i=i, pi=pi):
                w(s_, "sp", marks[("xld", i)])
                if i >= 2:
                    w(s_, "dve", marks[("lncols", i - 2)])
                xt = xld[:, D * pi:D * (pi + 1)]
                s_.activation(dump[:], xt, AF.Copy, accum_out=lnc[:, pi, 0:1])
                inc(s_.activation(dump[:], xt, AF.Square, accum_out=lnc[:, pi, 1:2]), "act")
            marks[("lnacc", i)] = now("act")

            @sec("vector")
            def _(v, i=i, pi=pi):
                w(v, "act", marks[("lnacc", i)])
                v.tensor_scalar(lnc[:, pi, 2:3], lnc[:, pi, 0:1], 1.0 / D, None, OP.mult)
                bar(v, v.scalar_tensor_tensor(lnc[:, pi, 5:6], lnc[:, pi, 0:1],
                                              1.0 / (D * D), lnc[:, pi, 0:1],
                                              OP.mult, OP.mult), "dve")
                inc(v.scalar_tensor_tensor(lnc[:, pi, 3:4], lnc[:, pi, 1:2], 1.0 / D,
                                           lnc[:, pi, 5:6], OP.mult, OP.subtract), "dve")
            marks[("lncols", i)] = now("dve")

            @sec("scalar")
            def _(s_, i=i, pi=pi):
                w(s_, "dve", marks[("lncols", i)])
                bar(s_, s_.activation(lnc[:, pi, 4:5], lnc[:, pi, 3:4], AF.Ln, bias=epsln[:, 0:1]), "act")
                inc(s_.activation(lnc[:, pi, 4:5], lnc[:, pi, 4:5], AF.Exp, scale=-0.5), "act")
            marks[("lnrstd", i)] = now("act")

            @sec("vector")
            def _(v, i=i, pi=pi):
                w(v, "act", marks[("lnrstd", i)])
                if i >= 2:
                    w(v, "pe", marks[("x1Tp", i - 2)])
                xt = xld[:, D * pi:D * (pi + 1)]
                inc(v.tensor_scalar(x1r[:, D * pi:D * (pi + 1)], xt,
                                    lnc[:, pi, 2:3], lnc[:, pi, 4:5],
                                    OP.subtract, OP.mult), "dve")
            marks[("x1", i)] = now("dve")

            @sec("tensor")
            def _(t, i=i, pi=pi):
                w(t, "dve", marks[("x1", i)])
                if i >= 1:
                    w(t, "dve", marks[("x1Tev", i - 1)])
                for j in range(16):
                    ins = t.transpose(pX1T[:, 128 * j:128 * (j + 1)],
                                      x1r[:, D * pi + 128 * j: D * pi + 128 * (j + 1)],
                                      identb[:])
                inc(ins, "pe")
            marks[("x1Tp", i)] = now("pe")

            @sec("vector")
            def _(v, i=i):
                w(v, "pe", marks[("x1Tp", i)])
                if i >= 2:
                    w(v, "pe", marks[("qkv", i - 2)])
                ri = i % 2
                v.tensor_copy(x1T[:, 2048 * ri:2048 * ri + 1024], pX1T[:, 0:1024])
                inc(v.tensor_copy(x1T[:, 2048 * ri + 1024:2048 * ri + 2048], pX1T[:, 1024:2048]), "dve")
            marks[("x1Tev", i)] = now("dve")

            @sec("tensor")
            def _(t, i=i):
                w(t, "dve", marks[("x1Tev", i)])
                w(t, "act", marks["wqkvT"])
                ri = i % 2
                wv_ = wqkvT[:].rearrange("p (w ic c) -> p w ic c", w=3, ic=16)
                for wi, pp in enumerate([pQ, pK, pV]):
                    if i >= 1:
                        if wi == 0:
                            w(t, "dve", marks[("qn", i - 1)])
                        elif wi == 1:
                            w(t, "act", marks[("kev", i - 1)])
                        else:
                            w(t, "act", marks[("vev", i - 1)])
                    for j in range(16):
                        ins = t.matmul(pp[:, 0:512],
                                       x1T[:, 2048 * ri + 128 * j: 2048 * ri + 128 * (j + 1)],
                                       wv_[:, wi, j, :],
                                       start=(j == 0), stop=(j == 15))
                    inc(ins, "pe")
                    marks[("qkvg%d" % wi, i)] = now("pe")
            marks[("qkv", i)] = now("pe")

            @sec("scalar")
            def _(s_, i=i, pi=pi):
                w(s_, "pe", marks[("qkvg0", i)])
                if i >= 1:
                    w(s_, "dve", marks[("qn", i - 1)])
                inc(s_.activation(qr_s[:], pQ[:, 0:512], AF.Square), "act")
                marks[("qev", i)] = now("act")
                w(s_, "pe", marks[("qkvg1", i)])
                if i >= 2:
                    w(s_, "pl", marks[("kn", i - 2)])
                if i >= 1:
                    w(s_, "pl", marks[("knrope", i - 1)])
                s_.copy(ksb[:, 512 * pi:512 * (pi + 1)], pK[:, 0:512])
                inc(s_.activation(kr_s[:], ksb[:, 512 * pi:512 * (pi + 1)], AF.Square), "act")
                marks[("kev", i)] = now("act")
                w(s_, "pe", marks[("qkvg2", i)])
                inc(s_.copy(vview[:, :, i, 0:128], pV[:].rearrange("p (h c) -> p h c", h=4)), "act")
                marks[("vev", i)] = now("act")

            @sec("vector")
            def _(v, i=i, pi=pi):
                w(v, "act", marks[("qev", i)])
                bar(v, v.reduce_sum(qc[:, pi, 0:4],
                                    qr_s[:].rearrange("p (h c) -> p h c", h=4),
                                    axis=mybir.AxisListType.X), "dve")
                inc(v.tensor_scalar(qc[:, pi, 0:4], qc[:, pi, 0:4], 1.0 / DH, 1e-6,
                                    OP.mult, OP.add), "dve")
            marks[("qA", i)] = now("dve")

            @sec("vector")
            def _(v, i=i, pi=pi):
                w(v, "act", marks[("kev", i)])
                bar(v, v.reduce_sum(kc[:, pi, 0:4],
                                    kr_s[:].rearrange("p (h c) -> p h c", h=4),
                                    axis=mybir.AxisListType.X), "dve")
                inc(v.tensor_scalar(kc[:, pi, 0:4], kc[:, pi, 0:4], 1.0 / DH, 1e-6,
                                    OP.mult, OP.add), "dve")
            marks[("kA", i)] = now("dve")

            @sec("scalar")
            def _(s_, i=i, pi=pi):
                w(s_, "dve", marks[("qA", i)])
                w(s_, "act", marks["gbc4"])
                bar(s_, s_.activation(qc[:, pi, 0:4], qc[:, pi, 0:4], AF.Ln), "act")
                inc(s_.activation(qc[:, pi, 0:4], qc[:, pi, 0:4], AF.Exp, scale=-0.5), "act")
                marks[("qB", i)] = now("act")
                w(s_, "dve", marks[("kA", i)])
                bar(s_, s_.activation(kc[:, pi, 0:4], kc[:, pi, 0:4], AF.Ln), "act")
                inc(s_.activation(kc[:, pi, 0:4], kc[:, pi, 0:4], AF.Exp, scale=-0.5), "act")
                marks[("kB", i)] = now("act")

            @sec("vector")
            def _(v, i=i, pi=pi):
                w(v, "act", marks[("qB", i)])
                bar(v, v.tensor_tensor(qc[:, pi, 4:8], qc[:, pi, 0:4], gbc4[:, 0:4], OP.mult), "dve")
                qv = pQ[:, 0:512].rearrange("p (h two d) -> p h two d", h=4, two=2)
                rv = qr_s[:].rearrange("p (h two d) -> p h two d", h=4, two=2)
                tmp = qtmp[:].rearrange("p (h two d) -> p h two d", h=4, two=2)
                cb = cosld[:, 64 * pi:64 * (pi + 1)].rearrange("p (o d) -> p o d", o=1).broadcast_to((128, 4, 64))
                sbr = sinld[:, 64 * pi:64 * (pi + 1)].rearrange("p (o d) -> p o d", o=1).broadcast_to((128, 4, 64))
                if i >= 2:
                    w(v, "pe", marks[("qkTp", i - 2)])
                v.tensor_tensor(rv[:, :, 0, :], qv[:, :, 0, :], cb, OP.mult)
                v.tensor_tensor(rv[:, :, 1, :], qv[:, :, 1, :], cb, OP.mult)
                v.tensor_tensor(tmp[:, :, 0, :], qv[:, :, 1, :], sbr, OP.mult)
                v.tensor_tensor(tmp[:, :, 1, :], qv[:, :, 0, :], sbr, OP.mult)
                v.tensor_tensor(rv[:, :, 0, :], rv[:, :, 0, :], tmp[:, :, 0, :], OP.add)
                v.tensor_tensor(rv[:, :, 1, :], rv[:, :, 1, :], tmp[:, :, 1, :], OP.subtract)
                qnv = qn_r[:, 512 * pi:512 * (pi + 1)]
                for hh in range(4):
                    ins = v.tensor_scalar(qnv[:, 128 * hh:128 * (hh + 1)],
                                          qr_s[:, 128 * hh:128 * (hh + 1)],
                                          qc[:, pi, 4 + hh:5 + hh], None, OP.mult)
                inc(ins, "dve")
            marks[("qn", i)] = now("dve")

            @sec("gpsimd")
            def _(g_, i=i, pi=pi):
                w(g_, "act", marks[("kB", i)])
                kv = ksb[:, 512 * pi:512 * (pi + 1)].rearrange("p (h two d) -> p h two d", h=4, two=2)
                rv = kr_s[:].rearrange("p (h two d) -> p h two d", h=4, two=2)
                tmp = ktmp[:].rearrange("p (h two d) -> p h two d", h=4, two=2)
                cb = cosld[:, 64 * pi:64 * (pi + 1)].rearrange("p (o d) -> p o d", o=1).broadcast_to((128, 4, 64))
                sbr = sinld[:, 64 * pi:64 * (pi + 1)].rearrange("p (o d) -> p o d", o=1).broadcast_to((128, 4, 64))
                if i >= 2:
                    w(g_, "pe", marks[("qkTp", i - 2)])
                g_.tensor_tensor(rv[:, :, 0, :], kv[:, :, 0, :], cb, OP.mult)
                g_.tensor_tensor(rv[:, :, 1, :], kv[:, :, 1, :], cb, OP.mult)
                g_.tensor_tensor(tmp[:, :, 0, :], kv[:, :, 1, :], sbr, OP.mult)
                g_.tensor_tensor(tmp[:, :, 1, :], kv[:, :, 0, :], sbr, OP.mult)
                g_.tensor_tensor(rv[:, :, 0, :], rv[:, :, 0, :], tmp[:, :, 0, :], OP.add)
                g_.tensor_tensor(rv[:, :, 1, :], rv[:, :, 1, :], tmp[:, :, 1, :], OP.subtract)
                knv = kn_r[:, 512 * pi:512 * (pi + 1)]
                for hh in range(4):
                    ins = g_.tensor_scalar(knv[:, 128 * hh:128 * (hh + 1)],
                                           kr_s[:, 128 * hh:128 * (hh + 1)],
                                           kc[:, pi, 0 + hh:1 + hh], None, OP.mult)
                inc(ins, "pl")
            marks[("kn", i)] = now("pl")
            marks[("knrope", i)] = marks[("kn", i)]

            @sec("tensor")
            def _(t, i=i, pi=pi):
                w(t, "dve", marks[("qn", i)])
                w(t, "pl", marks[("kn", i)])
                if i >= 1:
                    w(t, "dve", marks[("qkTev", i - 1)])
                for hh in range(4):
                    t.transpose(pQKT[:, 128 * hh:128 * (hh + 1)],
                                qn_r[:, 512 * pi + 128 * hh: 512 * pi + 128 * (hh + 1)], identb[:])
                for hh in range(4):
                    ins = t.transpose(pQKT[:, 512 + 128 * hh:512 + 128 * (hh + 1)],
                                      kn_r[:, 512 * pi + 128 * hh: 512 * pi + 128 * (hh + 1)], identb[:])
                inc(ins, "pe")
            marks[("qkTp", i)] = now("pe")

            @sec("vector")
            def _(v, i=i):
                w(v, "pe", marks[("qkTp", i)])
                v.tensor_copy(qfmv[:, :, 128 * i:128 * (i + 1)],
                              pQKT[:, 0:512].rearrange("p (h c) -> p h c", h=4))
                inc(v.tensor_copy(kfmv[:, :, 128 * i:128 * (i + 1)],
                                  pQKT[:, 512:1024].rearrange("p (h c) -> p h c", h=4)), "dve")
            marks[("qkTev", i)] = now("dve")

            if debug and i == 0:
                @sec("gpsimd")
                def _(g_):
                    w(g_, "dve", marks[("qn", 0)])
                    w(g_, "pl", marks[("kn", 0)])
                    inc(g_.dma_start(out=dbg_t["d_x1t0"], in_=x1r[:, 0:D]), "pld", 16)
                    inc(g_.dma_start(out=dbg_t["d_lnc"], in_=lncol[:]), "pld", 16)
                    inc(g_.dma_start(out=dbg_t["d_qn0"], in_=qn_r[:, 0:512]), "pld", 16)
                    inc(g_.dma_start(out=dbg_t["d_kn0"], in_=kn_r[:, 0:512]), "pld", 16)
                    inc(g_.dma_start(out=dbg_t["d_ksb0"], in_=ksb[:, 0:512]), "pld", 16)
                    inc(g_.dma_start(out=dbg_t["d_qcol"], in_=qcol[:]), "pld", 16)
                marks["dbg_t0"] = now("pld")

        marks["p1done_act"] = now("act")
        marks["p1done_dve"] = now("dve")
        marks["p1done_pe"] = now("pe")

        if debug:
            @sec("gpsimd")
            def _(g_):
                w(g_, "act", marks["p1done_act"])
                inc(g_.dma_start(out=dbg_t["d_knlast"], in_=kn_r[:]), "pld", 16)
                inc(g_.dma_start(out=dbg_t["d_kclast"], in_=kcol[:]), "pld", 16)
                inc(g_.dma_start(out=dbg_t["d_ksblast"], in_=ksb[:]), "pld", 16)
                inc(g_.dma_start(out=dbg_t["d_qfm"], in_=qfm[:]), "pld", 16)
                inc(g_.dma_start(out=dbg_t["d_kfm"], in_=kfm[:]), "pld", 16)
                inc(g_.dma_start(out=dbg_t["d_vres"], in_=vres[:]), "pld", 16)
            marks["dbg_p1"] = now("pld")

        p01.close()

        p23 = ExitStack()
        sb2 = lambda n, s_, d: p23.enter_context(nc.sbuf_tensor(n, s_, d))
        ps2 = lambda n, s_, d: p23.enter_context(nc.psum_tensor(n, s_, d))
        kps = sb2("kps", [JL, S], F32)
        kpn = sb2("kpn", [JL, S], F32)
        rown = sb2("rown", [1, S], F32)
        cent = sb2("cent", [JL, NB], F32)
        centn = sb2("centn", [JL, NB], F32)
        qsum = sb2("qsum", [128, 1], F32)
        qsumb = sb2("qsumb", [128, 1], BF16)
        qp_s = sb2("qp_s", [JL, 1], F32)
        srow = sb2("srow", [1, 256], F32)
        nq_s = sb2("nq_s", [1, 4], F32)
        cmpm = sb2("cmpm", [64, 64], F32)
        cntc = sb2("cntc", [64, 2], F32)
        penr = sb2("penr", [1, 64], F32)
        nbrow = sb2("nbrow", [1, 32], F32)
        pexp = sb2("pexp", [128, 2 * 512], BF16)
        rcj = sb2("rcj", [128, 4], F32)
        ydiv = sb2("ydiv", [128, 512], BF16)
        yslab = sb2("yslab", [128, S], BF16)
        pkg = sb2("pkg", [128, 704], I32)
        dsc1 = sb2("dsc1", [128, 704], I32)
        dsc2 = sb2("dsc2", [128, 704], I32)
        dsc3 = sb2("dsc3", [128, 704], I32)
        dsc4 = sb2("dsc4", [128, 704], I32)
        dqg = sb2("dqg", [128, 5632], BF16)
        dstage = sb2("dstage", [128, 6144], BF16)
        pS0 = ps2("pS0", [128, 512], F32)
        pS1 = ps2("pS1", [128, 512], F32)
        pv_banks = [ps2(f"pv{j}", [128, 512], F32) for j in range(4)]
        pStat = ps2("pStat", [128, 512], F32)
        pYT = ps2("pYT", [128, 1024], BF16)

        marks["pYTfree"] = ("act", 0)

        dq_steps = []

        def dq_planes(eng, pkv, dqv, nib_ot, dep_marks, j0, j1):
            def fn():
                secname = "vector" if eng == "dve" else "gpsimd"

                @sec(secname)
                def _(e):
                    for (sname, mkey) in dep_marks:
                        w(e, sname, marks.get(mkey, 0))
                    half = slice(0, 704)
                    for j in range(j0, j1):
                        e.tensor_scalar(dsc1[:, half], pkv, 4 * j, 15,
                                        OP.logical_shift_right, OP.bitwise_and)
                        e.tensor_scalar(dsc2[:, half], dsc1[:, half], 7, 6,
                                        OP.bitwise_and, OP.logical_shift_left)
                        e.tensor_scalar(dsc3[:, half], dsc2[:, half], 2, 0x3E80, OP.mult, OP.add)
                        e.tensor_scalar(dsc4[:, half], dsc2[:, half], 0x3F00, None, OP.add)
                        e.tensor_tensor(dsc3[:, half], dsc3[:, half], dsc4[:, half], OP.min)
                        e.scalar_tensor_tensor(dsc3[:, half], dsc2[:, half], 252, dsc3[:, half],
                                               OP.mult, OP.min)
                        e.tensor_scalar(dsc1[:, half], dsc1[:, half], 8, 12,
                                        OP.bitwise_and, OP.logical_shift_left)
                        e.tensor_tensor(dsc4[:, half], dsc3[:, half], dsc1[:, half], OP.bitwise_or)
                        src = dsc4[:, half].bitcast(U16).rearrange(
                            "p (ot g two) -> p ot g two", ot=nib_ot, two=2)[:, :, :, 0:1]
                        ins = e.tensor_copy(dqv[:, :, :, j:j + 1], src)
                    inc(ins, "dve" if eng == "dve" else "pl")
            return fn

        def gu_chunk(k):
            def load():
                @sec("sync")
                def _(sy):
                    if k > 0:
                        w(sy, "dve", marks[("gu_pl", k - 1)])
                    else:
                        w(sy, "pe", marks["p1done_pe"])
                        w(sy, "act", marks["p1done_act"])
                        w(sy, "dve", marks[("qn", NT - 1)])
                        w(sy, "pl", marks[("kn", NT - 1)])
                        if debug:
                            w(sy, "pld", marks.get("dbg_p1", 0))
                    inc(sy.dma_start(out=pkg[:].rearrange("p (ot g) -> p ot g", ot=22),
                                     in_=gup_s.rearrange("(ot p) g -> p ot g", p=128)[:, 22 * k:22 * (k + 1), :]), "sp", 16)
                marks[("gu_ld", k)] = now("sp")
            dq_steps.append(load)

            dqv = dqg[:].bitcast(U16).rearrange("p (ot g j) -> p ot g j", ot=22, g=32)

            def dve_a():
                dq_planes("dve", pkg[:, 0:704], dqv, 22,
                          [("sp", ("gu_ld", k)), ("pe", ("gu_tp", k - 1))], 0, 3)()
                marks[("gu_dva", k)] = now("dve")

            def dve_b():
                dq_planes("dve", pkg[:, 0:704], dqv, 22, [], 3, 6)()
                marks[("gu_dvb", k)] = now("dve")
            dq_steps.append(dve_a)
            dq_steps.append(dve_b)

            def dve_c():
                dq_planes("dve", pkg[:, 0:704], dqv, 22, [], 6, 8)()
                marks[("gu_pl", k)] = now("dve")
            dq_steps.append(dve_c)

            def scales_T():
                @sec("vector")
                def _(v):
                    scgv = scg[:].rearrange("p (g t) -> p g t", g=2)
                    for ot in range(22):
                        for g in range(2):
                            ins = v.tensor_scalar(dqg[:, 256 * ot + 128 * g:256 * ot + 128 * (g + 1)],
                                                  dqg[:, 256 * ot + 128 * g:256 * ot + 128 * (g + 1)],
                                                  scgv[:, g, 22 * k + ot:22 * k + ot + 1], None, OP.mult)
                    inc(ins, "dve")
                marks[("gu_sc", k)] = now("dve")
                for ih in range(2):
                    o0 = 0
                    for nb_ in [4, 4, 4, 4, 4, 2]:
                        @sec("tensor")
                        def _(t, ih=ih, o0=o0, nb_=nb_):
                            w(t, "dve", marks[("gu_sc", k)])
                            fs, fc = marks["pYTfree"]
                            w(t, fs, fc)
                            for u in range(nb_):
                                ot = o0 + u
                                ins = t.transpose(pYT[:, 128 * u:128 * (u + 1)],
                                                  dqg[:, 256 * ot + 128 * ih:256 * ot + 128 * (ih + 1)],
                                                  identb[:])
                            inc(ins, "pe")
                        marks["pYTw"] = now("pe")

                        @sec("scalar")
                        def _(s_, ih=ih, o0=o0, nb_=nb_):
                            w(s_, "pe", marks["pYTw"])
                            w(s_, "pld", marks.get(("gu_st", k - 1), 0))
                            dsv = dstage[:, 0:5632].rearrange("p (i ot c) -> p i ot c", i=2, ot=22)
                            inc(s_.copy(dsv[:, ih, o0:o0 + nb_, :], pYT[:, 0:128 * nb_]), "act")
                        marks["pYTfree"] = ("act", now("act"))
                        o0 += nb_
                marks[("gu_tp", k)] = now("pe")
                marks[("gu_tpev", k)] = now("act")
            dq_steps.append(scales_T)

            def store():
                @sec("gpsimd")
                def _(g_):
                    w(g_, "act", marks[("gu_tpev", k)])
                    dsv = dstage[:, 0:5632].rearrange("p (i c) -> p i c", i=2)
                    dst = bass.AP(wguT_in, 2816 * k,
                                  [[2 * IFFN, 128], [128 * 2 * IFFN, 2], [1, 2816]])
                    inc(g_.dma_start(out=dst, in_=dsv), "pld", 16)
                marks[("gu_st", k)] = now("pld")
            dq_steps.append(store)

        def dn_chunk(m):
            def load():
                @sec("sync")
                def _(sy):
                    if m == 0:
                        w(sy, "dve", marks[("gu_sc", 3)])
                    else:
                        w(sy, "dve", marks[("dn_pl", m - 1)])
                    inc(sy.dma_start(out=pkg[:].rearrange("p (ot g) -> p ot g", ot=8),
                                     in_=dnp_s.rearrange("(ot p) g -> p ot g", p=128)[:, 8 * m:8 * (m + 1), :]), "sp", 16)
                marks[("dn_ld", m)] = now("sp")
            dq_steps.append(load)

            dqv = dqg[:].bitcast(U16).rearrange("p (ot g j) -> p ot g j", ot=8, g=88)

            def dve_a():
                dq_planes("dve", pkg[:, 0:704], dqv, 8,
                          [("sp", ("dn_ld", m)),
                           ("pe", ("dn_tp", m - 1) if m > 0 else ("gu_tp", 3))], 0, 3)()
                marks[("dn_dva", m)] = now("dve")

            def dve_b():
                dq_planes("dve", pkg[:, 0:704], dqv, 8, [], 3, 6)()
                marks[("dn_dvb", m)] = now("dve")
            dq_steps.append(dve_a)
            dq_steps.append(dve_b)

            def dve_c():
                dq_planes("dve", pkg[:, 0:704], dqv, 8, [], 6, 8)()
                marks[("dn_pl", m)] = now("dve")
            dq_steps.append(dve_c)

            def scales_T():
                @sec("vector")
                def _(v):
                    scdv = scd[:].rearrange("p (s t) -> p s t", s=11)
                    for ot in range(8):
                        for sg_ in range(11):
                            ins = v.tensor_scalar(dqg[:, 704 * ot + 64 * sg_:704 * ot + 64 * (sg_ + 1)],
                                                  dqg[:, 704 * ot + 64 * sg_:704 * ot + 64 * (sg_ + 1)],
                                                  scdv[:, sg_, 8 * m + ot:8 * m + ot + 1], None, OP.mult)
                    inc(ins, "dve")
                marks[("dn_sc", m)] = now("dve")
                for ot in range(8):
                    @sec("tensor")
                    def _(t, ot=ot):
                        w(t, "dve", marks[("dn_sc", m)])
                        fs, fc = marks["pYTfree"]
                        w(t, fs, fc)
                        for ih in range(5):
                            t.transpose(pYT[:, 128 * ih:128 * (ih + 1)],
                                        dqg[:, 704 * ot + 128 * ih:704 * ot + 128 * (ih + 1)],
                                        identb[:])
                        ins = t.transpose(pYT[0:64, 640:768],
                                          dqg[:, 704 * ot + 640:704 * ot + 704], identb[:])
                        inc(ins, "pe")
                    marks["pYTw"] = now("pe")

                    @sec("scalar")
                    def _(s_, ot=ot):
                        w(s_, "pe", marks["pYTw"])
                        if ot == 0:
                            w(s_, "pld", marks.get(("dn_st", m - 1), 0))
                        dsv = dstage[:].rearrange("p (i ot c) -> p i ot c", i=6, ot=8)
                        inc(s_.copy(dsv[:, 0:6, ot, :],
                                    pYT[:, 0:768].rearrange("p (i c) -> p i c", i=6)), "act")
                    marks["pYTfree"] = ("act", now("act"))
                marks[("dn_tp", m)] = now("pe")
                marks[("dn_tpev", m)] = now("act")
            dq_steps.append(scales_T)

            def store():
                @sec("gpsimd")
                def _(g_):
                    w(g_, "act", marks[("dn_tpev", m)])
                    dsv = dstage[:].rearrange("p (i ot c) -> p i ot c", i=6, ot=8)
                    dst1 = bass.AP(wdT_in, 1024 * m, [[D, 128], [128 * D, 5], [1, 1024]])
                    src1 = dsv[:, 0:5, :, :].rearrange("p i ot c -> p i (ot c)")
                    inc(g_.dma_start(out=dst1, in_=src1), "pld", 16)
                    dst2 = bass.AP(wdT_in, 640 * D + 1024 * m, [[D, 64], [1, 1024]])
                    src2 = dsv[0:64, 5, :, :].rearrange("p ot c -> p (ot c)")
                    inc(g_.dma_start(out=dst2, in_=src2), "pld", 16)
                marks[("dn_st", m)] = now("pld")
            dq_steps.append(store)

        for k in range(4):
            gu_chunk(k)
        for m in range(2):
            dn_chunk(m)

        def dq_collectives():
            @sec("gpsimd")
            def _(g_):
                w(g_, "pld", marks[("gu_st", 3)])
                inc(g_.collective_compute(
                    "AllGather", OP.bypass, replica_groups=[list(range(8))],
                    ins=[wguT_in.ap().opt()], outs=[wguT_all.ap().opt()]), "cgu")
                w(g_, "pld", marks[("dn_st", 1)])
                inc(g_.collective_compute(
                    "AllGather", OP.bypass, replica_groups=[list(range(8))],
                    ins=[wdT_in.ap().opt()], outs=[wdT_all.ap().opt()]), "cwd")
        dq_steps.append(dq_collectives)

        def emit_stats(h):
            @sec("vector")
            def _(v):
                w(v, "act", marks["p1done_act"])
                w(v, "dve", marks["p1done_dve"])
                if debug:
                    w(v, "pld", marks.get("dbg_p1", 0))
                if h > 0:
                    w(v, "pe", marks[("qp_mm", h - 1)])
                bar(v, v.reduce_sum(qsum[:], qfmv[:, h, :], axis=mybir.AxisListType.X), "dve")
                inc(v.tensor_copy(qsumb[:], qsum[:]), "dve")
            marks[("qsum", h)] = now("dve")

            for cc_ in range(8):
                @sec("tensor")
                def _(t, cc_=cc_):
                    if cc_ == 0:
                        w(t, "dve", marks[("qsum", h)])
                        if debug:
                            w(t, "pld", marks.get("dbg_p1", 0))
                        if h > 0:
                            w(t, "dve", marks[("kpn_used", h - 1)])
                    w(t, "dve", marks.get(("kp_ev", h, cc_ - 1), 0))
                    inc(t.matmul(pStat[0:JL, 0:512], projb[:],
                                 kfmv[:, h, 512 * cc_:512 * (cc_ + 1)],
                                 start=True, stop=True), "pe")
                marks[("kp_mm", h, cc_)] = now("pe")

                @sec("vector")
                def _(v, cc_=cc_):
                    w(v, "pe", marks[("kp_mm", h, cc_)])
                    inc(v.tensor_copy(kps[:, 512 * cc_:512 * (cc_ + 1)], pStat[0:JL, 0:512]), "dve")
                marks[("kp_ev", h, cc_)] = now("dve")

            @sec("vector")
            def _(v):
                w(v, "dve", marks[("kp_ev", h, 7)])
                inc(v.tensor_tensor(kpn[:], kps[:], kps[:], OP.mult), "dve")
            marks[("kpsq", h)] = now("dve")

            for cc_ in range(8):
                @sec("tensor")
                def _(t, cc_=cc_):
                    w(t, "dve", marks[("kpsq", h)])
                    w(t, "dve", marks.get(("cs_ev", h, cc_ - 1), 0))
                    inc(t.matmul(pStat[0:1, 0:512], _ones32(),
                                 kpn[:, 512 * cc_:512 * (cc_ + 1)],
                                 start=True, stop=True), "pe")
                marks[("cs_mm", h, cc_)] = now("pe")

                @sec("vector")
                def _(v, cc_=cc_):
                    w(v, "pe", marks[("cs_mm", h, cc_)])
                    inc(v.tensor_copy(rown[0:1, 512 * cc_:512 * (cc_ + 1)], pStat[0:1, 0:512]), "dve")
                marks[("cs_ev", h, cc_)] = now("dve")

            @sec("scalar")
            def _(s_):
                w(s_, "dve", marks[("cs_ev", h, 7)])
                bar(s_, s_.activation(rown[0:1, :], rown[0:1, :], AF.Ln), "act")
                inc(s_.activation(rown[0:1, :], rown[0:1, :], AF.Exp, scale=0.5), "act")
            marks[("nrm_sq", h)] = now("act")

            @sec("vector")
            def _(v):
                w(v, "act", marks[("nrm_sq", h)])
                bar(v, v.tensor_scalar(rown[0:1, :], rown[0:1, :], 1e-12, None, OP.max), "dve")
                inc(v.reciprocal(rown[0:1, :], rown[0:1, :]), "dve")
            marks[("nrm_rr", h)] = now("dve")

            for cc_ in range(8):
                @sec("tensor")
                def _(t, cc_=cc_):
                    w(t, "dve", marks[("nrm_rr", h)])
                    w(t, "dve", marks.get(("kpn_mul", h, cc_ - 1), 0))
                    inc(t.matmul(pStat[0:JL, 0:512], onesr[0:1, 0:JL],
                                 rown[0:1, 512 * cc_:512 * (cc_ + 1)],
                                 start=True, stop=True), "pe")
                marks[("bc_mm", h, cc_)] = now("pe")

                @sec("vector")
                def _(v, cc_=cc_):
                    w(v, "pe", marks[("bc_mm", h, cc_)])
                    inc(v.tensor_tensor(kpn[:, 512 * cc_:512 * (cc_ + 1)],
                                        kps[:, 512 * cc_:512 * (cc_ + 1)],
                                        pStat[0:JL, 0:512], OP.mult), "dve")
                marks[("kpn_mul", h, cc_)] = now("dve")
            marks[("kpn", h)] = now("dve")

            @sec("vector")
            def _(v):
                w(v, "dve", marks[("kpn", h)])
                bar(v, v.reduce_sum(cent[:], kpn[:].rearrange("p (b c) -> p b c", b=NB),
                                    axis=mybir.AxisListType.X), "dve")
                inc(v.tensor_tensor(centn[:], cent[:], cent[:], OP.mult), "dve")
            marks[("cent", h)] = now("dve")

            @sec("tensor")
            def _(t):
                w(t, "dve", marks[("cent", h)])
                inc(t.matmul(pStat[0:1, 0:NB], _ones32(), centn[:], start=True, stop=True), "pe")
            marks[("cn_mm", h)] = now("pe")

            @sec("scalar")
            def _(s_):
                w(s_, "pe", marks[("cn_mm", h)])
                bar(s_, s_.copy(srow[0:1, 192:256], pStat[0:1, 0:NB]), "act")
                bar(s_, s_.activation(srow[0:1, 192:256], srow[0:1, 192:256], AF.Ln), "act")
                inc(s_.activation(srow[0:1, 192:256], srow[0:1, 192:256], AF.Exp, scale=0.5), "act")
            marks[("cn_sq", h)] = now("act")

            @sec("vector")
            def _(v):
                w(v, "act", marks[("cn_sq", h)])
                bar(v, v.tensor_scalar(srow[0:1, 192:256], srow[0:1, 192:256], 1e-12, None, OP.max), "dve")
                inc(v.reciprocal(srow[0:1, 192:256], srow[0:1, 192:256]), "dve")
            marks[("cn_rr", h)] = now("dve")

            @sec("tensor")
            def _(t):
                w(t, "dve", marks[("cn_rr", h)])
                inc(t.matmul(pStat[0:JL, 0:NB], onesr[0:1, 0:JL], srow[0:1, 192:256],
                             start=True, stop=True), "pe")
            marks[("cnbc_mm", h)] = now("pe")

            @sec("vector")
            def _(v):
                w(v, "pe", marks[("cnbc_mm", h)])
                inc(v.tensor_tensor(centn[:], cent[:], pStat[0:JL, 0:NB], OP.mult), "dve")
            marks[("centn", h)] = now("dve")

            @sec("vector")
            def _(v):
                w(v, "dve", marks[("centn", h)])
                ce = centn[:].rearrange("p (b o) -> p b o", o=1).broadcast_to((JL, NB, 64))
                inc(v.tensor_tensor(kps[:].rearrange("p (b c) -> p b c", b=NB),
                                    kpn[:].rearrange("p (b c) -> p b c", b=NB),
                                    ce, OP.mult), "dve")
            marks[("prod", h)] = now("dve")

            for cc_ in range(8):
                @sec("tensor")
                def _(t, cc_=cc_):
                    w(t, "dve", marks[("prod", h)])
                    w(t, "dve", marks.get(("dots_ev", h, cc_ - 1), 0))
                    inc(t.matmul(pStat[0:1, 0:512], _ones32(),
                                 kps[:, 512 * cc_:512 * (cc_ + 1)],
                                 start=True, stop=True), "pe")
                marks[("dots_mm", h, cc_)] = now("pe")

                @sec("vector")
                def _(v, cc_=cc_):
                    w(v, "pe", marks[("dots_mm", h, cc_)])
                    inc(v.tensor_copy(rown[0:1, 512 * cc_:512 * (cc_ + 1)], pStat[0:1, 0:512]), "dve")
                marks[("dots_ev", h, cc_)] = now("dve")

            @sec("vector")
            def _(v):
                w(v, "dve", marks[("dots_ev", h, 7)])
                bar(v, v.tensor_reduce(srow[0:1, 128:192], rown[0:1, :].rearrange("o (b c) -> o b c", b=NB),
                                       axis=mybir.AxisListType.X, op=OP.min), "dve")
                bar(v, v.tensor_scalar(srow[0:1, 64:128], srow[0:1, 128:192], -1.0, 1.0, OP.mult, OP.add), "dve")
                inc(v.tensor_scalar(srow[0:1, 64:128], srow[0:1, 64:128], 0.0, 1.0, OP.max, OP.min), "dve")
            marks[("radii", h)] = now("dve")

            @sec("tensor")
            def _(t):
                w(t, "dve", marks[("qsum", h)])
                w(t, "dve", marks[("dots_ev", h, 7)])
                inc(t.matmul(pStat[0:JL, 64:65], projb[:], qsumb[:], start=True, stop=True), "pe")
            marks[("qp_mm", h)] = now("pe")

            @sec("scalar")
            def _(s_):
                w(s_, "pe", marks[("qp_mm", h)])
                inc(s_.copy(qp_s[:], pStat[0:JL, 64:65]), "act")
            marks[("qp_ev", h)] = now("act")

            @sec("vector")
            def _(v):
                w(v, "act", marks[("qp_ev", h)])
                inc(v.tensor_tensor(kps[0:JL, 0:1], qp_s[:], qp_s[:], OP.mult), "dve")
            marks[("qpsq", h)] = now("dve")

            @sec("tensor")
            def _(t):
                w(t, "dve", marks[("qpsq", h)])
                inc(t.matmul(pStat[0:1, 65:66], _ones32(), kps[0:JL, 0:1], start=True, stop=True), "pe")
            marks[("nq_mm", h)] = now("pe")

            @sec("scalar")
            def _(s_):
                w(s_, "pe", marks[("nq_mm", h)])
                bar(s_, s_.copy(nq_s[0:1, 0:1], pStat[0:1, 65:66]), "act")
                bar(s_, s_.activation(nq_s[0:1, 0:1], nq_s[0:1, 0:1], AF.Ln), "act")
                inc(s_.activation(nq_s[0:1, 0:1], nq_s[0:1, 0:1], AF.Exp, scale=0.5), "act")
            marks[("nq_sq", h)] = now("act")

            @sec("vector")
            def _(v):
                w(v, "act", marks[("nq_sq", h)])
                bar(v, v.tensor_scalar(nq_s[0:1, 0:1], nq_s[0:1, 0:1], 1e-12, None, OP.max), "dve")
                inc(v.reciprocal(nq_s[0:1, 0:1], nq_s[0:1, 0:1]), "dve")
            marks[("nq_rr", h)] = now("dve")

            @sec("tensor")
            def _(t):
                w(t, "dve", marks[("nq_rr", h)])
                inc(t.matmul(pStat[0:1, 0:NB], qp_s[:], centn[:], start=True, stop=True), "pe")
            marks[("sc_mm", h)] = now("pe")

            @sec("vector")
            def _(v):
                w(v, "pe", marks[("sc_mm", h)])
                bar(v, v.tensor_scalar(srow[0:1, 0:64], pStat[0:1, 0:NB], nq_s[0:1, 0:1], None, OP.mult), "dve")
                bar(v, v.tensor_tensor(srow[0:1, 0:64], srow[0:1, 0:64], srow[0:1, 64:128], OP.add), "dve")
                inc(v.memset(srow[0:1, 62:64], 1e9), "dve")
            marks[("scores", h)] = now("dve")

            if debug and h == 0:
                @sec("gpsimd")
                def _(g_):
                    w(g_, "dve", marks[("scores", h)])
                    inc(g_.dma_start(out=dbg_t["d_kp"], in_=kps[:]), "pld", 16)
                    inc(g_.dma_start(out=dbg_t["d_kpn"], in_=kpn[:]), "pld", 16)
                    inc(g_.dma_start(out=dbg_t["d_cent"], in_=centn[:]), "pld", 16)
                    inc(g_.dma_start(out=dbg_t["d_scores"], in_=srow[0:1, 0:64]), "pld", 16)
                marks[("dbg_stats", h)] = now("pld")

            @sec("tensor")
            def _(t):
                w(t, "dve", marks[("scores", h)])
                if debug and h == 0:
                    w(t, "pld", marks[("dbg_stats", h)])
                t.matmul(pStat[0:64, 0:64], onesr[0:1, 0:64], srow[0:1, 0:64], start=True, stop=True)
                inc(t.matmul(pStat[0:64, 64:65], srow[0:1, 0:64], onesr[0:1, 0:1], start=True, stop=True), "pe")
            marks[("mask_mm", h)] = now("pe")

            @sec("vector")
            def _(v):
                w(v, "pe", marks[("mask_mm", h)])
                bar(v, v.tensor_scalar(cmpm[:], pStat[0:64, 0:64], pStat[0:64, 64:65], None, OP.is_gt), "dve")
                bar(v, v.reduce_sum(cntc[0:64, 0:1], cmpm[:], axis=mybir.AxisListType.X), "dve")
                bar(v, v.tensor_scalar(cntc[0:64, 0:1], cntc[0:64, 0:1], 33.5, None, OP.is_lt), "dve")
                inc(v.tensor_scalar(cntc[0:64, 1:2], cntc[0:64, 0:1], -1.0, 30000.0,
                                    OP.add, OP.mult), "dve")
            marks[("pen_col", h)] = now("dve")

            @sec("tensor")
            def _(t):
                w(t, "dve", marks[("pen_col", h)])
                inc(t.transpose(pStat[0:1, 66:130], cntc[0:64, 1:2], identf[0:64, 0:64]), "pe")
            marks[("penr_mm", h)] = now("pe")

            @sec("scalar")
            def _(s_):
                w(s_, "pe", marks[("penr_mm", h)])
                inc(s_.copy(penr[0:1, :], pStat[0:1, 66:130]), "act")
            marks[("penr", h)] = now("act")

            @sec("vector")
            def _(v):
                w(v, "act", marks[("penr", h)])
                inc(v.tensor_scalar(nbrow[0:1, :], onesr[0:1, 0:32], negb[0:1, h:h + 1], None, OP.mult), "dve")
            marks[("nbrow", h)] = now("dve")

            @sec("tensor")
            def _(t):
                w(t, "dve", marks[("nbrow", h)])
                pe_ = penr[0:1, :].rearrange("o (t two) -> o t two", two=2)
                t.matmul(pStat[0:128, 128:160], indu[0:1, :], pe_[:, :, 0:1].squeeze(2), start=True, stop=False)
                t.matmul(pStat[0:128, 128:160], indl[0:1, :], pe_[:, :, 1:2].squeeze(2), start=False, stop=False)
                inc(t.matmul(pStat[0:128, 128:160], onesr[0:1, :], nbrow[0:1, :], start=False, stop=True), "pe")
            marks[("ptab_mm", h)] = now("pe")

            @sec("scalar")
            def _(s_):
                w(s_, "pe", marks[("ptab_mm", h)])
                inc(s_.copy(pen_tab[:, 32 * h:32 * (h + 1)], pStat[0:128, 128:160]), "act")
            marks[("ptab", h)] = now("act")
            marks[("kpn_used", h)] = now("dve")

        def emit_attend(h):
            for ch in range(8):
                A0 = now("act")

                def expmark(hh, cc, tt):
                    if (hh, cc) == (h, ch):
                        return A0 + tt + 1
                    return marks[("exp", hh, cc, tt)]

                @sec("tensor")
                def _(t, h=h, ch=ch, A0=A0):
                    for t_i in range(34):
                        if t_i < 32:
                            if t_i == 0 and ch == 0:
                                w(t, "act", marks[("ptab", h)])
                            if t_i >= 2:
                                w(t, "act", expmark(h, ch, t_i - 2))
                            elif ch > 0:
                                w(t, "act", expmark(h, ch - 1, 30 + t_i))
                            elif h > 0:
                                w(t, "act", expmark(h - 1, 7, 30 + t_i))
                            pS = pS0 if t_i % 2 == 0 else pS1
                            inc(t.matmul(pS[:, 0:512],
                                         kfmv[:, h, 128 * t_i:128 * (t_i + 1)],
                                         qfmv[:, h, 512 * ch:512 * (ch + 1)],
                                         start=True, stop=True), "pe")
                            marks[("qk", h, ch, t_i)] = now("pe")
                        if t_i >= 2:
                            tp = t_i - 2
                            w(t, "act", expmark(h, ch, tp))
                            if tp == 0:
                                pm = marks.get(("yfin", h, ch - 1)) if ch > 0 else marks.get(("yfin", h - 1, 7))
                                if pm:
                                    w(t, "dve", pm)
                            for j in range(4):
                                ins = t.matmul(pv_banks[j][:, 0:129],
                                               pexp[:, 512 * (tp % 2) + 128 * j:512 * (tp % 2) + 128 * (j + 1)],
                                               vview[:, h, tp, 0:129],
                                               start=(tp == 0), stop=(tp == 31))
                            inc(ins, "pe")
                            marks[("pv", h, ch, tp)] = now("pe")

                @sec("scalar")
                def _(s_, h=h, ch=ch):
                    for t_i in range(32):
                        w(s_, "pe", marks[("qk", h, ch, t_i)])
                        prev = None
                        if t_i >= 2:
                            prev = ("pv", h, ch, t_i - 2)
                        elif ch > 0:
                            prev = ("pv", h, ch - 1, 30 + t_i)
                        elif h > 0:
                            prev = ("pv", h - 1, 7, 30 + t_i)
                        if prev is not None:
                            w(s_, "pe", marks[prev])
                        pS = pS0 if t_i % 2 == 0 else pS1
                        ins = s_.activation(pexp[:, 512 * (t_i % 2):512 * (t_i % 2 + 1)],
                                            pS[:, 0:512], AF.Exp,
                                            bias=pen_tab[:, 32 * h + t_i:32 * h + t_i + 1],
                                            scale=SC_QK)
                        inc(ins, "act")
                        marks[("exp", h, ch, t_i)] = now("act")
                assert now("act") == A0 + 32

                @sec("vector")
                def _(v, h=h, ch=ch):
                    w(v, "pe", marks[("pv", h, ch, 31)])
                    for j in range(4):
                        ins = v.reciprocal(rcj[:, j:j + 1], pv_banks[j][:, 128:129])
                    bar(v, ins, "dve")
                    for j in range(4):
                        ins = v.tensor_scalar(ydiv[:, 128 * j:128 * (j + 1)],
                                              pv_banks[j][:, 0:128], rcj[:, j:j + 1], None, OP.mult)
                    inc(ins, "dve")
                marks[("yfin", h, ch)] = now("dve")

                @sec("tensor")
                def _(t, h=h, ch=ch):
                    w(t, "dve", marks[("yfin", h, ch)])
                    fs, fc = marks["pYTfree"]
                    w(t, fs, fc)
                    for j in range(4):
                        ins = t.transpose(pYT[:, 128 * j:128 * (j + 1)],
                                          ydiv[:, 128 * j:128 * (j + 1)], identb[:])
                    inc(ins, "pe")
                marks["pYTw"] = now("pe")

                @sec("vector")
                def _(v, h=h, ch=ch):
                    w(v, "pe", marks["pYTw"])
                    w(v, "pld", marks.get(("yslab_st", h - 1), 0))
                    inc(v.tensor_copy(yslab[:, 512 * ch:512 * (ch + 1)], pYT[:, 0:512]), "dve")
                marks["pYTfree"] = ("dve", now("dve"))
                marks[("ychunk", h, ch)] = now("dve")

                for _pop in range(2):
                    if dq_steps:
                        dq_steps.pop(0)()

            @sec("gpsimd")
            def _(g_, h=h):
                w(g_, "dve", marks[("ychunk", h, 7)])
                dst = y_in.ap().rearrange("(h p) s -> p h s", p=128)[:, h, :]
                inc(g_.dma_start(out=dst, in_=yslab[:]), "pld", 16)
            marks[("yslab_st", h)] = now("pld")

        for h in range(HPC):
            emit_stats(h)
            emit_attend(h)

        while dq_steps:
            dq_steps.pop(0)()

        @sec("gpsimd")
        def _(g_):
            w(g_, "pld", marks[("yslab_st", HPC - 1)])
            inc(g_.collective_compute(
                "AllGather", OP.bypass,
                replica_groups=[[0, 1, 2, 3], [4, 5, 6, 7]],
                ins=[y_in.ap().opt()], outs=[y_all.ap().opt()]), "cy")

        if debug:
            @sec("gpsimd")
            def _(g_):
                w(g_, "cy", 1)
                inc(g_.dma_start(out=dbg_t["d_yin"], in_=y_in.ap()), "pld", 16)
                inc(g_.dma_start(out=dbg_t["d_yall"], in_=y_all.ap()), "pld", 16)
                inc(g_.dma_start(out=dbg_t["d_pen"], in_=pen_tab[:]), "pld", 16)

        marks["attn_done_pe"] = now("pe")
        p23.close()
        pA.close()

        pB = ExitStack()
        x2T = pB.enter_context(nc.sbuf_tensor("x2T", [128, 16 * QTOK], BF16))
        x2v = x2T[:].rearrange("p (dc s) -> p dc s", dc=16)

        p3b = ExitStack()
        sb3 = lambda n, s_, d: p3b.enter_context(nc.sbuf_tensor(n, s_, d))
        ps3 = lambda n, s_, d: p3b.enter_context(nc.psum_tensor(n, s_, d))
        yfmb = sb3("yfmb", [128, 16 * QTOK], BF16)
        woTs = sb3("woTs", [128, 2 * 16 * 512], BF16)
        atst = sb3("atst", [128, 2 * 512], F32)
        h1t = sb3("h1t", [128, 2 * D], F32)
        x2b = sb3("x2b", [128, 2 * D], BF16)
        ln2c = sb3("ln2c", [128, 2 * 8], F32)
        dmp2 = sb3("dmp2", [128, 2 * D], F32)
        pWo = [ps3("pWo0", [128, 512], F32), ps3("pWo1", [128, 512], F32)]
        pX2T = ps3("pX2T", [128, 2048], BF16)

        @sec("gpsimd")
        def _(g_):
            w(g_, "cy", 1)
            with g_.register("rg0") as r0, g_.register("rg1") as r1:
                g_.reg_load(r0, pidc[0:1, 0:1])
                g_.reg_mod(r1, r0, 4)
                g_.reg_mul(r1, r1, QTOK)
                src = bass.AP(y_all, r1, [[S, 128], [128 * S, 16], [1, QTOK]])
                inc(g_.dma_start(out=yfmb[:].rearrange("p (dc s) -> p dc s", dc=16), in_=src), "pld", 16)
        marks["yfm_ld"] = now("pld")

        for oc in range(4):
            @sec("sync")
            def _(sy, oc=oc):
                w(sy, "cwo", 1)
                if oc == 0:
                    w(sy, "pe", marks["attn_done_pe"])
                    w(sy, "act")
                    w(sy, "dve")
                    w(sy, "pld")
                if oc >= 2:
                    w(sy, "pe", marks[("wo_mm", oc - 2, 7)])
                src = woT_all.ap().rearrange("(dc p) c -> p dc c", p=128)[:, :, 512 * oc:512 * (oc + 1)]
                inc(sy.dma_start(out=woTs[:].rearrange("p (r dc c) -> p r dc c", r=2, dc=16)[:, oc % 2, :, :],
                                 in_=src), "sp", 16)
            marks[("woT_ld", oc)] = now("sp")

            for ss in range(8):
                @sec("tensor")
                def _(t, oc=oc, ss=ss):
                    w(t, "sp", marks[("woT_ld", oc)])
                    if oc == 0 and ss == 0:
                        w(t, "pld", marks["yfm_ld"])
                    idx = oc * 8 + ss
                    if idx >= 2:
                        w(t, "dve", marks[("wo_ev", idx - 2)])
                    yv = yfmb[:].rearrange("p (dc s) -> p dc s", dc=16)
                    wv_ = woTs[:].rearrange("p (r dc c) -> p r dc c", r=2, dc=16)
                    for dc in range(16):
                        ins = t.matmul(pWo[idx % 2][:, 0:512],
                                       yv[:, dc, 128 * ss:128 * (ss + 1)],
                                       wv_[:, oc % 2, dc, :],
                                       start=(dc == 0), stop=(dc == 15))
                    inc(ins, "pe")
                marks[("wo_mm", oc, ss)] = now("pe")

                @sec("vector")
                def _(v, oc=oc, ss=ss):
                    idx = oc * 8 + ss
                    w(v, "pe", marks[("wo_mm", oc, ss)])
                    if idx >= 2:
                        w(v, "pld", marks[("at_st", idx - 2)])
                    inc(v.tensor_copy(atst[:, 512 * (idx % 2):512 * (idx % 2 + 1)], pWo[idx % 2][:, 0:512]), "dve")
                marks[("wo_ev", oc * 8 + ss)] = now("dve")

                @sec("gpsimd")
                def _(g_, oc=oc, ss=ss):
                    idx = oc * 8 + ss
                    w(g_, "dve", marks[("wo_ev", idx)])
                    inc(g_.dma_start(out=attn_d.ap()[128 * ss:128 * (ss + 1), 512 * oc:512 * (oc + 1)],
                                     in_=atst[:, 512 * (idx % 2):512 * (idx % 2 + 1)]), "pld", 16)
                marks[("at_st", oc * 8 + ss)] = now("pld")

        if debug:
            @sec("gpsimd")
            def _(g_):
                w(g_, "pld", marks[("at_st", 31)])
                inc(g_.dma_start(out=dbg_t["d_attn"], in_=attn_d.ap()), "pld", 16)
                inc(g_.dma_start(out=dbg_t["d_woT"], in_=woT_all.ap()), "pld", 16)

        for ss in range(8):
            pi = ss % 2

            @sec("sync")
            def _(sy, ss=ss, pi=pi):
                w(sy, "pld", marks[("at_st", 24 + ss)])
                if ss >= 2:
                    w(sy, "act", marks[("ln2acc", ss - 2)])
                    w(sy, "dve", marks[("x2", ss - 2)])
                inc(sy.dma_start(out=h1t[:, D * pi:D * (pi + 1)],
                                 in_=attn_d.ap()[128 * ss:128 * (ss + 1), :]), "sp", 16)
                inc(sy.dma_start(out=dmp2[:, D * pi:D * (pi + 1)],
                                 in_=x_q[128 * ss:128 * (ss + 1), :]), "sp", 16)
            marks[("h1_ld", ss)] = now("sp")

            @sec("vector")
            def _(v, ss=ss, pi=pi):
                w(v, "sp", marks[("h1_ld", ss)])
                inc(v.tensor_tensor(h1t[:, D * pi:D * (pi + 1)],
                                    h1t[:, D * pi:D * (pi + 1)], dmp2[:, D * pi:D * (pi + 1)], OP.add), "dve")
            marks[("h1", ss)] = now("dve")

            @sec("scalar")
            def _(s_, ss=ss, pi=pi):
                w(s_, "dve", marks[("h1", ss)])
                xt = h1t[:, D * pi:D * (pi + 1)]
                s_.activation(dmp2[:, D * pi:D * (pi + 1)], xt, AF.Copy, accum_out=ln2c[:, 8 * pi + 0:8 * pi + 1])
                inc(s_.activation(dmp2[:, D * pi:D * (pi + 1)], xt, AF.Square, accum_out=ln2c[:, 8 * pi + 1:8 * pi + 2]), "act")
            marks[("ln2acc", ss)] = now("act")

            @sec("vector")
            def _(v, ss=ss, pi=pi):
                w(v, "act", marks[("ln2acc", ss)])
                o = 8 * pi
                v.tensor_scalar(ln2c[:, o + 2:o + 3], ln2c[:, o + 0:o + 1], 1.0 / D, None, OP.mult)
                bar(v, v.scalar_tensor_tensor(ln2c[:, o + 5:o + 6], ln2c[:, o + 0:o + 1],
                                              1.0 / (D * D), ln2c[:, o + 0:o + 1],
                                              OP.mult, OP.mult), "dve")
                inc(v.scalar_tensor_tensor(ln2c[:, o + 3:o + 4], ln2c[:, o + 1:o + 2], 1.0 / D,
                                           ln2c[:, o + 5:o + 6], OP.mult, OP.subtract), "dve")
            marks[("ln2cols", ss)] = now("dve")

            @sec("scalar")
            def _(s_, ss=ss, pi=pi):
                w(s_, "dve", marks[("ln2cols", ss)])
                o = 8 * pi
                bar(s_, s_.activation(ln2c[:, o + 4:o + 5], ln2c[:, o + 3:o + 4], AF.Ln, bias=epsln[:, 0:1]), "act")
                inc(s_.activation(ln2c[:, o + 4:o + 5], ln2c[:, o + 4:o + 5], AF.Exp, scale=-0.5), "act")
            marks[("ln2rstd", ss)] = now("act")

            @sec("vector")
            def _(v, ss=ss, pi=pi):
                w(v, "act", marks[("ln2rstd", ss)])
                o = 8 * pi
                if ss >= 2:
                    w(v, "pe", marks[("x2Tp", ss - 2)])
                inc(v.tensor_scalar(x2b[:, D * pi:D * (pi + 1)], h1t[:, D * pi:D * (pi + 1)],
                                    ln2c[:, o + 2:o + 3], ln2c[:, o + 4:o + 5],
                                    OP.subtract, OP.mult), "dve")
            marks[("x2", ss)] = now("dve")

            @sec("tensor")
            def _(t, ss=ss, pi=pi):
                w(t, "dve", marks[("x2", ss)])
                if ss >= 1:
                    w(t, "dve", marks[("x2Tev", ss - 1)])
                for j in range(16):
                    ins = t.transpose(pX2T[:, 128 * j:128 * (j + 1)],
                                      x2b[:, D * pi + 128 * j:D * pi + 128 * (j + 1)], identb[:])
                inc(ins, "pe")
            marks[("x2Tp", ss)] = now("pe")

            @sec("vector")
            def _(v, ss=ss):
                w(v, "pe", marks[("x2Tp", ss)])
                v.tensor_copy(x2v[:, 0:8, 128 * ss:128 * (ss + 1)],
                              pX2T[:, 0:1024].rearrange("p (dc c) -> p dc c", dc=8))
                inc(v.tensor_copy(x2v[:, 8:16, 128 * ss:128 * (ss + 1)],
                                  pX2T[:, 1024:2048].rearrange("p (dc c) -> p dc c", dc=8)), "dve")
            marks[("x2Tev", ss)] = now("dve")
        marks["x2T"] = now("dve")

        if debug:
            @sec("gpsimd")
            def _(g_):
                w(g_, "dve", marks["x2T"])
                inc(g_.dma_start(out=dbg_t["d_x2t"], in_=x2T[:]), "pld", 16)
                w(g_, "cgu", 1)
                inc(g_.dma_start(out=dbg_t["d_wg"], in_=wguT_all.ap()), "pld", 16)
                w(g_, "cwd", 1)
                inc(g_.dma_start(out=dbg_t["d_wd"], in_=wdT_all.ap()), "pld", 16)

        p3b.close()

        p4base = ExitStack()
        hfm = p4base.enter_context(nc.sbuf_tensor("hfm", [128, 44 * QTOK], BF16))

        p4a = ExitStack()
        wgs = p4a.enter_context(nc.sbuf_tensor("wgs", [128, 2 * 2 * 16 * 128], BF16))
        gsil = p4a.enter_context(nc.sbuf_tensor("gsil", [128, 512], BF16))
        pGt = p4a.enter_context(nc.psum_tensor("pGt", [128, 512], F32))
        pUp = p4a.enter_context(nc.psum_tensor("pUp", [128, 512], F32))

        for ip in range(44):
            @sec("sync")
            def _(sy, ip=ip):
                w(sy, "cgu", 1)
                w(sy, "dve", marks["x2T"])
                if ip >= 2:
                    w(sy, "pe", marks[("gu_mm", ip - 2, 1)])
                wgv = wgs[:].rearrange("p (r gu dc c) -> p r gu dc c", r=2, gu=2, dc=16)
                src_g = wguT_all.ap().rearrange("(dc p) c -> p dc c", p=128)[:, :, 128 * ip:128 * (ip + 1)]
                src_u = wguT_all.ap().rearrange("(dc p) c -> p dc c", p=128)[:, :, IFFN + 128 * ip:IFFN + 128 * (ip + 1)]
                inc(sy.dma_start(out=wgv[:, ip % 2, 0, :, :], in_=src_g), "sp", 16)
                inc(sy.dma_start(out=wgv[:, ip % 2, 1, :, :], in_=src_u), "sp", 16)
            marks[("wg_ld", ip)] = now("sp")

            for sc_ in range(2):
                @sec("tensor")
                def _(t, ip=ip, sc_=sc_):
                    w(t, "sp", marks[("wg_ld", ip)])
                    if ip == 0 and sc_ == 0:
                        w(t, "dve", marks["x2T"])
                    if not (ip == 0 and sc_ == 0):
                        pidx = (ip, sc_ - 1) if sc_ > 0 else (ip - 1, 1)
                        w(t, "dve", marks[("h_mul", pidx[0], pidx[1])])
                        w(t, "act", marks[("gsil", pidx[0], pidx[1])])
                    wgv = wgs[:].rearrange("p (r gu dc c) -> p r gu dc c", r=2, gu=2, dc=16)
                    for dc in range(16):
                        t.matmul(pGt[:, 0:512], wgv[:, ip % 2, 0, dc, :],
                                 x2v[:, dc, 512 * sc_:512 * (sc_ + 1)],
                                 start=(dc == 0), stop=(dc == 15))
                    for dc in range(16):
                        ins = t.matmul(pUp[:, 0:512], wgv[:, ip % 2, 1, dc, :],
                                       x2v[:, dc, 512 * sc_:512 * (sc_ + 1)],
                                       start=(dc == 0), stop=(dc == 15))
                    inc(ins, "pe")
                marks[("gu_mm", ip, sc_)] = now("pe")

                @sec("scalar")
                def _(s_, ip=ip, sc_=sc_):
                    w(s_, "pe", marks[("gu_mm", ip, sc_)])
                    inc(s_.activation(gsil[:], pGt[:, 0:512], AF.Silu), "act")
                marks[("gsil", ip, sc_)] = now("act")

                @sec("vector")
                def _(v, ip=ip, sc_=sc_):
                    w(v, "act", marks[("gsil", ip, sc_)])
                    inc(v.tensor_tensor(hfm[:, QTOK * ip + 512 * sc_:QTOK * ip + 512 * (sc_ + 1)],
                                        pUp[:, 0:512], gsil[:], OP.mult), "dve")
                marks[("h_mul", ip, sc_)] = now("dve")
        marks["hfm"] = now("dve")

        if debug:
            @sec("gpsimd")
            def _(g_):
                w(g_, "dve", marks["hfm"])
                inc(g_.dma_start(out=dbg_t["d_hfm"], in_=hfm[:]), "pld", 16)

        p4a.close()

        p4b = ExitStack()
        sb4 = lambda n, s_, d: p4b.enter_context(nc.sbuf_tensor(n, s_, d))
        ps4 = lambda n, s_, d: p4b.enter_context(nc.psum_tensor(n, s_, d))
        wds = sb4("wds", [128, 2 * 44 * 128], BF16)
        ytev = sb4("ytev", [128, 512], F32)
        ysb4 = sb4("ysb4", [128, 4 * D], F32)
        rescol = sb4("rescol", [128, 8], F32)
        h1l = sb4("h1l", [128, D], F32)
        xql = sb4("xql", [128, D], F32)
        ynrm = sb4("ynrm", [128, D], F32)
        pYd = ps4("pYd", [128, 512], F32)
        pYd2 = ps4("pYd2", [128, 512], F32)
        pTr = ps4("pTr", [128, 512], F32)

        for sc_ in range(2):
            for ho in range(16):
                idx = sc_ * 16 + ho

                @sec("sync")
                def _(sy, sc_=sc_, ho=ho, idx=idx):
                    w(sy, "cwd", 1)
                    w(sy, "dve", marks["hfm"])
                    if idx >= 2:
                        w(sy, "pe", marks[("dn_mmB", idx - 2)])
                    wdv = wds[:].rearrange("p (r ic c) -> p r ic c", r=2, ic=44)
                    src = wdT_all.ap().rearrange("(ic p) c -> p ic c", p=128)[:, :, 128 * ho:128 * (ho + 1)]
                    inc(sy.dma_start(out=wdv[:, idx % 2, :, :], in_=src), "sp", 16)
                marks[("wd_ld", idx)] = now("sp")

                @sec("tensor")
                def _(t, sc_=sc_, ho=ho, idx=idx):
                    w(t, "sp", marks[("wd_ld", idx)])
                    if idx >= 2:
                        w(t, "dve", marks[("yt_ev", idx - 2)])
                    wdv = wds[:].rearrange("p (r ic c) -> p r ic c", r=2, ic=44)
                    pp = pYd if idx % 2 == 0 else pYd2
                    for ic in range(44):
                        ins = t.matmul(pp[:, 0:512], wdv[:, idx % 2, ic, :],
                                       hfm[:, QTOK * ic + 512 * sc_:QTOK * ic + 512 * (sc_ + 1)],
                                       start=(ic == 0), stop=(ic == 43))
                    inc(ins, "pe")
                marks[("dn_mmB", idx)] = now("pe")

                @sec("vector")
                def _(v, idx=idx):
                    w(v, "pe", marks[("dn_mmB", idx)])
                    if idx >= 1:
                        w(v, "pe", marks[("yt_tp", idx - 1)])
                    pp = pYd if idx % 2 == 0 else pYd2
                    inc(v.tensor_copy(ytev[:], pp[:, 0:512]), "dve")
                marks[("yt_ev", idx)] = now("dve")

                @sec("tensor")
                def _(t, idx=idx):
                    w(t, "dve", marks[("yt_ev", idx)])
                    if idx >= 1:
                        w(t, "dve", marks[("ytr_ev", idx - 1)])
                    for j in range(4):
                        ins = t.transpose(pTr[:, 128 * j:128 * (j + 1)],
                                          ytev[:, 128 * j:128 * (j + 1)], identf[:])
                    inc(ins, "pe")
                marks[("yt_tp", idx)] = now("pe")

                @sec("vector")
                def _(v, sc_=sc_, ho=ho, idx=idx):
                    w(v, "pe", marks[("yt_tp", idx)])
                    if sc_ > 0 and ho == 0:
                        w(v, "dve", marks[("fin", sc_ - 1, 3)])
                    yv = ysb4[:].rearrange("p (ssl c) -> p ssl c", ssl=4)
                    inc(v.tensor_copy(yv[:, :, 128 * ho:128 * (ho + 1)],
                                      pTr[:, 0:512].rearrange("p (ssl c) -> p ssl c", ssl=4)), "dve")
                marks[("ytr_ev", idx)] = now("dve")

            for ssl in range(4):
                ss = 4 * sc_ + ssl

                @sec("sync")
                def _(sy, ss=ss, sc_=sc_, ssl=ssl):
                    if not (sc_ == 0 and ssl == 0):
                        prevss = ss - 1
                        w(sy, "dve", marks[("fin", prevss // 4, prevss % 4)])
                    inc(sy.dma_start(out=h1l[:], in_=attn_d.ap()[128 * ss:128 * (ss + 1), :]), "sp", 16)
                    inc(sy.dma_start(out=xql[:], in_=x_q[128 * ss:128 * (ss + 1), :]), "sp", 16)
                marks[("fin_ld", sc_, ssl)] = now("sp")

                @sec("scalar")
                def _(s_, sc_=sc_, ssl=ssl):
                    w(s_, "dve", marks[("ytr_ev", sc_ * 16 + 15)])
                    if not (sc_ == 0 and ssl == 0):
                        prevss = 4 * sc_ + ssl - 1
                        w(s_, "dve", marks[("fin", prevss // 4, prevss % 4)])
                        w(s_, "pld", marks[("out_st", prevss)])
                    yv = ysb4[:].rearrange("p (s c) -> p s c", s=4)
                    bar(s_, s_.activation(ynrm[:], yv[:, ssl, :], AF.Square,
                                          accum_out=rescol[:, 2 * ssl:2 * ssl + 1]), "act")
                    bar(s_, s_.activation(rescol[:, 2 * ssl + 1:2 * ssl + 2], rescol[:, 2 * ssl:2 * ssl + 1],
                                          AF.Ln, scale=1.0 / D, bias=epsrms[:, 0:1]), "act")
                    inc(s_.activation(rescol[:, 2 * ssl + 1:2 * ssl + 2],
                                      rescol[:, 2 * ssl + 1:2 * ssl + 2], AF.Exp, scale=-0.5), "act")
                marks[("rms", sc_, ssl)] = now("act")

                @sec("vector")
                def _(v, sc_=sc_, ssl=ssl):
                    w(v, "act", marks[("rms", sc_, ssl)])
                    w(v, "sp", marks[("fin_ld", sc_, ssl)])
                    ss2 = 4 * sc_ + ssl
                    if ss2 >= 1:
                        w(v, "pld", marks[("out_st", ss2 - 1)])
                    yv = ysb4[:].rearrange("p (s c) -> p s c", s=4)
                    v.tensor_scalar(ynrm[:], yv[:, ssl, :], rescol[:, 2 * ssl + 1:2 * ssl + 2], None, OP.mult)
                    v.tensor_tensor(ynrm[:], ynrm[:], h1l[:], OP.add)
                    inc(v.tensor_tensor(ynrm[:], ynrm[:], xql[:], OP.add), "dve")
                marks[("fin", sc_, ssl)] = now("dve")

                @sec("gpsimd")
                def _(g_, ss=ss, sc_=sc_, ssl=ssl):
                    w(g_, "dve", marks[("fin", sc_, ssl)])
                    inc(g_.dma_start(out=out_q[128 * ss:128 * (ss + 1), :], in_=ynrm[:]), "pld", 16)
                marks[("out_st", ss)] = now("pld")

        @sec("sync")
        def _(sy):
            w(sy, "pld", now("pld"))

        p4b.close()
        p4base.close()
        pB.close()

    return nc


_NC_CACHE = {}


def _get_nc(debug=False):
    if debug not in _NC_CACHE:
        _NC_CACHE[debug] = build_nc(debug=debug)
    return _NC_CACHE[debug]


def _host_constants():
    import jax
    import jax.numpy as jnp
    try:
        cpu = jax.local_devices(backend="cpu")[0]
        dev_ctx = jax.default_device(cpu)
    except Exception:
        import contextlib
        dev_ctx = contextlib.nullcontext()
    with dev_ctx:
        key = jax.random.key(42, impl="threefry2x32")
        proj = np.asarray(jax.random.normal(key, (DH, JL), jnp.float32)
                          / np.sqrt(JL))
        inv_freq = 1.0 / (10000.0 ** (jnp.arange(0, DH, 2, dtype=jnp.float32) / DH))
        freqs = jnp.outer(jnp.arange(S, dtype=jnp.float32), inv_freq)
        cos = np.asarray(jnp.cos(freqs), np.float32)
        sin = np.asarray(jnp.sin(freqs), np.float32)
    return proj.astype(BF), cos, sin


def kernel(L0_attn_x, L0_attn_Wq, L0_attn_Wk, L0_attn_Wv, L0_attn_Wo,
           L0_attn_q_gain, L0_ffn_gate_up_packed, L0_ffn_gate_up_scales,
           L0_ffn_down_packed, L0_ffn_down_scales, _debug=False):
    x = np.asarray(L0_attn_x)
    Wq = np.asarray(L0_attn_Wq)
    Wk = np.asarray(L0_attn_Wk)
    Wv = np.asarray(L0_attn_Wv)
    Wo = np.asarray(L0_attn_Wo)
    gain = np.asarray(L0_attn_q_gain)
    gup = np.asarray(L0_ffn_gate_up_packed)
    gus = np.asarray(L0_ffn_gate_up_scales)
    dnp = np.asarray(L0_ffn_down_packed)
    dns = np.asarray(L0_ffn_down_scales)

    projb, cos, sin = _host_constants()
    identb = np.eye(128).astype(BF)
    identf = np.eye(128, dtype=np.float32)

    nc = _get_nc(debug=_debug)
    in_maps = []
    for c in range(8):
        b, g = c // 4, c % 4
        dns_rows = np.ascontiguousarray(
            np.stack([dns[(11 * c + s_) // 2] for s_ in range(11)], axis=0))
        in_maps.append(dict(
            x_b=np.ascontiguousarray(x[b]),
            x_q=np.ascontiguousarray(x[b, QTOK * g:QTOK * (g + 1)]),
            wq_h=np.ascontiguousarray(Wq[512 * g:512 * (g + 1)]),
            wk_h=np.ascontiguousarray(Wk[512 * g:512 * (g + 1)]),
            wv_h=np.ascontiguousarray(Wv[512 * g:512 * (g + 1)]),
            wo_s=np.ascontiguousarray(Wo[:, 256 * c:256 * (c + 1)]),
            gup_s=np.ascontiguousarray(gup[:, 32 * c:32 * (c + 1)]),
            gus_s=np.ascontiguousarray(gus[2 * c:2 * (c + 1)]),
            dnp_s=np.ascontiguousarray(dnp[:, 88 * c:88 * (c + 1)]),
            dns_s=dns_rows,
            gain_s=np.ascontiguousarray(gain[4 * g:4 * (g + 1)]).reshape(1, 4).astype(np.float32),
            cos_t=cos, sin_t=sin, proj_c=projb,
            id_b=identb, id_f=identf,
        ))

    res = run_bass_kernel_spmd(nc, in_maps, core_ids=list(range(8)))
    out = np.empty((B, S, D), np.float32)
    for c in range(8):
        b, g = c // 4, c % 4
        out[b, QTOK * g:QTOK * (g + 1)] = np.asarray(res.results[c]["out_q"])
    if _debug:
        kernel._dbg = res.results
    return out


# version 06
# speedup vs baseline: 1.1613x; 1.1613x over previous
import numpy as np
import ml_dtypes
import concourse.bass as bass
import concourse.mybir as mybir
from concourse.alu_op_type import AluOpType
from concourse.bass_utils import run_bass_kernel_spmd
from contextlib import ExitStack

F32 = mybir.dt.float32
BF16 = mybir.dt.bfloat16
F16 = mybir.dt.float16
I32 = mybir.dt.int32
U16 = mybir.dt.uint16
U32 = mybir.dt.uint32
AF = mybir.ActivationFunctionType
OP = AluOpType
BF = ml_dtypes.bfloat16

B, S, D, H, DH = 2, 4096, 2048, 16, 128
NT = S // 128
HPC = 4
QTOK = 1024
NB = 64
JL = 32
IFFN = 5632
SC_QK = float(1.0 / np.sqrt(128.0))
BCOEF = float(np.sqrt(128.0) * 1.0001)


def build_nc(debug=False):
    nc = bass.Bass("TRN2", target_bir_lowering=False, enable_partition_id=True)

    x_b = nc.dram_tensor("x_b", [S, D], F32, kind="ExternalInput").ap()
    x_q = nc.dram_tensor("x_q", [QTOK, D], F32, kind="ExternalInput").ap()
    wq_h = nc.dram_tensor("wq_h", [512, D], F32, kind="ExternalInput").ap()
    wk_h = nc.dram_tensor("wk_h", [512, D], F32, kind="ExternalInput").ap()
    wv_h = nc.dram_tensor("wv_h", [512, D], F32, kind="ExternalInput").ap()
    wo_s = nc.dram_tensor("wo_s", [D, 256], F32, kind="ExternalInput").ap()
    gup_s = nc.dram_tensor("gup_s", [2 * IFFN, 32], I32, kind="ExternalInput").ap()
    gus_s = nc.dram_tensor("gus_s", [2, 2 * IFFN], F16, kind="ExternalInput").ap()
    dnp_s = nc.dram_tensor("dnp_s", [D, 88], I32, kind="ExternalInput").ap()
    dns_s = nc.dram_tensor("dns_s", [11, D], F16, kind="ExternalInput").ap()
    gain_s = nc.dram_tensor("gain_s", [1, 4], F32, kind="ExternalInput").ap()
    cos_t = nc.dram_tensor("cos_t", [S, 64], F32, kind="ExternalInput").ap()
    sin_t = nc.dram_tensor("sin_t", [S, 64], F32, kind="ExternalInput").ap()
    proj_c = nc.dram_tensor("proj_c", [128, JL], BF16, kind="ExternalInput").ap()
    id_b = nc.dram_tensor("id_b", [128, 128], BF16, kind="ExternalInput").ap()
    id_f = nc.dram_tensor("id_f", [128, 128], F32, kind="ExternalInput").ap()

    woT_in = nc.dram_tensor("woT_in", [256, D], BF16)
    woT_all = nc.dram_tensor("woT_all", [D, D], BF16)
    wguT_in = nc.dram_tensor("wguT_in", [256, 2 * IFFN], BF16)
    wguT_all = nc.dram_tensor("wguT_all", [D, 2 * IFFN], BF16)
    wdT_in = nc.dram_tensor("wdT_in", [704, D], BF16)
    wdT_all = nc.dram_tensor("wdT_all", [IFFN, D], BF16)
    y_in = nc.dram_tensor("y_in", [512, S], BF16)
    y_all = nc.dram_tensor("y_all", [2048, S], BF16)
    attn_d = nc.dram_tensor("attn_d", [QTOK, D], F32)

    out_q = nc.dram_tensor("out_q", [QTOK, D], F32, kind="ExternalOutput").ap()
    dbg_t = {}
    if debug:
        for nm, shp, dt in [
            ("d_qfm", [128, HPC * S], BF16), ("d_kfm", [128, HPC * S], BF16),
            ("d_vres", [128, HPC * 136 * NT], BF16),
            ("d_kp", [JL, S], F32), ("d_kpn", [JL, S], F32),
            ("d_cent", [JL, NB], F32), ("d_scores", [1, NB], F32),
            ("d_pen", [128, HPC * 32], F32), ("d_yin", [512, S], BF16),
            ("d_yall", [2048, S], BF16), ("d_attn", [QTOK, D], F32),
            ("d_x2t", [128, 16 * QTOK], BF16), ("d_wg", [D, 2 * IFFN], BF16),
            ("d_wd", [IFFN, D], BF16), ("d_hfm", [128, 44 * QTOK], BF16),
            ("d_woT", [D, D], BF16),
            ("d_x1t0", [128, D], BF16), ("d_lnc", [128, 16], F32),
            ("d_knlast", [128, 1024], BF16), ("d_kclast", [128, 16], F32),
            ("d_ksblast", [128, 1024], F32),
            ("d_qn0", [128, 512], BF16), ("d_kn0", [128, 512], BF16),
            ("d_ksb0", [128, 512], F32), ("d_qcol", [128, 16], F32),
        ]:
            dbg_t[nm] = nc.dram_tensor(nm, shp, dt, kind="ExternalOutput").ap()

    ctx = ExitStack()
    with ctx:
        SEM, CNT = {}, {}
        for s_ in ["sp", "pld", "pe", "act", "dve", "pl",
                   "cwo", "cgu", "cwd", "cy"]:
            SEM[s_] = ctx.enter_context(nc.semaphore(s_))
            CNT[s_] = 0

        def inc(instr, s_, n=1):
            instr.then_inc(SEM[s_], n)
            CNT[s_] += n
            return CNT[s_]

        def now(s_):
            return CNT[s_]

        def w(eng, s_, c=None):
            c = CNT[s_] if c is None else c
            if c > 0:
                eng.wait_ge(SEM[s_], c)

        def bar(eng, instr, s_):
            inc(instr, s_)
            eng.wait_ge(SEM[s_], CNT[s_])

        def sbp(name, shape, dt):
            return ctx.enter_context(nc.sbuf_tensor(name, shape, dt))

        identb = sbp("identb", [128, 128], BF16)
        identf = sbp("identf", [128, 128], F32)
        projb = sbp("projb", [128, JL], BF16)
        gains = sbp("gains", [1, 4], F32)
        negb = sbp("negb", [1, 4], F32)
        gbc4 = sbp("gbc4", [128, 4], F32)
        onesr = sbp("onesr", [1, 128], F32)
        indu = sbp("indu", [1, 128], F32)
        indl = sbp("indl", [1, 128], F32)
        epsln = sbp("epsln", [128, 1], F32)
        epsrms = sbp("epsrms", [128, 1], F32)
        pen_tab = sbp("pen_tab", [128, HPC * 32], F32)
        pidc = sbp("pidc", [1, 1], U32)
        scg = sbp("scg", [128, 2 * 88], F32)
        scd = sbp("scd", [128, 11 * 16], F32)
        ones32_col = sbp("ones32_col", [JL, 1], F32)

        block = ctx.enter_context(nc.Block())

        def sec(name):
            def deco(fn):
                getattr(block, name)(fn)
            return deco

        marks = {}

        def _ones32():
            return ones32_col[0:JL, 0:1]

        pA = ExitStack()
        qfm = pA.enter_context(nc.sbuf_tensor("qfm", [128, HPC * S], BF16))
        kfm = pA.enter_context(nc.sbuf_tensor("kfm", [128, HPC * S], BF16))
        vres = pA.enter_context(nc.sbuf_tensor("vres", [128, HPC * 136 * NT], BF16))
        qfmv = qfm[:].rearrange("p (h s) -> p h s", h=HPC)
        kfmv = kfm[:].rearrange("p (h s) -> p h s", h=HPC)
        vview = vres[:].rearrange("p (h t c) -> p h t c", h=HPC, t=NT)

        p01 = ExitStack()
        sb1 = lambda n, s_, d: p01.enter_context(nc.sbuf_tensor(n, s_, d))
        ps1 = lambda n, s_, d: p01.enter_context(nc.psum_tensor(n, s_, d))
        xld = sb1("xld", [128, 2 * D], F32)
        x1r = sb1("x1r", [128, 2 * D], BF16)
        x1T = sb1("x1T", [128, 2 * D], BF16)
        dump = sb1("dump", [128, D], BF16)
        lncol = sb1("lncol", [128, 2 * 8], F32)
        cosld = sb1("cosld", [128, 2 * 64], F32)
        sinld = sb1("sinld", [128, 2 * 64], F32)
        wqkvT = sb1("wqkvT", [128, 3 * 16 * 512], BF16)
        pW = ps1("pW", [128, 512], F32)
        pQ = ps1("pQ", [128, 1024], F32)
        pK = ps1("pK", [128, 512], F32)
        pV = ps1("pV", [128, 512], F32)
        pX1T = ps1("pX1T", [128, 2048], BF16)
        pQKT = ps1("pQKT", [128, 1024], BF16)

        p0t = ExitStack()
        sb0 = lambda n, s_, d: p0t.enter_context(nc.sbuf_tensor(n, s_, d))
        wsl = sb0("wsl", [128, D], F32)
        wstage = sb0("wstage", [128, 2 * 512], BF16)
        g88 = sb0("g88", [88, 256], F16)
        d16 = sb0("d16", [16, 11 * 128], F16)
        cscg = sb0("cscg", [88, 128], F32)
        cscd = sb0("cscd", [16, 128], F32)

        @sec("sync")
        def _(sy):
            for dst, src in [(identb[:], id_b), (identf[:], id_f),
                             (projb[:], proj_c), (gains[:], gain_s),
                             (pidc[:], nc.partition_id_tensor.ap())]:
                inc(sy.dma_start(out=dst, in_=src), "sp", 16)
            inc(sy.dma_start(out=g88[:].rearrange("t (g q) -> t g q", g=2),
                             in_=gus_s.rearrange("g (t q) -> t g q", q=128)), "sp", 16)
            inc(sy.dma_start(out=d16[:].rearrange("t (s q) -> t s q", s=11),
                             in_=dns_s.rearrange("s (t q) -> t s q", q=128)), "sp", 16)
        marks["consts"] = now("sp")

        @sec("vector")
        def _(v):
            w(v, "sp", marks["consts"])
            v.memset(onesr[:], 1.0)
            v.memset(ones32_col[:], 1.0)
            v.memset(indu[0:1, 0:64], 1.0)
            v.memset(indu[0:1, 64:128], 0.0)
            v.memset(indl[0:1, 0:64], 0.0)
            v.memset(indl[0:1, 64:128], 1.0)
            v.memset(epsln[:], 1e-5)
            v.memset(epsrms[:], 1e-6)
            v.memset(vres[:].rearrange("p (h t c) -> p h t c", h=HPC, t=NT)[:, :, :, 128:129], 1.0)
            bar(v, v.tensor_scalar(negb[:], gains[:], -1.0, None, OP.mult), "dve")
            bar(v, v.tensor_tensor(negb[:], negb[:], gains[:], OP.max), "dve")
            inc(v.tensor_scalar(negb[:], negb[:], -BCOEF, None, OP.mult), "dve")
        marks["p0v"] = now("dve")

        @sec("tensor")
        def _(t):
            w(t, "dve", marks["p0v"])
            inc(t.matmul(pW[:, 0:4], onesr[:], gains[:], start=True, stop=True), "pe")
        marks["pG4"] = now("pe")

        @sec("scalar")
        def _(s_):
            w(s_, "pe", marks["pG4"])
            inc(s_.copy(gbc4[:], pW[:, 0:4]), "act")
        marks["gbc4"] = now("act")

        marks["wTevlast"] = 0
        marks["wTplast"] = 0
        for wi, wsrc in enumerate([wq_h, wk_h, wv_h]):
            for rc in range(4):
                @sec("sync")
                def _(sy, wi=wi, rc=rc, wsrc=wsrc):
                    if wi == 0 and rc == 0:
                        w(sy, "sp", marks["consts"])
                    else:
                        w(sy, "pe", marks["wTplast"])
                    inc(sy.dma_start(out=wsl[:], in_=wsrc[128 * rc:128 * (rc + 1), :]), "sp", 16)
                marks["wld"] = now("sp")

                for jb in range(4):
                    @sec("tensor")
                    def _(t, wi=wi, rc=rc, jb=jb):
                        w(t, "sp", marks["wld"])
                        w(t, "act", marks["wTevlast"] or marks["gbc4"])
                        for u in range(4):
                            j = 4 * jb + u
                            ins = t.transpose(pW[:, 128 * u:128 * (u + 1)],
                                              wsl[:, 128 * j:128 * (j + 1)], identf[:])
                        inc(ins, "pe")
                    marks["wTplast"] = now("pe")

                    @sec("scalar")
                    def _(s_, wi=wi, rc=rc, jb=jb):
                        w(s_, "pe", marks["wTplast"])
                        dst = wqkvT[:].rearrange("p (w ic c) -> p w ic c", w=3, ic=16)
                        inc(s_.copy(dst[:, wi, 4 * jb:4 * (jb + 1), 128 * rc:128 * (rc + 1)],
                                    pW[:, 0:512].rearrange("p (j c) -> p j c", j=4)), "act")
                    marks["wTevlast"] = now("act")
        marks["wqkvT"] = now("act")

        marks["wost"] = 0
        for L in range(4):
            @sec("sync")
            def _(sy, L=L):
                w(sy, "pe", marks["wTplast"])
                inc(sy.dma_start(out=wsl[:, 0:1024].rearrange("p (r c) -> p r c", r=4),
                                 in_=wo_s.rearrange("(l r p) c -> p l r c", p=128, r=4)[:, L, :, :]), "sp", 16)
            marks["wold"] = now("sp")
            for ih in range(2):
                @sec("tensor")
                def _(t, L=L, ih=ih):
                    w(t, "sp", marks["wold"])
                    w(t, "act", marks["wTevlast"])
                    for rc in range(4):
                        ins = t.transpose(pW[:, 128 * rc:128 * (rc + 1)],
                                          wsl[:, 0:1024].rearrange("p (r c) -> p r c", r=4)[:, rc, 128 * ih:128 * (ih + 1)],
                                          identf[:])
                    inc(ins, "pe")
                marks["wTplast"] = now("pe")

                @sec("scalar")
                def _(s_, L=L, ih=ih):
                    w(s_, "pe", marks["wTplast"])
                    par = (2 * L + ih) % 2
                    w(s_, "pld", marks["wost"])
                    inc(s_.copy(wstage[:, 512 * par:512 * (par + 1)], pW[:, 0:512]), "act")
                marks["wTevlast"] = now("act")

                @sec("gpsimd")
                def _(g_, L=L, ih=ih):
                    w(g_, "act", marks["wTevlast"])
                    par = (2 * L + ih) % 2
                    dst = bass.AP(woT_in, ih * 128 * D + 512 * L, [[D, 128], [1, 512]])
                    inc(g_.dma_start(out=dst, in_=wstage[:, 512 * par:512 * (par + 1)]), "pld", 16)
                marks["wost"] = now("pld")
        marks["wo_all_st"] = now("pld")

        for g in range(2):
            @sec("vector")
            def _(v, g=g):
                if g > 0:
                    w(v, "pe", marks["wTplast"])
                gv = g88[:].rearrange("t (g q) -> t g q", g=2)
                inc(v.tensor_copy(cscg[:], gv[:, g, :]), "dve")
            marks["scc"] = now("dve")

            @sec("tensor")
            def _(t, g=g):
                w(t, "act", marks["wTevlast"])
                w(t, "dve", marks["scc"])
                inc(t.transpose(pW[:, 0:88], cscg[:], identf[0:88, 0:88]), "pe")
            marks["wTplast"] = now("pe")

            @sec("scalar")
            def _(s_, g=g):
                w(s_, "pe", marks["wTplast"])
                dst = scg[:].rearrange("p (g t) -> p g t", g=2)
                inc(s_.copy(dst[:, g, :], pW[:, 0:88]), "act")
            marks["wTevlast"] = now("act")
        for sg in range(11):
            @sec("vector")
            def _(v, sg=sg):
                w(v, "pe", marks["wTplast"])
                dv_ = d16[:].rearrange("t (s q) -> t s q", s=11)
                inc(v.tensor_copy(cscd[:], dv_[:, sg, :]), "dve")
            marks["scc"] = now("dve")

            @sec("tensor")
            def _(t, sg=sg):
                w(t, "act", marks["wTevlast"])
                w(t, "dve", marks["scc"])
                inc(t.transpose(pW[:, 0:16], cscd[:], identf[0:16, 0:16]), "pe")
            marks["wTplast"] = now("pe")

            @sec("scalar")
            def _(s_, sg=sg):
                w(s_, "pe", marks["wTplast"])
                dst = scd[:].rearrange("p (s t) -> p s t", s=11)
                inc(s_.copy(dst[:, sg, :], pW[:, 0:16]), "act")
            marks["wTevlast"] = now("act")
        marks["scales"] = now("act")

        @sec("gpsimd")
        def _(g_):
            w(g_, "pld", marks["wo_all_st"])
            inc(g_.collective_compute(
                "AllGather", OP.bypass, replica_groups=[list(range(8))],
                ins=[woT_in.ap().opt()], outs=[woT_all.ap().opt()]), "cwo")

        p0t.close()
        ksb = sb1("ksb", [128, 2 * 512], F32)
        qtmp = sb1("qtmp", [128, 512], F32)
        ktmp = sb1("ktmp", [128, 512], F32)
        qr_s = sb1("qr_s", [128, 512], F32)
        kr_s = sb1("kr_s", [128, 512], F32)
        qn_r = sb1("qn_r", [128, 2 * 512], BF16)
        kn_r = sb1("kn_r", [128, 2 * 512], BF16)
        qcol = sb1("qcol", [128, 2 * 8], F32)
        kcol = sb1("kcol", [128, 2 * 8], F32)

        lnc = lncol[:].rearrange("p (r c) -> p r c", r=2)
        qc = qcol[:].rearrange("p (r c) -> p r c", r=2)
        kc = kcol[:].rearrange("p (r c) -> p r c", r=2)

        for i in range(NT):
            pi = i % 2

            if debug and i == 2:
                @sec("vector")
                def _(v):
                    w(v, "pld", marks["dbg_t0"])
                    v.engine_nop() if False else None
                    inc(v.memset(qtmp[0:1, 0:1], 0.0), "dve")
                @sec("scalar")
                def _(s_):
                    w(s_, "pld", marks["dbg_t0"])
                    inc(s_.copy(epsln[0:1, 0:1], epsln[0:1, 0:1]), "act")
                @sec("gpsimd")
                def _(g_):
                    w(g_, "pld", marks["dbg_t0"])
                    inc(g_.memset(ktmp[0:1, 0:1], 0.0), "pl")

            @sec("sync")
            def _(sy, i=i, pi=pi):
                if i >= 2:
                    w(sy, "act", marks[("lnacc", i - 2)])
                    w(sy, "dve", marks[("qn", i - 2)])
                    w(sy, "pl", marks[("kn", i - 2)])
                inc(sy.dma_start(out=xld[:, D * pi:D * (pi + 1)],
                                 in_=x_b[128 * i:128 * (i + 1), :]), "sp", 16)
                inc(sy.dma_start(out=cosld[:, 64 * pi:64 * (pi + 1)],
                                 in_=cos_t[128 * i:128 * (i + 1), :]), "sp", 16)
                inc(sy.dma_start(out=sinld[:, 64 * pi:64 * (pi + 1)],
                                 in_=sin_t[128 * i:128 * (i + 1), :]), "sp", 16)
            marks[("xld", i)] = now("sp")

            @sec("scalar")
            def _(s_, i=i, pi=pi):
                w(s_, "sp", marks[("xld", i)])
                if i >= 2:
                    w(s_, "dve", marks[("lncols", i - 2)])
                xt = xld[:, D * pi:D * (pi + 1)]
                s_.activation(dump[:], xt, AF.Copy, accum_out=lnc[:, pi, 0:1])
                inc(s_.activation(dump[:], xt, AF.Square, accum_out=lnc[:, pi, 1:2]), "act")
            marks[("lnacc", i)] = now("act")

            @sec("vector")
            def _(v, i=i, pi=pi):
                w(v, "act", marks[("lnacc", i)])
                v.tensor_scalar(lnc[:, pi, 2:3], lnc[:, pi, 0:1], 1.0 / D, None, OP.mult)
                bar(v, v.scalar_tensor_tensor(lnc[:, pi, 5:6], lnc[:, pi, 0:1],
                                              1.0 / (D * D), lnc[:, pi, 0:1],
                                              OP.mult, OP.mult), "dve")
                inc(v.scalar_tensor_tensor(lnc[:, pi, 3:4], lnc[:, pi, 1:2], 1.0 / D,
                                           lnc[:, pi, 5:6], OP.mult, OP.subtract), "dve")
            marks[("lncols", i)] = now("dve")

            @sec("scalar")
            def _(s_, i=i, pi=pi):
                w(s_, "dve", marks[("lncols", i)])
                bar(s_, s_.activation(lnc[:, pi, 4:5], lnc[:, pi, 3:4], AF.Ln, bias=epsln[:, 0:1]), "act")
                inc(s_.activation(lnc[:, pi, 4:5], lnc[:, pi, 4:5], AF.Exp, scale=-0.5), "act")
            marks[("lnrstd", i)] = now("act")

            @sec("vector")
            def _(v, i=i, pi=pi):
                w(v, "act", marks[("lnrstd", i)])
                if i >= 2:
                    w(v, "pe", marks[("x1Tp", i - 2)])
                xt = xld[:, D * pi:D * (pi + 1)]
                inc(v.tensor_scalar(x1r[:, D * pi:D * (pi + 1)], xt,
                                    lnc[:, pi, 2:3], lnc[:, pi, 4:5],
                                    OP.subtract, OP.mult), "dve")
            marks[("x1", i)] = now("dve")

            @sec("tensor")
            def _(t, i=i, pi=pi):
                w(t, "dve", marks[("x1", i)])
                if i >= 1:
                    w(t, "dve", marks[("x1Tev", i - 1)])
                for j in range(16):
                    ins = t.transpose(pX1T[:, 128 * j:128 * (j + 1)],
                                      x1r[:, D * pi + 128 * j: D * pi + 128 * (j + 1)],
                                      identb[:])
                inc(ins, "pe")
            marks[("x1Tp", i)] = now("pe")

            @sec("vector")
            def _(v, i=i):
                w(v, "pe", marks[("x1Tp", i)])
                if i >= 2:
                    w(v, "pe", marks[("qkv", i - 2)])
                ri = i % 2
                v.tensor_copy(x1T[:, 2048 * ri:2048 * ri + 1024], pX1T[:, 0:1024])
                inc(v.tensor_copy(x1T[:, 2048 * ri + 1024:2048 * ri + 2048], pX1T[:, 1024:2048]), "dve")
            marks[("x1Tev", i)] = now("dve")

            @sec("tensor")
            def _(t, i=i):
                w(t, "dve", marks[("x1Tev", i)])
                w(t, "act", marks["wqkvT"])
                ri = i % 2
                wv_ = wqkvT[:].rearrange("p (w ic c) -> p w ic c", w=3, ic=16)
                qh = 512 * (i % 2)
                views = [pQ[:, qh:qh + 512], pK[:, 0:512], pV[:, 0:512]]
                for wi, pp in enumerate(views):
                    if wi == 0 and i >= 2:
                        w(t, "dve", marks[("qn", i - 2)])
                    elif wi == 1 and i >= 1:
                        w(t, "act", marks[("kev", i - 1)])
                    elif wi == 2 and i >= 1:
                        w(t, "act", marks[("vev", i - 1)])
                    for j in range(16):
                        ins = t.matmul(pp,
                                       x1T[:, 2048 * ri + 128 * j: 2048 * ri + 128 * (j + 1)],
                                       wv_[:, wi, j, :],
                                       start=(j == 0), stop=(j == 15))
                    inc(ins, "pe")
                    marks[("qkvg%d" % wi, i)] = now("pe")
            marks[("qkv", i)] = now("pe")

            @sec("scalar")
            def _(s_, i=i, pi=pi):
                w(s_, "pe", marks[("qkvg0", i)])
                if i >= 1:
                    w(s_, "dve", marks[("qn", i - 1)])
                inc(s_.activation(qr_s[:], pQ[:, 512 * (i % 2):512 * (i % 2) + 512], AF.Square), "act")
                marks[("qev", i)] = now("act")
                w(s_, "pe", marks[("qkvg1", i)])
                if i >= 2:
                    w(s_, "pl", marks[("kn", i - 2)])
                if i >= 1:
                    w(s_, "pl", marks[("knrope", i - 1)])
                s_.copy(ksb[:, 512 * pi:512 * (pi + 1)], pK[:, 0:512])
                inc(s_.activation(kr_s[:], ksb[:, 512 * pi:512 * (pi + 1)], AF.Square), "act")
                marks[("kev", i)] = now("act")
                w(s_, "pe", marks[("qkvg2", i)])
                inc(s_.copy(vview[:, :, i, 0:128], pV[:].rearrange("p (h c) -> p h c", h=4)), "act")
                marks[("vev", i)] = now("act")

            @sec("vector")
            def _(v, i=i, pi=pi):
                w(v, "act", marks[("qev", i)])
                bar(v, v.reduce_sum(qc[:, pi, 0:4],
                                    qr_s[:].rearrange("p (h c) -> p h c", h=4),
                                    axis=mybir.AxisListType.X), "dve")
                inc(v.tensor_scalar(qc[:, pi, 0:4], qc[:, pi, 0:4], 1.0 / DH, 1e-6,
                                    OP.mult, OP.add), "dve")
            marks[("qA", i)] = now("dve")

            @sec("vector")
            def _(v, i=i, pi=pi):
                w(v, "act", marks[("kev", i)])
                bar(v, v.reduce_sum(kc[:, pi, 0:4],
                                    kr_s[:].rearrange("p (h c) -> p h c", h=4),
                                    axis=mybir.AxisListType.X), "dve")
                inc(v.tensor_scalar(kc[:, pi, 0:4], kc[:, pi, 0:4], 1.0 / DH, 1e-6,
                                    OP.mult, OP.add), "dve")
            marks[("kA", i)] = now("dve")

            @sec("scalar")
            def _(s_, i=i, pi=pi):
                w(s_, "dve", marks[("qA", i)])
                w(s_, "act", marks["gbc4"])
                bar(s_, s_.activation(qc[:, pi, 0:4], qc[:, pi, 0:4], AF.Ln), "act")
                inc(s_.activation(qc[:, pi, 0:4], qc[:, pi, 0:4], AF.Exp, scale=-0.5), "act")
                marks[("qB", i)] = now("act")
                w(s_, "dve", marks[("kA", i)])
                bar(s_, s_.activation(kc[:, pi, 0:4], kc[:, pi, 0:4], AF.Ln), "act")
                inc(s_.activation(kc[:, pi, 0:4], kc[:, pi, 0:4], AF.Exp, scale=-0.5), "act")
                marks[("kB", i)] = now("act")

            @sec("vector")
            def _(v, i=i, pi=pi):
                w(v, "act", marks[("qB", i)])
                bar(v, v.tensor_tensor(qc[:, pi, 4:8], qc[:, pi, 0:4], gbc4[:, 0:4], OP.mult), "dve")
                qv = pQ[:, 512 * pi:512 * (pi + 1)].rearrange("p (h two d) -> p h two d", h=4, two=2)
                rv = qr_s[:].rearrange("p (h two d) -> p h two d", h=4, two=2)
                tmp = qtmp[:].rearrange("p (h two d) -> p h two d", h=4, two=2)
                cb = cosld[:, 64 * pi:64 * (pi + 1)].rearrange("p (o d) -> p o d", o=1).broadcast_to((128, 4, 64))
                sbr = sinld[:, 64 * pi:64 * (pi + 1)].rearrange("p (o d) -> p o d", o=1).broadcast_to((128, 4, 64))
                if i >= 2:
                    w(v, "pe", marks[("qkTp", i - 2)])
                v.tensor_tensor(rv[:, :, 0, :], qv[:, :, 0, :], cb, OP.mult)
                v.tensor_tensor(rv[:, :, 1, :], qv[:, :, 1, :], cb, OP.mult)
                v.tensor_tensor(tmp[:, :, 0, :], qv[:, :, 1, :], sbr, OP.mult)
                v.tensor_tensor(tmp[:, :, 1, :], qv[:, :, 0, :], sbr, OP.mult)
                v.tensor_tensor(rv[:, :, 0, :], rv[:, :, 0, :], tmp[:, :, 0, :], OP.add)
                v.tensor_tensor(rv[:, :, 1, :], rv[:, :, 1, :], tmp[:, :, 1, :], OP.subtract)
                qnv = qn_r[:, 512 * pi:512 * (pi + 1)]
                for hh in range(4):
                    ins = v.tensor_scalar(qnv[:, 128 * hh:128 * (hh + 1)],
                                          qr_s[:, 128 * hh:128 * (hh + 1)],
                                          qc[:, pi, 4 + hh:5 + hh], None, OP.mult)
                inc(ins, "dve")
            marks[("qn", i)] = now("dve")

            @sec("gpsimd")
            def _(g_, i=i, pi=pi):
                w(g_, "act", marks[("kB", i)])
                kv = ksb[:, 512 * pi:512 * (pi + 1)].rearrange("p (h two d) -> p h two d", h=4, two=2)
                rv = kr_s[:].rearrange("p (h two d) -> p h two d", h=4, two=2)
                tmp = ktmp[:].rearrange("p (h two d) -> p h two d", h=4, two=2)
                cb = cosld[:, 64 * pi:64 * (pi + 1)].rearrange("p (o d) -> p o d", o=1).broadcast_to((128, 4, 64))
                sbr = sinld[:, 64 * pi:64 * (pi + 1)].rearrange("p (o d) -> p o d", o=1).broadcast_to((128, 4, 64))
                if i >= 2:
                    w(g_, "pe", marks[("qkTp", i - 2)])
                g_.tensor_tensor(rv[:, :, 0, :], kv[:, :, 0, :], cb, OP.mult)
                g_.tensor_tensor(rv[:, :, 1, :], kv[:, :, 1, :], cb, OP.mult)
                g_.tensor_tensor(tmp[:, :, 0, :], kv[:, :, 1, :], sbr, OP.mult)
                g_.tensor_tensor(tmp[:, :, 1, :], kv[:, :, 0, :], sbr, OP.mult)
                g_.tensor_tensor(rv[:, :, 0, :], rv[:, :, 0, :], tmp[:, :, 0, :], OP.add)
                g_.tensor_tensor(rv[:, :, 1, :], rv[:, :, 1, :], tmp[:, :, 1, :], OP.subtract)
                knv = kn_r[:, 512 * pi:512 * (pi + 1)]
                for hh in range(4):
                    ins = g_.tensor_scalar(knv[:, 128 * hh:128 * (hh + 1)],
                                           kr_s[:, 128 * hh:128 * (hh + 1)],
                                           kc[:, pi, 0 + hh:1 + hh], None, OP.mult)
                inc(ins, "pl")
            marks[("kn", i)] = now("pl")
            marks[("knrope", i)] = marks[("kn", i)]

            def emit_qkt(I):
                PI = I % 2

                @sec("tensor")
                def _(t, I=I, PI=PI):
                    w(t, "dve", marks[("qn", I)])
                    w(t, "pl", marks[("kn", I)])
                    if I >= 1:
                        w(t, "dve", marks[("qkTev", I - 1)])
                    for hh in range(4):
                        t.transpose(pQKT[:, 128 * hh:128 * (hh + 1)],
                                    qn_r[:, 512 * PI + 128 * hh: 512 * PI + 128 * (hh + 1)], identb[:])
                    for hh in range(4):
                        ins = t.transpose(pQKT[:, 512 + 128 * hh:512 + 128 * (hh + 1)],
                                          kn_r[:, 512 * PI + 128 * hh: 512 * PI + 128 * (hh + 1)], identb[:])
                    inc(ins, "pe")
                marks[("qkTp", I)] = now("pe")

                @sec("vector")
                def _(v, I=I):
                    w(v, "pe", marks[("qkTp", I)])
                    v.tensor_copy(qfmv[:, :, 128 * I:128 * (I + 1)],
                                  pQKT[:, 0:512].rearrange("p (h c) -> p h c", h=4))
                    inc(v.tensor_copy(kfmv[:, :, 128 * I:128 * (I + 1)],
                                      pQKT[:, 512:1024].rearrange("p (h c) -> p h c", h=4)), "dve")
                marks[("qkTev", I)] = now("dve")

            if i >= 1:
                emit_qkt(i - 1)
            if i == NT - 1:
                emit_qkt(i)

            if debug and i == 0:
                @sec("gpsimd")
                def _(g_):
                    w(g_, "dve", marks[("qn", 0)])
                    w(g_, "pl", marks[("kn", 0)])
                    inc(g_.dma_start(out=dbg_t["d_x1t0"], in_=x1r[:, 0:D]), "pld", 16)
                    inc(g_.dma_start(out=dbg_t["d_lnc"], in_=lncol[:]), "pld", 16)
                    inc(g_.dma_start(out=dbg_t["d_qn0"], in_=qn_r[:, 0:512]), "pld", 16)
                    inc(g_.dma_start(out=dbg_t["d_kn0"], in_=kn_r[:, 0:512]), "pld", 16)
                    inc(g_.dma_start(out=dbg_t["d_ksb0"], in_=ksb[:, 0:512]), "pld", 16)
                    inc(g_.dma_start(out=dbg_t["d_qcol"], in_=qcol[:]), "pld", 16)
                marks["dbg_t0"] = now("pld")

        marks["p1done_act"] = now("act")
        marks["p1done_dve"] = now("dve")
        marks["p1done_pe"] = now("pe")

        if debug:
            @sec("gpsimd")
            def _(g_):
                w(g_, "act", marks["p1done_act"])
                inc(g_.dma_start(out=dbg_t["d_knlast"], in_=kn_r[:]), "pld", 16)
                inc(g_.dma_start(out=dbg_t["d_kclast"], in_=kcol[:]), "pld", 16)
                inc(g_.dma_start(out=dbg_t["d_ksblast"], in_=ksb[:]), "pld", 16)
                inc(g_.dma_start(out=dbg_t["d_qfm"], in_=qfm[:]), "pld", 16)
                inc(g_.dma_start(out=dbg_t["d_kfm"], in_=kfm[:]), "pld", 16)
                inc(g_.dma_start(out=dbg_t["d_vres"], in_=vres[:]), "pld", 16)
            marks["dbg_p1"] = now("pld")

        p01.close()

        p23 = ExitStack()
        sb2 = lambda n, s_, d: p23.enter_context(nc.sbuf_tensor(n, s_, d))
        ps2 = lambda n, s_, d: p23.enter_context(nc.psum_tensor(n, s_, d))
        kps = sb2("kps", [JL, S], F32)
        kpn = sb2("kpn", [JL, S], F32)
        rown = sb2("rown", [1, S], F32)
        cent = sb2("cent", [JL, NB], F32)
        centn = sb2("centn", [JL, NB], F32)
        qsum = sb2("qsum", [128, 1], F32)
        qsumb = sb2("qsumb", [128, 1], BF16)
        qp_s = sb2("qp_s", [JL, 1], F32)
        srow = sb2("srow", [1, 256], F32)
        nq_s = sb2("nq_s", [1, 4], F32)
        cmpm = sb2("cmpm", [64, 64], F32)
        cntc = sb2("cntc", [64, 2], F32)
        penr = sb2("penr", [1, 64], F32)
        nbrow = sb2("nbrow", [1, 32], F32)
        pexp = sb2("pexp", [128, 2 * 512], BF16)
        rcj = sb2("rcj", [128, 4], F32)
        ydiv = sb2("ydiv", [128, 512], BF16)
        yslab = sb2("yslab", [128, S], BF16)
        pkg = sb2("pkg", [128, 704], I32)
        dsc1 = sb2("dsc1", [128, 704], I32)
        dsc2 = sb2("dsc2", [128, 704], I32)
        dsc3 = sb2("dsc3", [128, 704], I32)
        dsc4 = sb2("dsc4", [128, 704], I32)
        dqg = sb2("dqg", [128, 5632], BF16)
        dstage = sb2("dstage", [128, 6144], BF16)
        pS0 = ps2("pS0", [128, 512], F32)
        pS1 = ps2("pS1", [128, 512], F32)
        pv_banks = [ps2(f"pv{j}", [128, 512], F32) for j in range(4)]
        pStat = ps2("pStat", [128, 512], F32)
        pYT = ps2("pYT", [128, 1024], BF16)

        marks["pYTfree"] = ("act", 0)

        dq_steps = []

        def dq_planes(eng, pkv, dqv, nib_ot, dep_marks, j0, j1):
            def fn():
                secname = "vector" if eng == "dve" else "gpsimd"

                @sec(secname)
                def _(e):
                    for (sname, mkey) in dep_marks:
                        w(e, sname, marks.get(mkey, 0))
                    half = slice(0, 704)
                    for j in range(j0, j1):
                        e.tensor_scalar(dsc1[:, half], pkv, 4 * j, 15,
                                        OP.logical_shift_right, OP.bitwise_and)
                        e.tensor_scalar(dsc2[:, half], dsc1[:, half], 7, 6,
                                        OP.bitwise_and, OP.logical_shift_left)
                        e.tensor_scalar(dsc3[:, half], dsc2[:, half], 2, 0x3E80, OP.mult, OP.add)
                        e.tensor_scalar(dsc4[:, half], dsc2[:, half], 0x3F00, None, OP.add)
                        e.tensor_tensor(dsc3[:, half], dsc3[:, half], dsc4[:, half], OP.min)
                        e.scalar_tensor_tensor(dsc3[:, half], dsc2[:, half], 252, dsc3[:, half],
                                               OP.mult, OP.min)
                        e.tensor_scalar(dsc1[:, half], dsc1[:, half], 8, 12,
                                        OP.bitwise_and, OP.logical_shift_left)
                        e.tensor_tensor(dsc4[:, half], dsc3[:, half], dsc1[:, half], OP.bitwise_or)
                        src = dsc4[:, half].bitcast(U16).rearrange(
                            "p (ot g two) -> p ot g two", ot=nib_ot, two=2)[:, :, :, 0:1]
                        ins = e.tensor_copy(dqv[:, :, :, j:j + 1], src)
                    inc(ins, "dve" if eng == "dve" else "pl")
            return fn

        def gu_chunk(k):
            def load():
                @sec("sync")
                def _(sy):
                    if k > 0:
                        w(sy, "dve", marks[("gu_pl", k - 1)])
                    else:
                        w(sy, "pe", marks["p1done_pe"])
                        w(sy, "act", marks["p1done_act"])
                        w(sy, "dve", marks[("qn", NT - 1)])
                        w(sy, "pl", marks[("kn", NT - 1)])
                        if debug:
                            w(sy, "pld", marks.get("dbg_p1", 0))
                    inc(sy.dma_start(out=pkg[:].rearrange("p (ot g) -> p ot g", ot=22),
                                     in_=gup_s.rearrange("(ot p) g -> p ot g", p=128)[:, 22 * k:22 * (k + 1), :]), "sp", 16)
                marks[("gu_ld", k)] = now("sp")
            dq_steps.append(load)

            dqv = dqg[:].bitcast(U16).rearrange("p (ot g j) -> p ot g j", ot=22, g=32)

            def dve_a():
                dq_planes("dve", pkg[:, 0:704], dqv, 22,
                          [("sp", ("gu_ld", k)), ("pe", ("gu_tp", k - 1))], 0, 3)()
                marks[("gu_dva", k)] = now("dve")

            def dve_b():
                dq_planes("dve", pkg[:, 0:704], dqv, 22, [], 3, 6)()
                marks[("gu_dvb", k)] = now("dve")
            dq_steps.append(dve_a)
            dq_steps.append(dve_b)

            def dve_c():
                dq_planes("dve", pkg[:, 0:704], dqv, 22, [], 6, 8)()
                marks[("gu_pl", k)] = now("dve")
            dq_steps.append(dve_c)

            def scales_T():
                @sec("vector")
                def _(v):
                    scgv = scg[:].rearrange("p (g t) -> p g t", g=2)
                    for ot in range(22):
                        for g in range(2):
                            ins = v.tensor_scalar(dqg[:, 256 * ot + 128 * g:256 * ot + 128 * (g + 1)],
                                                  dqg[:, 256 * ot + 128 * g:256 * ot + 128 * (g + 1)],
                                                  scgv[:, g, 22 * k + ot:22 * k + ot + 1], None, OP.mult)
                    inc(ins, "dve")
                marks[("gu_sc", k)] = now("dve")
                for ih in range(2):
                    o0 = 0
                    for nb_ in [4, 4, 4, 4, 4, 2]:
                        @sec("tensor")
                        def _(t, ih=ih, o0=o0, nb_=nb_):
                            w(t, "dve", marks[("gu_sc", k)])
                            fs, fc = marks["pYTfree"]
                            w(t, fs, fc)
                            for u in range(nb_):
                                ot = o0 + u
                                ins = t.transpose(pYT[:, 128 * u:128 * (u + 1)],
                                                  dqg[:, 256 * ot + 128 * ih:256 * ot + 128 * (ih + 1)],
                                                  identb[:])
                            inc(ins, "pe")
                        marks["pYTw"] = now("pe")

                        @sec("scalar")
                        def _(s_, ih=ih, o0=o0, nb_=nb_):
                            w(s_, "pe", marks["pYTw"])
                            w(s_, "pld", marks.get(("gu_st", k - 1), 0))
                            dsv = dstage[:, 0:5632].rearrange("p (i ot c) -> p i ot c", i=2, ot=22)
                            inc(s_.copy(dsv[:, ih, o0:o0 + nb_, :], pYT[:, 0:128 * nb_]), "act")
                        marks["pYTfree"] = ("act", now("act"))
                        o0 += nb_
                marks[("gu_tp", k)] = now("pe")
                marks[("gu_tpev", k)] = now("act")
            dq_steps.append(scales_T)

            def store():
                @sec("gpsimd")
                def _(g_):
                    w(g_, "act", marks[("gu_tpev", k)])
                    dsv = dstage[:, 0:5632].rearrange("p (i c) -> p i c", i=2)
                    dst = bass.AP(wguT_in, 2816 * k,
                                  [[2 * IFFN, 128], [128 * 2 * IFFN, 2], [1, 2816]])
                    inc(g_.dma_start(out=dst, in_=dsv), "pld", 16)
                marks[("gu_st", k)] = now("pld")
            dq_steps.append(store)

        def dn_chunk(m):
            def load():
                @sec("sync")
                def _(sy):
                    if m == 0:
                        w(sy, "dve", marks[("gu_sc", 3)])
                    else:
                        w(sy, "dve", marks[("dn_pl", m - 1)])
                    inc(sy.dma_start(out=pkg[:].rearrange("p (ot g) -> p ot g", ot=8),
                                     in_=dnp_s.rearrange("(ot p) g -> p ot g", p=128)[:, 8 * m:8 * (m + 1), :]), "sp", 16)
                marks[("dn_ld", m)] = now("sp")
            dq_steps.append(load)

            dqv = dqg[:].bitcast(U16).rearrange("p (ot g j) -> p ot g j", ot=8, g=88)

            def dve_a():
                dq_planes("dve", pkg[:, 0:704], dqv, 8,
                          [("sp", ("dn_ld", m)),
                           ("pe", ("dn_tp", m - 1) if m > 0 else ("gu_tp", 3))], 0, 3)()
                marks[("dn_dva", m)] = now("dve")

            def dve_b():
                dq_planes("dve", pkg[:, 0:704], dqv, 8, [], 3, 6)()
                marks[("dn_dvb", m)] = now("dve")
            dq_steps.append(dve_a)
            dq_steps.append(dve_b)

            def dve_c():
                dq_planes("dve", pkg[:, 0:704], dqv, 8, [], 6, 8)()
                marks[("dn_pl", m)] = now("dve")
            dq_steps.append(dve_c)

            def scales_T():
                @sec("vector")
                def _(v):
                    scdv = scd[:].rearrange("p (s t) -> p s t", s=11)
                    for ot in range(8):
                        for sg_ in range(11):
                            ins = v.tensor_scalar(dqg[:, 704 * ot + 64 * sg_:704 * ot + 64 * (sg_ + 1)],
                                                  dqg[:, 704 * ot + 64 * sg_:704 * ot + 64 * (sg_ + 1)],
                                                  scdv[:, sg_, 8 * m + ot:8 * m + ot + 1], None, OP.mult)
                    inc(ins, "dve")
                marks[("dn_sc", m)] = now("dve")
                for ot in range(8):
                    @sec("tensor")
                    def _(t, ot=ot):
                        w(t, "dve", marks[("dn_sc", m)])
                        fs, fc = marks["pYTfree"]
                        w(t, fs, fc)
                        for ih in range(5):
                            t.transpose(pYT[:, 128 * ih:128 * (ih + 1)],
                                        dqg[:, 704 * ot + 128 * ih:704 * ot + 128 * (ih + 1)],
                                        identb[:])
                        ins = t.transpose(pYT[0:64, 640:768],
                                          dqg[:, 704 * ot + 640:704 * ot + 704], identb[:])
                        inc(ins, "pe")
                    marks["pYTw"] = now("pe")

                    @sec("scalar")
                    def _(s_, ot=ot):
                        w(s_, "pe", marks["pYTw"])
                        if ot == 0:
                            w(s_, "pld", marks.get(("dn_st", m - 1), 0))
                        dsv = dstage[:].rearrange("p (i ot c) -> p i ot c", i=6, ot=8)
                        inc(s_.copy(dsv[:, 0:6, ot, :],
                                    pYT[:, 0:768].rearrange("p (i c) -> p i c", i=6)), "act")
                    marks["pYTfree"] = ("act", now("act"))
                marks[("dn_tp", m)] = now("pe")
                marks[("dn_tpev", m)] = now("act")
            dq_steps.append(scales_T)

            def store():
                @sec("gpsimd")
                def _(g_):
                    w(g_, "act", marks[("dn_tpev", m)])
                    dsv = dstage[:].rearrange("p (i ot c) -> p i ot c", i=6, ot=8)
                    dst1 = bass.AP(wdT_in, 1024 * m, [[D, 128], [128 * D, 5], [1, 1024]])
                    src1 = dsv[:, 0:5, :, :].rearrange("p i ot c -> p i (ot c)")
                    inc(g_.dma_start(out=dst1, in_=src1), "pld", 16)
                    dst2 = bass.AP(wdT_in, 640 * D + 1024 * m, [[D, 64], [1, 1024]])
                    src2 = dsv[0:64, 5, :, :].rearrange("p ot c -> p (ot c)")
                    inc(g_.dma_start(out=dst2, in_=src2), "pld", 16)
                marks[("dn_st", m)] = now("pld")
            dq_steps.append(store)

        for k in range(4):
            gu_chunk(k)
        for m in range(2):
            dn_chunk(m)

        def dq_collectives():
            @sec("gpsimd")
            def _(g_):
                w(g_, "pld", marks[("gu_st", 3)])
                inc(g_.collective_compute(
                    "AllGather", OP.bypass, replica_groups=[list(range(8))],
                    ins=[wguT_in.ap().opt()], outs=[wguT_all.ap().opt()]), "cgu")
                w(g_, "pld", marks[("dn_st", 1)])
                inc(g_.collective_compute(
                    "AllGather", OP.bypass, replica_groups=[list(range(8))],
                    ins=[wdT_in.ap().opt()], outs=[wdT_all.ap().opt()]), "cwd")
        dq_steps.append(dq_collectives)

        def emit_stats(h):
            @sec("vector")
            def _(v):
                w(v, "act", marks["p1done_act"])
                w(v, "dve", marks["p1done_dve"])
                if debug:
                    w(v, "pld", marks.get("dbg_p1", 0))
                if h > 0:
                    w(v, "pe", marks[("qp_mm", h - 1)])
                bar(v, v.reduce_sum(qsum[:], qfmv[:, h, :], axis=mybir.AxisListType.X), "dve")
                inc(v.tensor_copy(qsumb[:], qsum[:]), "dve")
            marks[("qsum", h)] = now("dve")

            for cc_ in range(8):
                @sec("tensor")
                def _(t, cc_=cc_):
                    if cc_ == 0:
                        w(t, "dve", marks[("qsum", h)])
                        if debug:
                            w(t, "pld", marks.get("dbg_p1", 0))
                        if h > 0:
                            w(t, "dve", marks[("kpn_used", h - 1)])
                    w(t, "dve", marks.get(("kp_ev", h, cc_ - 1), 0))
                    inc(t.matmul(pStat[0:JL, 0:512], projb[:],
                                 kfmv[:, h, 512 * cc_:512 * (cc_ + 1)],
                                 start=True, stop=True), "pe")
                marks[("kp_mm", h, cc_)] = now("pe")

                @sec("vector")
                def _(v, cc_=cc_):
                    w(v, "pe", marks[("kp_mm", h, cc_)])
                    inc(v.tensor_copy(kps[:, 512 * cc_:512 * (cc_ + 1)], pStat[0:JL, 0:512]), "dve")
                marks[("kp_ev", h, cc_)] = now("dve")

            @sec("vector")
            def _(v):
                w(v, "dve", marks[("kp_ev", h, 7)])
                inc(v.tensor_tensor(kpn[:], kps[:], kps[:], OP.mult), "dve")
            marks[("kpsq", h)] = now("dve")

            for cc_ in range(8):
                @sec("tensor")
                def _(t, cc_=cc_):
                    w(t, "dve", marks[("kpsq", h)])
                    w(t, "dve", marks.get(("cs_ev", h, cc_ - 1), 0))
                    inc(t.matmul(pStat[0:1, 0:512], _ones32(),
                                 kpn[:, 512 * cc_:512 * (cc_ + 1)],
                                 start=True, stop=True), "pe")
                marks[("cs_mm", h, cc_)] = now("pe")

                @sec("vector")
                def _(v, cc_=cc_):
                    w(v, "pe", marks[("cs_mm", h, cc_)])
                    inc(v.tensor_copy(rown[0:1, 512 * cc_:512 * (cc_ + 1)], pStat[0:1, 0:512]), "dve")
                marks[("cs_ev", h, cc_)] = now("dve")

            @sec("scalar")
            def _(s_):
                w(s_, "dve", marks[("cs_ev", h, 7)])
                bar(s_, s_.activation(rown[0:1, :], rown[0:1, :], AF.Ln), "act")
                inc(s_.activation(rown[0:1, :], rown[0:1, :], AF.Exp, scale=0.5), "act")
            marks[("nrm_sq", h)] = now("act")

            @sec("vector")
            def _(v):
                w(v, "act", marks[("nrm_sq", h)])
                bar(v, v.tensor_scalar(rown[0:1, :], rown[0:1, :], 1e-12, None, OP.max), "dve")
                inc(v.reciprocal(rown[0:1, :], rown[0:1, :]), "dve")
            marks[("nrm_rr", h)] = now("dve")

            for cc_ in range(8):
                @sec("tensor")
                def _(t, cc_=cc_):
                    w(t, "dve", marks[("nrm_rr", h)])
                    w(t, "dve", marks.get(("kpn_mul", h, cc_ - 1), 0))
                    inc(t.matmul(pStat[0:JL, 0:512], onesr[0:1, 0:JL],
                                 rown[0:1, 512 * cc_:512 * (cc_ + 1)],
                                 start=True, stop=True), "pe")
                marks[("bc_mm", h, cc_)] = now("pe")

                @sec("vector")
                def _(v, cc_=cc_):
                    w(v, "pe", marks[("bc_mm", h, cc_)])
                    inc(v.tensor_tensor(kpn[:, 512 * cc_:512 * (cc_ + 1)],
                                        kps[:, 512 * cc_:512 * (cc_ + 1)],
                                        pStat[0:JL, 0:512], OP.mult), "dve")
                marks[("kpn_mul", h, cc_)] = now("dve")
            marks[("kpn", h)] = now("dve")

            @sec("vector")
            def _(v):
                w(v, "dve", marks[("kpn", h)])
                bar(v, v.reduce_sum(cent[:], kpn[:].rearrange("p (b c) -> p b c", b=NB),
                                    axis=mybir.AxisListType.X), "dve")
                inc(v.tensor_tensor(centn[:], cent[:], cent[:], OP.mult), "dve")
            marks[("cent", h)] = now("dve")

            @sec("tensor")
            def _(t):
                w(t, "dve", marks[("cent", h)])
                inc(t.matmul(pStat[0:1, 0:NB], _ones32(), centn[:], start=True, stop=True), "pe")
            marks[("cn_mm", h)] = now("pe")

            @sec("scalar")
            def _(s_):
                w(s_, "pe", marks[("cn_mm", h)])
                bar(s_, s_.copy(srow[0:1, 192:256], pStat[0:1, 0:NB]), "act")
                bar(s_, s_.activation(srow[0:1, 192:256], srow[0:1, 192:256], AF.Ln), "act")
                inc(s_.activation(srow[0:1, 192:256], srow[0:1, 192:256], AF.Exp, scale=0.5), "act")
            marks[("cn_sq", h)] = now("act")

            @sec("vector")
            def _(v):
                w(v, "act", marks[("cn_sq", h)])
                bar(v, v.tensor_scalar(srow[0:1, 192:256], srow[0:1, 192:256], 1e-12, None, OP.max), "dve")
                inc(v.reciprocal(srow[0:1, 192:256], srow[0:1, 192:256]), "dve")
            marks[("cn_rr", h)] = now("dve")

            @sec("tensor")
            def _(t):
                w(t, "dve", marks[("cn_rr", h)])
                inc(t.matmul(pStat[0:JL, 0:NB], onesr[0:1, 0:JL], srow[0:1, 192:256],
                             start=True, stop=True), "pe")
            marks[("cnbc_mm", h)] = now("pe")

            @sec("vector")
            def _(v):
                w(v, "pe", marks[("cnbc_mm", h)])
                inc(v.tensor_tensor(centn[:], cent[:], pStat[0:JL, 0:NB], OP.mult), "dve")
            marks[("centn", h)] = now("dve")

            @sec("vector")
            def _(v):
                w(v, "dve", marks[("centn", h)])
                ce = centn[:].rearrange("p (b o) -> p b o", o=1).broadcast_to((JL, NB, 64))
                inc(v.tensor_tensor(kps[:].rearrange("p (b c) -> p b c", b=NB),
                                    kpn[:].rearrange("p (b c) -> p b c", b=NB),
                                    ce, OP.mult), "dve")
            marks[("prod", h)] = now("dve")

            for cc_ in range(8):
                @sec("tensor")
                def _(t, cc_=cc_):
                    w(t, "dve", marks[("prod", h)])
                    w(t, "dve", marks.get(("dots_ev", h, cc_ - 1), 0))
                    inc(t.matmul(pStat[0:1, 0:512], _ones32(),
                                 kps[:, 512 * cc_:512 * (cc_ + 1)],
                                 start=True, stop=True), "pe")
                marks[("dots_mm", h, cc_)] = now("pe")

                @sec("vector")
                def _(v, cc_=cc_):
                    w(v, "pe", marks[("dots_mm", h, cc_)])
                    inc(v.tensor_copy(rown[0:1, 512 * cc_:512 * (cc_ + 1)], pStat[0:1, 0:512]), "dve")
                marks[("dots_ev", h, cc_)] = now("dve")

            @sec("vector")
            def _(v):
                w(v, "dve", marks[("dots_ev", h, 7)])
                bar(v, v.tensor_reduce(srow[0:1, 128:192], rown[0:1, :].rearrange("o (b c) -> o b c", b=NB),
                                       axis=mybir.AxisListType.X, op=OP.min), "dve")
                bar(v, v.tensor_scalar(srow[0:1, 64:128], srow[0:1, 128:192], -1.0, 1.0, OP.mult, OP.add), "dve")
                inc(v.tensor_scalar(srow[0:1, 64:128], srow[0:1, 64:128], 0.0, 1.0, OP.max, OP.min), "dve")
            marks[("radii", h)] = now("dve")

            @sec("tensor")
            def _(t):
                w(t, "dve", marks[("qsum", h)])
                w(t, "dve", marks[("dots_ev", h, 7)])
                inc(t.matmul(pStat[0:JL, 64:65], projb[:], qsumb[:], start=True, stop=True), "pe")
            marks[("qp_mm", h)] = now("pe")

            @sec("scalar")
            def _(s_):
                w(s_, "pe", marks[("qp_mm", h)])
                inc(s_.copy(qp_s[:], pStat[0:JL, 64:65]), "act")
            marks[("qp_ev", h)] = now("act")

            @sec("vector")
            def _(v):
                w(v, "act", marks[("qp_ev", h)])
                inc(v.tensor_tensor(kps[0:JL, 0:1], qp_s[:], qp_s[:], OP.mult), "dve")
            marks[("qpsq", h)] = now("dve")

            @sec("tensor")
            def _(t):
                w(t, "dve", marks[("qpsq", h)])
                inc(t.matmul(pStat[0:1, 65:66], _ones32(), kps[0:JL, 0:1], start=True, stop=True), "pe")
            marks[("nq_mm", h)] = now("pe")

            @sec("scalar")
            def _(s_):
                w(s_, "pe", marks[("nq_mm", h)])
                bar(s_, s_.copy(nq_s[0:1, 0:1], pStat[0:1, 65:66]), "act")
                bar(s_, s_.activation(nq_s[0:1, 0:1], nq_s[0:1, 0:1], AF.Ln), "act")
                inc(s_.activation(nq_s[0:1, 0:1], nq_s[0:1, 0:1], AF.Exp, scale=0.5), "act")
            marks[("nq_sq", h)] = now("act")

            @sec("vector")
            def _(v):
                w(v, "act", marks[("nq_sq", h)])
                bar(v, v.tensor_scalar(nq_s[0:1, 0:1], nq_s[0:1, 0:1], 1e-12, None, OP.max), "dve")
                inc(v.reciprocal(nq_s[0:1, 0:1], nq_s[0:1, 0:1]), "dve")
            marks[("nq_rr", h)] = now("dve")

            @sec("tensor")
            def _(t):
                w(t, "dve", marks[("nq_rr", h)])
                inc(t.matmul(pStat[0:1, 0:NB], qp_s[:], centn[:], start=True, stop=True), "pe")
            marks[("sc_mm", h)] = now("pe")

            @sec("vector")
            def _(v):
                w(v, "pe", marks[("sc_mm", h)])
                bar(v, v.tensor_scalar(srow[0:1, 0:64], pStat[0:1, 0:NB], nq_s[0:1, 0:1], None, OP.mult), "dve")
                bar(v, v.tensor_tensor(srow[0:1, 0:64], srow[0:1, 0:64], srow[0:1, 64:128], OP.add), "dve")
                inc(v.memset(srow[0:1, 62:64], 1e9), "dve")
            marks[("scores", h)] = now("dve")

            if debug and h == 0:
                @sec("gpsimd")
                def _(g_):
                    w(g_, "dve", marks[("scores", h)])
                    inc(g_.dma_start(out=dbg_t["d_kp"], in_=kps[:]), "pld", 16)
                    inc(g_.dma_start(out=dbg_t["d_kpn"], in_=kpn[:]), "pld", 16)
                    inc(g_.dma_start(out=dbg_t["d_cent"], in_=centn[:]), "pld", 16)
                    inc(g_.dma_start(out=dbg_t["d_scores"], in_=srow[0:1, 0:64]), "pld", 16)
                marks[("dbg_stats", h)] = now("pld")

            @sec("tensor")
            def _(t):
                w(t, "dve", marks[("scores", h)])
                if debug and h == 0:
                    w(t, "pld", marks[("dbg_stats", h)])
                t.matmul(pStat[0:64, 0:64], onesr[0:1, 0:64], srow[0:1, 0:64], start=True, stop=True)
                inc(t.matmul(pStat[0:64, 64:65], srow[0:1, 0:64], onesr[0:1, 0:1], start=True, stop=True), "pe")
            marks[("mask_mm", h)] = now("pe")

            @sec("vector")
            def _(v):
                w(v, "pe", marks[("mask_mm", h)])
                bar(v, v.tensor_scalar(cmpm[:], pStat[0:64, 0:64], pStat[0:64, 64:65], None, OP.is_gt), "dve")
                bar(v, v.reduce_sum(cntc[0:64, 0:1], cmpm[:], axis=mybir.AxisListType.X), "dve")
                bar(v, v.tensor_scalar(cntc[0:64, 0:1], cntc[0:64, 0:1], 33.5, None, OP.is_lt), "dve")
                inc(v.tensor_scalar(cntc[0:64, 1:2], cntc[0:64, 0:1], -1.0, 30000.0,
                                    OP.add, OP.mult), "dve")
            marks[("pen_col", h)] = now("dve")

            @sec("tensor")
            def _(t):
                w(t, "dve", marks[("pen_col", h)])
                inc(t.transpose(pStat[0:1, 66:130], cntc[0:64, 1:2], identf[0:64, 0:64]), "pe")
            marks[("penr_mm", h)] = now("pe")

            @sec("scalar")
            def _(s_):
                w(s_, "pe", marks[("penr_mm", h)])
                inc(s_.copy(penr[0:1, :], pStat[0:1, 66:130]), "act")
            marks[("penr", h)] = now("act")

            @sec("vector")
            def _(v):
                w(v, "act", marks[("penr", h)])
                inc(v.tensor_scalar(nbrow[0:1, :], onesr[0:1, 0:32], negb[0:1, h:h + 1], None, OP.mult), "dve")
            marks[("nbrow", h)] = now("dve")

            @sec("tensor")
            def _(t):
                w(t, "dve", marks[("nbrow", h)])
                pe_ = penr[0:1, :].rearrange("o (t two) -> o t two", two=2)
                t.matmul(pStat[0:128, 128:160], indu[0:1, :], pe_[:, :, 0:1].squeeze(2), start=True, stop=False)
                t.matmul(pStat[0:128, 128:160], indl[0:1, :], pe_[:, :, 1:2].squeeze(2), start=False, stop=False)
                inc(t.matmul(pStat[0:128, 128:160], onesr[0:1, :], nbrow[0:1, :], start=False, stop=True), "pe")
            marks[("ptab_mm", h)] = now("pe")

            @sec("scalar")
            def _(s_):
                w(s_, "pe", marks[("ptab_mm", h)])
                inc(s_.copy(pen_tab[:, 32 * h:32 * (h + 1)], pStat[0:128, 128:160]), "act")
            marks[("ptab", h)] = now("act")
            marks[("kpn_used", h)] = now("dve")

        def emit_attend(h):
            for ch in range(8):
                A0 = now("act")

                def expmark(hh, cc, tt):
                    if (hh, cc) == (h, ch):
                        return A0 + tt + 1
                    return marks[("exp", hh, cc, tt)]

                @sec("tensor")
                def _(t, h=h, ch=ch, A0=A0):
                    for t_i in range(34):
                        if t_i < 32:
                            if t_i == 0 and ch == 0:
                                w(t, "act", marks[("ptab", h)])
                            if t_i >= 2:
                                w(t, "act", expmark(h, ch, t_i - 2))
                            elif ch > 0:
                                w(t, "act", expmark(h, ch - 1, 30 + t_i))
                            elif h > 0:
                                w(t, "act", expmark(h - 1, 7, 30 + t_i))
                            pS = pS0 if t_i % 2 == 0 else pS1
                            inc(t.matmul(pS[:, 0:512],
                                         kfmv[:, h, 128 * t_i:128 * (t_i + 1)],
                                         qfmv[:, h, 512 * ch:512 * (ch + 1)],
                                         start=True, stop=True), "pe")
                            marks[("qk", h, ch, t_i)] = now("pe")
                        if t_i >= 2:
                            tp = t_i - 2
                            w(t, "act", expmark(h, ch, tp))
                            if tp == 0:
                                pm = marks.get(("yfin", h, ch - 1)) if ch > 0 else marks.get(("yfin", h - 1, 7))
                                if pm:
                                    w(t, "dve", pm)
                            for j in range(4):
                                ins = t.matmul(pv_banks[j][:, 0:129],
                                               pexp[:, 512 * (tp % 2) + 128 * j:512 * (tp % 2) + 128 * (j + 1)],
                                               vview[:, h, tp, 0:129],
                                               start=(tp == 0), stop=(tp == 31))
                            inc(ins, "pe")
                            marks[("pv", h, ch, tp)] = now("pe")

                @sec("scalar")
                def _(s_, h=h, ch=ch):
                    for t_i in range(32):
                        w(s_, "pe", marks[("qk", h, ch, t_i)])
                        prev = None
                        if t_i >= 2:
                            prev = ("pv", h, ch, t_i - 2)
                        elif ch > 0:
                            prev = ("pv", h, ch - 1, 30 + t_i)
                        elif h > 0:
                            prev = ("pv", h - 1, 7, 30 + t_i)
                        if prev is not None:
                            w(s_, "pe", marks[prev])
                        pS = pS0 if t_i % 2 == 0 else pS1
                        ins = s_.activation(pexp[:, 512 * (t_i % 2):512 * (t_i % 2 + 1)],
                                            pS[:, 0:512], AF.Exp,
                                            bias=pen_tab[:, 32 * h + t_i:32 * h + t_i + 1],
                                            scale=SC_QK)
                        inc(ins, "act")
                        marks[("exp", h, ch, t_i)] = now("act")
                assert now("act") == A0 + 32

                @sec("vector")
                def _(v, h=h, ch=ch):
                    w(v, "pe", marks[("pv", h, ch, 31)])
                    for j in range(4):
                        ins = v.reciprocal(rcj[:, j:j + 1], pv_banks[j][:, 128:129])
                    bar(v, ins, "dve")
                    for j in range(4):
                        ins = v.tensor_scalar(ydiv[:, 128 * j:128 * (j + 1)],
                                              pv_banks[j][:, 0:128], rcj[:, j:j + 1], None, OP.mult)
                    inc(ins, "dve")
                marks[("yfin", h, ch)] = now("dve")

                @sec("tensor")
                def _(t, h=h, ch=ch):
                    w(t, "dve", marks[("yfin", h, ch)])
                    fs, fc = marks["pYTfree"]
                    w(t, fs, fc)
                    for j in range(4):
                        ins = t.transpose(pYT[:, 128 * j:128 * (j + 1)],
                                          ydiv[:, 128 * j:128 * (j + 1)], identb[:])
                    inc(ins, "pe")
                marks["pYTw"] = now("pe")

                @sec("vector")
                def _(v, h=h, ch=ch):
                    w(v, "pe", marks["pYTw"])
                    w(v, "pld", marks.get(("yslab_st", h - 1), 0))
                    inc(v.tensor_copy(yslab[:, 512 * ch:512 * (ch + 1)], pYT[:, 0:512]), "dve")
                marks["pYTfree"] = ("dve", now("dve"))
                marks[("ychunk", h, ch)] = now("dve")

                for _pop in range(2):
                    if dq_steps:
                        dq_steps.pop(0)()

            @sec("gpsimd")
            def _(g_, h=h):
                w(g_, "dve", marks[("ychunk", h, 7)])
                dst = y_in.ap().rearrange("(h p) s -> p h s", p=128)[:, h, :]
                inc(g_.dma_start(out=dst, in_=yslab[:]), "pld", 16)
            marks[("yslab_st", h)] = now("pld")

        for h in range(HPC):
            emit_stats(h)
            emit_attend(h)

        while dq_steps:
            dq_steps.pop(0)()

        @sec("gpsimd")
        def _(g_):
            w(g_, "pld", marks[("yslab_st", HPC - 1)])
            inc(g_.collective_compute(
                "AllGather", OP.bypass,
                replica_groups=[[0, 1, 2, 3], [4, 5, 6, 7]],
                ins=[y_in.ap().opt()], outs=[y_all.ap().opt()]), "cy")

        if debug:
            @sec("gpsimd")
            def _(g_):
                w(g_, "cy", 1)
                inc(g_.dma_start(out=dbg_t["d_yin"], in_=y_in.ap()), "pld", 16)
                inc(g_.dma_start(out=dbg_t["d_yall"], in_=y_all.ap()), "pld", 16)
                inc(g_.dma_start(out=dbg_t["d_pen"], in_=pen_tab[:]), "pld", 16)

        marks["attn_done_pe"] = now("pe")
        p23.close()
        pA.close()

        pB = ExitStack()
        x2T = pB.enter_context(nc.sbuf_tensor("x2T", [128, 16 * QTOK], BF16))
        x2v = x2T[:].rearrange("p (dc s) -> p dc s", dc=16)

        p3b = ExitStack()
        sb3 = lambda n, s_, d: p3b.enter_context(nc.sbuf_tensor(n, s_, d))
        ps3 = lambda n, s_, d: p3b.enter_context(nc.psum_tensor(n, s_, d))
        yfmb = sb3("yfmb", [128, 16 * QTOK], BF16)
        woTs = sb3("woTs", [128, 2 * 16 * 512], BF16)
        atst = sb3("atst", [128, 2 * 512], F32)
        h1t = sb3("h1t", [128, 2 * D], F32)
        x2b = sb3("x2b", [128, 2 * D], BF16)
        ln2c = sb3("ln2c", [128, 2 * 8], F32)
        dmp2 = sb3("dmp2", [128, 2 * D], F32)
        pWo = [ps3("pWo0", [128, 512], F32), ps3("pWo1", [128, 512], F32)]
        pX2T = ps3("pX2T", [128, 2048], BF16)

        @sec("gpsimd")
        def _(g_):
            w(g_, "cy", 1)
            with g_.register("rg0") as r0, g_.register("rg1") as r1:
                g_.reg_load(r0, pidc[0:1, 0:1])
                g_.reg_mod(r1, r0, 4)
                g_.reg_mul(r1, r1, QTOK)
                src = bass.AP(y_all, r1, [[S, 128], [128 * S, 16], [1, QTOK]])
                inc(g_.dma_start(out=yfmb[:].rearrange("p (dc s) -> p dc s", dc=16), in_=src), "pld", 16)
        marks["yfm_ld"] = now("pld")

        for oc in range(4):
            @sec("sync")
            def _(sy, oc=oc):
                w(sy, "cwo", 1)
                if oc == 0:
                    w(sy, "pe", marks["attn_done_pe"])
                    w(sy, "act")
                    w(sy, "dve")
                    w(sy, "pld")
                if oc >= 2:
                    w(sy, "pe", marks[("wo_mm", oc - 2, 7)])
                src = woT_all.ap().rearrange("(dc p) c -> p dc c", p=128)[:, :, 512 * oc:512 * (oc + 1)]
                inc(sy.dma_start(out=woTs[:].rearrange("p (r dc c) -> p r dc c", r=2, dc=16)[:, oc % 2, :, :],
                                 in_=src), "sp", 16)
            marks[("woT_ld", oc)] = now("sp")

            for ss in range(8):
                @sec("tensor")
                def _(t, oc=oc, ss=ss):
                    w(t, "sp", marks[("woT_ld", oc)])
                    if oc == 0 and ss == 0:
                        w(t, "pld", marks["yfm_ld"])
                    idx = oc * 8 + ss
                    if idx >= 2:
                        w(t, "dve", marks[("wo_ev", idx - 2)])
                    yv = yfmb[:].rearrange("p (dc s) -> p dc s", dc=16)
                    wv_ = woTs[:].rearrange("p (r dc c) -> p r dc c", r=2, dc=16)
                    for dc in range(16):
                        ins = t.matmul(pWo[idx % 2][:, 0:512],
                                       yv[:, dc, 128 * ss:128 * (ss + 1)],
                                       wv_[:, oc % 2, dc, :],
                                       start=(dc == 0), stop=(dc == 15))
                    inc(ins, "pe")
                marks[("wo_mm", oc, ss)] = now("pe")

                @sec("vector")
                def _(v, oc=oc, ss=ss):
                    idx = oc * 8 + ss
                    w(v, "pe", marks[("wo_mm", oc, ss)])
                    if idx >= 2:
                        w(v, "pld", marks[("at_st", idx - 2)])
                    inc(v.tensor_copy(atst[:, 512 * (idx % 2):512 * (idx % 2 + 1)], pWo[idx % 2][:, 0:512]), "dve")
                marks[("wo_ev", oc * 8 + ss)] = now("dve")

                @sec("gpsimd")
                def _(g_, oc=oc, ss=ss):
                    idx = oc * 8 + ss
                    w(g_, "dve", marks[("wo_ev", idx)])
                    inc(g_.dma_start(out=attn_d.ap()[128 * ss:128 * (ss + 1), 512 * oc:512 * (oc + 1)],
                                     in_=atst[:, 512 * (idx % 2):512 * (idx % 2 + 1)]), "pld", 16)
                marks[("at_st", oc * 8 + ss)] = now("pld")

        if debug:
            @sec("gpsimd")
            def _(g_):
                w(g_, "pld", marks[("at_st", 31)])
                inc(g_.dma_start(out=dbg_t["d_attn"], in_=attn_d.ap()), "pld", 16)
                inc(g_.dma_start(out=dbg_t["d_woT"], in_=woT_all.ap()), "pld", 16)

        for ss in range(8):
            pi = ss % 2

            @sec("sync")
            def _(sy, ss=ss, pi=pi):
                w(sy, "pld", marks[("at_st", 24 + ss)])
                if ss >= 2:
                    w(sy, "act", marks[("ln2acc", ss - 2)])
                    w(sy, "dve", marks[("x2", ss - 2)])
                inc(sy.dma_start(out=h1t[:, D * pi:D * (pi + 1)],
                                 in_=attn_d.ap()[128 * ss:128 * (ss + 1), :]), "sp", 16)
                inc(sy.dma_start(out=dmp2[:, D * pi:D * (pi + 1)],
                                 in_=x_q[128 * ss:128 * (ss + 1), :]), "sp", 16)
            marks[("h1_ld", ss)] = now("sp")

            @sec("vector")
            def _(v, ss=ss, pi=pi):
                w(v, "sp", marks[("h1_ld", ss)])
                inc(v.tensor_tensor(h1t[:, D * pi:D * (pi + 1)],
                                    h1t[:, D * pi:D * (pi + 1)], dmp2[:, D * pi:D * (pi + 1)], OP.add), "dve")
            marks[("h1", ss)] = now("dve")

            @sec("scalar")
            def _(s_, ss=ss, pi=pi):
                w(s_, "dve", marks[("h1", ss)])
                xt = h1t[:, D * pi:D * (pi + 1)]
                s_.activation(dmp2[:, D * pi:D * (pi + 1)], xt, AF.Copy, accum_out=ln2c[:, 8 * pi + 0:8 * pi + 1])
                inc(s_.activation(dmp2[:, D * pi:D * (pi + 1)], xt, AF.Square, accum_out=ln2c[:, 8 * pi + 1:8 * pi + 2]), "act")
            marks[("ln2acc", ss)] = now("act")

            @sec("vector")
            def _(v, ss=ss, pi=pi):
                w(v, "act", marks[("ln2acc", ss)])
                o = 8 * pi
                v.tensor_scalar(ln2c[:, o + 2:o + 3], ln2c[:, o + 0:o + 1], 1.0 / D, None, OP.mult)
                bar(v, v.scalar_tensor_tensor(ln2c[:, o + 5:o + 6], ln2c[:, o + 0:o + 1],
                                              1.0 / (D * D), ln2c[:, o + 0:o + 1],
                                              OP.mult, OP.mult), "dve")
                inc(v.scalar_tensor_tensor(ln2c[:, o + 3:o + 4], ln2c[:, o + 1:o + 2], 1.0 / D,
                                           ln2c[:, o + 5:o + 6], OP.mult, OP.subtract), "dve")
            marks[("ln2cols", ss)] = now("dve")

            @sec("scalar")
            def _(s_, ss=ss, pi=pi):
                w(s_, "dve", marks[("ln2cols", ss)])
                o = 8 * pi
                bar(s_, s_.activation(ln2c[:, o + 4:o + 5], ln2c[:, o + 3:o + 4], AF.Ln, bias=epsln[:, 0:1]), "act")
                inc(s_.activation(ln2c[:, o + 4:o + 5], ln2c[:, o + 4:o + 5], AF.Exp, scale=-0.5), "act")
            marks[("ln2rstd", ss)] = now("act")

            @sec("vector")
            def _(v, ss=ss, pi=pi):
                w(v, "act", marks[("ln2rstd", ss)])
                o = 8 * pi
                if ss >= 2:
                    w(v, "pe", marks[("x2Tp", ss - 2)])
                inc(v.tensor_scalar(x2b[:, D * pi:D * (pi + 1)], h1t[:, D * pi:D * (pi + 1)],
                                    ln2c[:, o + 2:o + 3], ln2c[:, o + 4:o + 5],
                                    OP.subtract, OP.mult), "dve")
            marks[("x2", ss)] = now("dve")

            @sec("tensor")
            def _(t, ss=ss, pi=pi):
                w(t, "dve", marks[("x2", ss)])
                if ss >= 1:
                    w(t, "dve", marks[("x2Tev", ss - 1)])
                for j in range(16):
                    ins = t.transpose(pX2T[:, 128 * j:128 * (j + 1)],
                                      x2b[:, D * pi + 128 * j:D * pi + 128 * (j + 1)], identb[:])
                inc(ins, "pe")
            marks[("x2Tp", ss)] = now("pe")

            @sec("vector")
            def _(v, ss=ss):
                w(v, "pe", marks[("x2Tp", ss)])
                v.tensor_copy(x2v[:, 0:8, 128 * ss:128 * (ss + 1)],
                              pX2T[:, 0:1024].rearrange("p (dc c) -> p dc c", dc=8))
                inc(v.tensor_copy(x2v[:, 8:16, 128 * ss:128 * (ss + 1)],
                                  pX2T[:, 1024:2048].rearrange("p (dc c) -> p dc c", dc=8)), "dve")
            marks[("x2Tev", ss)] = now("dve")
        marks["x2T"] = now("dve")

        if debug:
            @sec("gpsimd")
            def _(g_):
                w(g_, "dve", marks["x2T"])
                inc(g_.dma_start(out=dbg_t["d_x2t"], in_=x2T[:]), "pld", 16)
                w(g_, "cgu", 1)
                inc(g_.dma_start(out=dbg_t["d_wg"], in_=wguT_all.ap()), "pld", 16)
                w(g_, "cwd", 1)
                inc(g_.dma_start(out=dbg_t["d_wd"], in_=wdT_all.ap()), "pld", 16)

        p3b.close()

        p4base = ExitStack()
        hfm = p4base.enter_context(nc.sbuf_tensor("hfm", [128, 44 * QTOK], BF16))

        p4a = ExitStack()
        wgs = p4a.enter_context(nc.sbuf_tensor("wgs", [128, 2 * 2 * 16 * 128], BF16))
        gsil = p4a.enter_context(nc.sbuf_tensor("gsil", [128, 512], BF16))
        pGt = p4a.enter_context(nc.psum_tensor("pGt", [128, 512], F32))
        pUp = p4a.enter_context(nc.psum_tensor("pUp", [128, 512], F32))

        for ip in range(44):
            @sec("sync")
            def _(sy, ip=ip):
                w(sy, "cgu", 1)
                w(sy, "dve", marks["x2T"])
                if ip >= 2:
                    w(sy, "pe", marks[("gu_mm", ip - 2, 1)])
                wgv = wgs[:].rearrange("p (r gu dc c) -> p r gu dc c", r=2, gu=2, dc=16)
                src_g = wguT_all.ap().rearrange("(dc p) c -> p dc c", p=128)[:, :, 128 * ip:128 * (ip + 1)]
                src_u = wguT_all.ap().rearrange("(dc p) c -> p dc c", p=128)[:, :, IFFN + 128 * ip:IFFN + 128 * (ip + 1)]
                inc(sy.dma_start(out=wgv[:, ip % 2, 0, :, :], in_=src_g), "sp", 16)
                inc(sy.dma_start(out=wgv[:, ip % 2, 1, :, :], in_=src_u), "sp", 16)
            marks[("wg_ld", ip)] = now("sp")

            for sc_ in range(2):
                @sec("tensor")
                def _(t, ip=ip, sc_=sc_):
                    w(t, "sp", marks[("wg_ld", ip)])
                    if ip == 0 and sc_ == 0:
                        w(t, "dve", marks["x2T"])
                    if not (ip == 0 and sc_ == 0):
                        pidx = (ip, sc_ - 1) if sc_ > 0 else (ip - 1, 1)
                        w(t, "dve", marks[("h_mul", pidx[0], pidx[1])])
                        w(t, "act", marks[("gsil", pidx[0], pidx[1])])
                    wgv = wgs[:].rearrange("p (r gu dc c) -> p r gu dc c", r=2, gu=2, dc=16)
                    for dc in range(16):
                        t.matmul(pGt[:, 0:512], wgv[:, ip % 2, 0, dc, :],
                                 x2v[:, dc, 512 * sc_:512 * (sc_ + 1)],
                                 start=(dc == 0), stop=(dc == 15))
                    for dc in range(16):
                        ins = t.matmul(pUp[:, 0:512], wgv[:, ip % 2, 1, dc, :],
                                       x2v[:, dc, 512 * sc_:512 * (sc_ + 1)],
                                       start=(dc == 0), stop=(dc == 15))
                    inc(ins, "pe")
                marks[("gu_mm", ip, sc_)] = now("pe")

                @sec("scalar")
                def _(s_, ip=ip, sc_=sc_):
                    w(s_, "pe", marks[("gu_mm", ip, sc_)])
                    inc(s_.activation(gsil[:], pGt[:, 0:512], AF.Silu), "act")
                marks[("gsil", ip, sc_)] = now("act")

                @sec("vector")
                def _(v, ip=ip, sc_=sc_):
                    w(v, "act", marks[("gsil", ip, sc_)])
                    inc(v.tensor_tensor(hfm[:, QTOK * ip + 512 * sc_:QTOK * ip + 512 * (sc_ + 1)],
                                        pUp[:, 0:512], gsil[:], OP.mult), "dve")
                marks[("h_mul", ip, sc_)] = now("dve")
        marks["hfm"] = now("dve")

        if debug:
            @sec("gpsimd")
            def _(g_):
                w(g_, "dve", marks["hfm"])
                inc(g_.dma_start(out=dbg_t["d_hfm"], in_=hfm[:]), "pld", 16)

        p4a.close()

        p4b = ExitStack()
        sb4 = lambda n, s_, d: p4b.enter_context(nc.sbuf_tensor(n, s_, d))
        ps4 = lambda n, s_, d: p4b.enter_context(nc.psum_tensor(n, s_, d))
        wds = sb4("wds", [128, 2 * 44 * 128], BF16)
        ytev = sb4("ytev", [128, 512], F32)
        ysb4 = sb4("ysb4", [128, 4 * D], F32)
        rescol = sb4("rescol", [128, 8], F32)
        h1l = sb4("h1l", [128, D], F32)
        xql = sb4("xql", [128, D], F32)
        ynrm = sb4("ynrm", [128, D], F32)
        pYd = ps4("pYd", [128, 512], F32)
        pYd2 = ps4("pYd2", [128, 512], F32)
        pTr = ps4("pTr", [128, 512], F32)

        for sc_ in range(2):
            for ho in range(16):
                idx = sc_ * 16 + ho

                @sec("sync")
                def _(sy, sc_=sc_, ho=ho, idx=idx):
                    w(sy, "cwd", 1)
                    w(sy, "dve", marks["hfm"])
                    if idx >= 2:
                        w(sy, "pe", marks[("dn_mmB", idx - 2)])
                    wdv = wds[:].rearrange("p (r ic c) -> p r ic c", r=2, ic=44)
                    src = wdT_all.ap().rearrange("(ic p) c -> p ic c", p=128)[:, :, 128 * ho:128 * (ho + 1)]
                    inc(sy.dma_start(out=wdv[:, idx % 2, :, :], in_=src), "sp", 16)
                marks[("wd_ld", idx)] = now("sp")

                @sec("tensor")
                def _(t, sc_=sc_, ho=ho, idx=idx):
                    w(t, "sp", marks[("wd_ld", idx)])
                    if idx >= 2:
                        w(t, "dve", marks[("yt_ev", idx - 2)])
                    wdv = wds[:].rearrange("p (r ic c) -> p r ic c", r=2, ic=44)
                    pp = pYd if idx % 2 == 0 else pYd2
                    for ic in range(44):
                        ins = t.matmul(pp[:, 0:512], wdv[:, idx % 2, ic, :],
                                       hfm[:, QTOK * ic + 512 * sc_:QTOK * ic + 512 * (sc_ + 1)],
                                       start=(ic == 0), stop=(ic == 43))
                    inc(ins, "pe")
                marks[("dn_mmB", idx)] = now("pe")

                @sec("vector")
                def _(v, idx=idx):
                    w(v, "pe", marks[("dn_mmB", idx)])
                    if idx >= 1:
                        w(v, "pe", marks[("yt_tp", idx - 1)])
                    pp = pYd if idx % 2 == 0 else pYd2
                    inc(v.tensor_copy(ytev[:], pp[:, 0:512]), "dve")
                marks[("yt_ev", idx)] = now("dve")

                @sec("tensor")
                def _(t, idx=idx):
                    w(t, "dve", marks[("yt_ev", idx)])
                    if idx >= 1:
                        w(t, "dve", marks[("ytr_ev", idx - 1)])
                    for j in range(4):
                        ins = t.transpose(pTr[:, 128 * j:128 * (j + 1)],
                                          ytev[:, 128 * j:128 * (j + 1)], identf[:])
                    inc(ins, "pe")
                marks[("yt_tp", idx)] = now("pe")

                @sec("vector")
                def _(v, sc_=sc_, ho=ho, idx=idx):
                    w(v, "pe", marks[("yt_tp", idx)])
                    if sc_ > 0 and ho == 0:
                        w(v, "dve", marks[("fin", sc_ - 1, 3)])
                    yv = ysb4[:].rearrange("p (ssl c) -> p ssl c", ssl=4)
                    inc(v.tensor_copy(yv[:, :, 128 * ho:128 * (ho + 1)],
                                      pTr[:, 0:512].rearrange("p (ssl c) -> p ssl c", ssl=4)), "dve")
                marks[("ytr_ev", idx)] = now("dve")

            for ssl in range(4):
                ss = 4 * sc_ + ssl

                @sec("sync")
                def _(sy, ss=ss, sc_=sc_, ssl=ssl):
                    if not (sc_ == 0 and ssl == 0):
                        prevss = ss - 1
                        w(sy, "dve", marks[("fin", prevss // 4, prevss % 4)])
                    inc(sy.dma_start(out=h1l[:], in_=attn_d.ap()[128 * ss:128 * (ss + 1), :]), "sp", 16)
                    inc(sy.dma_start(out=xql[:], in_=x_q[128 * ss:128 * (ss + 1), :]), "sp", 16)
                marks[("fin_ld", sc_, ssl)] = now("sp")

                @sec("scalar")
                def _(s_, sc_=sc_, ssl=ssl):
                    w(s_, "dve", marks[("ytr_ev", sc_ * 16 + 15)])
                    if not (sc_ == 0 and ssl == 0):
                        prevss = 4 * sc_ + ssl - 1
                        w(s_, "dve", marks[("fin", prevss // 4, prevss % 4)])
                        w(s_, "pld", marks[("out_st", prevss)])
                    yv = ysb4[:].rearrange("p (s c) -> p s c", s=4)
                    bar(s_, s_.activation(ynrm[:], yv[:, ssl, :], AF.Square,
                                          accum_out=rescol[:, 2 * ssl:2 * ssl + 1]), "act")
                    bar(s_, s_.activation(rescol[:, 2 * ssl + 1:2 * ssl + 2], rescol[:, 2 * ssl:2 * ssl + 1],
                                          AF.Ln, scale=1.0 / D, bias=epsrms[:, 0:1]), "act")
                    inc(s_.activation(rescol[:, 2 * ssl + 1:2 * ssl + 2],
                                      rescol[:, 2 * ssl + 1:2 * ssl + 2], AF.Exp, scale=-0.5), "act")
                marks[("rms", sc_, ssl)] = now("act")

                @sec("vector")
                def _(v, sc_=sc_, ssl=ssl):
                    w(v, "act", marks[("rms", sc_, ssl)])
                    w(v, "sp", marks[("fin_ld", sc_, ssl)])
                    ss2 = 4 * sc_ + ssl
                    if ss2 >= 1:
                        w(v, "pld", marks[("out_st", ss2 - 1)])
                    yv = ysb4[:].rearrange("p (s c) -> p s c", s=4)
                    v.tensor_scalar(ynrm[:], yv[:, ssl, :], rescol[:, 2 * ssl + 1:2 * ssl + 2], None, OP.mult)
                    v.tensor_tensor(ynrm[:], ynrm[:], h1l[:], OP.add)
                    inc(v.tensor_tensor(ynrm[:], ynrm[:], xql[:], OP.add), "dve")
                marks[("fin", sc_, ssl)] = now("dve")

                @sec("gpsimd")
                def _(g_, ss=ss, sc_=sc_, ssl=ssl):
                    w(g_, "dve", marks[("fin", sc_, ssl)])
                    inc(g_.dma_start(out=out_q[128 * ss:128 * (ss + 1), :], in_=ynrm[:]), "pld", 16)
                marks[("out_st", ss)] = now("pld")

        @sec("sync")
        def _(sy):
            w(sy, "pld", now("pld"))

        p4b.close()
        p4base.close()
        pB.close()

    return nc


_NC_CACHE = {}


def _get_nc(debug=False):
    if debug not in _NC_CACHE:
        _NC_CACHE[debug] = build_nc(debug=debug)
    return _NC_CACHE[debug]


def _host_constants():
    import jax
    import jax.numpy as jnp
    try:
        cpu = jax.local_devices(backend="cpu")[0]
        dev_ctx = jax.default_device(cpu)
    except Exception:
        import contextlib
        dev_ctx = contextlib.nullcontext()
    with dev_ctx:
        key = jax.random.key(42, impl="threefry2x32")
        proj = np.asarray(jax.random.normal(key, (DH, JL), jnp.float32)
                          / np.sqrt(JL))
        inv_freq = 1.0 / (10000.0 ** (jnp.arange(0, DH, 2, dtype=jnp.float32) / DH))
        freqs = jnp.outer(jnp.arange(S, dtype=jnp.float32), inv_freq)
        cos = np.asarray(jnp.cos(freqs), np.float32)
        sin = np.asarray(jnp.sin(freqs), np.float32)
    return proj.astype(BF), cos, sin


def kernel(L0_attn_x, L0_attn_Wq, L0_attn_Wk, L0_attn_Wv, L0_attn_Wo,
           L0_attn_q_gain, L0_ffn_gate_up_packed, L0_ffn_gate_up_scales,
           L0_ffn_down_packed, L0_ffn_down_scales, _debug=False):
    x = np.asarray(L0_attn_x)
    Wq = np.asarray(L0_attn_Wq)
    Wk = np.asarray(L0_attn_Wk)
    Wv = np.asarray(L0_attn_Wv)
    Wo = np.asarray(L0_attn_Wo)
    gain = np.asarray(L0_attn_q_gain)
    gup = np.asarray(L0_ffn_gate_up_packed)
    gus = np.asarray(L0_ffn_gate_up_scales)
    dnp = np.asarray(L0_ffn_down_packed)
    dns = np.asarray(L0_ffn_down_scales)

    projb, cos, sin = _host_constants()
    identb = np.eye(128).astype(BF)
    identf = np.eye(128, dtype=np.float32)

    nc = _get_nc(debug=_debug)
    in_maps = []
    for c in range(8):
        b, g = c // 4, c % 4
        dns_rows = np.ascontiguousarray(
            np.stack([dns[(11 * c + s_) // 2] for s_ in range(11)], axis=0))
        in_maps.append(dict(
            x_b=np.ascontiguousarray(x[b]),
            x_q=np.ascontiguousarray(x[b, QTOK * g:QTOK * (g + 1)]),
            wq_h=np.ascontiguousarray(Wq[512 * g:512 * (g + 1)]),
            wk_h=np.ascontiguousarray(Wk[512 * g:512 * (g + 1)]),
            wv_h=np.ascontiguousarray(Wv[512 * g:512 * (g + 1)]),
            wo_s=np.ascontiguousarray(Wo[:, 256 * c:256 * (c + 1)]),
            gup_s=np.ascontiguousarray(gup[:, 32 * c:32 * (c + 1)]),
            gus_s=np.ascontiguousarray(gus[2 * c:2 * (c + 1)]),
            dnp_s=np.ascontiguousarray(dnp[:, 88 * c:88 * (c + 1)]),
            dns_s=dns_rows,
            gain_s=np.ascontiguousarray(gain[4 * g:4 * (g + 1)]).reshape(1, 4).astype(np.float32),
            cos_t=cos, sin_t=sin, proj_c=projb,
            id_b=identb, id_f=identf,
        ))

    res = run_bass_kernel_spmd(nc, in_maps, core_ids=list(range(8)))
    out = np.empty((B, S, D), np.float32)
    for c in range(8):
        b, g = c // 4, c % 4
        out[b, QTOK * g:QTOK * (g + 1)] = np.asarray(res.results[c]["out_q"])
    if _debug:
        kernel._dbg = res.results
    return out


# version 07 (final)
# speedup vs baseline: 1.1624x; 1.0010x over previous
import numpy as np
import ml_dtypes
import concourse.bass as bass
import concourse.mybir as mybir
from concourse.alu_op_type import AluOpType
from concourse.bass_utils import run_bass_kernel_spmd
from contextlib import ExitStack

F32 = mybir.dt.float32
BF16 = mybir.dt.bfloat16
F16 = mybir.dt.float16
I32 = mybir.dt.int32
U16 = mybir.dt.uint16
U32 = mybir.dt.uint32
AF = mybir.ActivationFunctionType
OP = AluOpType
BF = ml_dtypes.bfloat16

B, S, D, H, DH = 2, 4096, 2048, 16, 128
NT = S // 128
HPC = 4
QTOK = 1024
NB = 64
JL = 32
IFFN = 5632
SC_QK = float(1.0 / np.sqrt(128.0))
BCOEF = float(np.sqrt(128.0) * 1.0001)


def build_nc(debug=False):
    nc = bass.Bass("TRN2", target_bir_lowering=False, enable_partition_id=True)

    x_b = nc.dram_tensor("x_b", [S, D], F32, kind="ExternalInput").ap()
    x_q = nc.dram_tensor("x_q", [QTOK, D], F32, kind="ExternalInput").ap()
    wq_h = nc.dram_tensor("wq_h", [512, D], F32, kind="ExternalInput").ap()
    wk_h = nc.dram_tensor("wk_h", [512, D], F32, kind="ExternalInput").ap()
    wv_h = nc.dram_tensor("wv_h", [512, D], F32, kind="ExternalInput").ap()
    wo_s = nc.dram_tensor("wo_s", [D, 256], F32, kind="ExternalInput").ap()
    gup_s = nc.dram_tensor("gup_s", [2 * IFFN, 32], I32, kind="ExternalInput").ap()
    gus_s = nc.dram_tensor("gus_s", [2, 2 * IFFN], F16, kind="ExternalInput").ap()
    dnp_s = nc.dram_tensor("dnp_s", [D, 88], I32, kind="ExternalInput").ap()
    dns_s = nc.dram_tensor("dns_s", [11, D], F16, kind="ExternalInput").ap()
    gain_s = nc.dram_tensor("gain_s", [1, 4], F32, kind="ExternalInput").ap()
    cos_t = nc.dram_tensor("cos_t", [S, 64], F32, kind="ExternalInput").ap()
    sin_t = nc.dram_tensor("sin_t", [S, 64], F32, kind="ExternalInput").ap()
    proj_c = nc.dram_tensor("proj_c", [128, JL], BF16, kind="ExternalInput").ap()
    id_b = nc.dram_tensor("id_b", [128, 128], BF16, kind="ExternalInput").ap()
    id_f = nc.dram_tensor("id_f", [128, 128], F32, kind="ExternalInput").ap()

    woT_in = nc.dram_tensor("woT_in", [256, D], BF16)
    woT_all = nc.dram_tensor("woT_all", [D, D], BF16)
    wguT_in = nc.dram_tensor("wguT_in", [256, 2 * IFFN], BF16)
    wguT_all = nc.dram_tensor("wguT_all", [D, 2 * IFFN], BF16)
    wdT_in = nc.dram_tensor("wdT_in", [704, D], BF16)
    wdT_all = nc.dram_tensor("wdT_all", [IFFN, D], BF16)
    y_in = nc.dram_tensor("y_in", [512, S], BF16)
    y_all = nc.dram_tensor("y_all", [2048, S], BF16)
    attn_d = nc.dram_tensor("attn_d", [QTOK, D], F32)

    out_q = nc.dram_tensor("out_q", [QTOK, D], F32, kind="ExternalOutput").ap()
    dbg_t = {}
    if debug:
        for nm, shp, dt in [
            ("d_qfm", [128, HPC * S], BF16), ("d_kfm", [128, HPC * S], BF16),
            ("d_vres", [128, HPC * 136 * NT], BF16),
            ("d_kp", [JL, S], F32), ("d_kpn", [JL, S], F32),
            ("d_cent", [JL, NB], F32), ("d_scores", [1, NB], F32),
            ("d_pen", [128, HPC * 32], F32), ("d_yin", [512, S], BF16),
            ("d_yall", [2048, S], BF16), ("d_attn", [QTOK, D], F32),
            ("d_x2t", [128, 16 * QTOK], BF16), ("d_wg", [D, 2 * IFFN], BF16),
            ("d_wd", [IFFN, D], BF16), ("d_hfm", [128, 44 * QTOK], BF16),
            ("d_woT", [D, D], BF16),
            ("d_x1t0", [128, D], BF16), ("d_lnc", [128, 16], F32),
            ("d_knlast", [128, 1024], BF16), ("d_kclast", [128, 16], F32),
            ("d_ksblast", [128, 1024], F32),
            ("d_qn0", [128, 512], BF16), ("d_kn0", [128, 512], BF16),
            ("d_ksb0", [128, 512], F32), ("d_qcol", [128, 16], F32),
        ]:
            dbg_t[nm] = nc.dram_tensor(nm, shp, dt, kind="ExternalOutput").ap()

    ctx = ExitStack()
    with ctx:
        SEM, CNT = {}, {}
        for s_ in ["sp", "pld", "pe", "act", "dve", "pl",
                   "cwo", "cgu", "cwd", "cy",
                   "sx0", "sx1", "so0", "so1", "sg0", "sg1",
                   "sd0", "sd1", "sh0", "sh1", "sgu"]:
            SEM[s_] = ctx.enter_context(nc.semaphore(s_))
            CNT[s_] = 0

        def inc(instr, s_, n=1):
            instr.then_inc(SEM[s_], n)
            CNT[s_] += n
            return CNT[s_]

        def now(s_):
            return CNT[s_]

        def w(eng, s_, c=None):
            c = CNT[s_] if c is None else c
            if c > 0:
                eng.wait_ge(SEM[s_], c)

        def bar(eng, instr, s_):
            inc(instr, s_)
            eng.wait_ge(SEM[s_], CNT[s_])

        def sbp(name, shape, dt):
            return ctx.enter_context(nc.sbuf_tensor(name, shape, dt))

        identb = sbp("identb", [128, 128], BF16)
        identf = sbp("identf", [128, 128], F32)
        projb = sbp("projb", [128, JL], BF16)
        gains = sbp("gains", [1, 4], F32)
        negb = sbp("negb", [1, 4], F32)
        gbc4 = sbp("gbc4", [128, 4], F32)
        onesr = sbp("onesr", [1, 128], F32)
        indu = sbp("indu", [1, 128], F32)
        indl = sbp("indl", [1, 128], F32)
        epsln = sbp("epsln", [128, 1], F32)
        epsrms = sbp("epsrms", [128, 1], F32)
        pen_tab = sbp("pen_tab", [128, HPC * 32], F32)
        pidc = sbp("pidc", [1, 1], U32)
        scg = sbp("scg", [128, 2 * 88], F32)
        scd = sbp("scd", [128, 11 * 16], F32)
        ones32_col = sbp("ones32_col", [JL, 1], F32)

        block = ctx.enter_context(nc.Block())

        def sec(name):
            def deco(fn):
                getattr(block, name)(fn)
            return deco

        marks = {}

        def _ones32():
            return ones32_col[0:JL, 0:1]

        pA = ExitStack()
        qfm = pA.enter_context(nc.sbuf_tensor("qfm", [128, HPC * S], BF16))
        kfm = pA.enter_context(nc.sbuf_tensor("kfm", [128, HPC * S], BF16))
        vres = pA.enter_context(nc.sbuf_tensor("vres", [128, HPC * 136 * NT], BF16))
        qfmv = qfm[:].rearrange("p (h s) -> p h s", h=HPC)
        kfmv = kfm[:].rearrange("p (h s) -> p h s", h=HPC)
        vview = vres[:].rearrange("p (h t c) -> p h t c", h=HPC, t=NT)

        p01 = ExitStack()
        sb1 = lambda n, s_, d: p01.enter_context(nc.sbuf_tensor(n, s_, d))
        ps1 = lambda n, s_, d: p01.enter_context(nc.psum_tensor(n, s_, d))
        xld = sb1("xld", [128, 2 * D], F32)
        x1r = sb1("x1r", [128, 2 * D], BF16)
        x1T = sb1("x1T", [128, 2 * D], BF16)
        dump = sb1("dump", [128, D], BF16)
        lncol = sb1("lncol", [128, 2 * 8], F32)
        cosld = sb1("cosld", [128, 2 * 64], F32)
        sinld = sb1("sinld", [128, 2 * 64], F32)
        wqkvT = sb1("wqkvT", [128, 3 * 16 * 512], BF16)
        pW = ps1("pW", [128, 512], F32)
        pQ = ps1("pQ", [128, 1024], F32)
        pK = ps1("pK", [128, 512], F32)
        pV = ps1("pV", [128, 512], F32)
        pX1T = ps1("pX1T", [128, 2048], BF16)
        pQKT = ps1("pQKT", [128, 1024], BF16)

        p0t = ExitStack()
        sb0 = lambda n, s_, d: p0t.enter_context(nc.sbuf_tensor(n, s_, d))
        wsl = sb0("wsl", [128, D], F32)
        wstage = sb0("wstage", [128, 2 * 512], BF16)
        g88 = sb0("g88", [88, 256], F16)
        d16 = sb0("d16", [16, 11 * 128], F16)
        cscg = sb0("cscg", [88, 128], F32)
        cscd = sb0("cscd", [16, 128], F32)

        @sec("sync")
        def _(sy):
            for dst, src in [(identb[:], id_b), (identf[:], id_f),
                             (projb[:], proj_c), (gains[:], gain_s),
                             (pidc[:], nc.partition_id_tensor.ap())]:
                inc(sy.dma_start(out=dst, in_=src), "sp", 16)
            inc(sy.dma_start(out=g88[:].rearrange("t (g q) -> t g q", g=2),
                             in_=gus_s.rearrange("g (t q) -> t g q", q=128)), "sp", 16)
            inc(sy.dma_start(out=d16[:].rearrange("t (s q) -> t s q", s=11),
                             in_=dns_s.rearrange("s (t q) -> t s q", q=128)), "sp", 16)
        marks["consts"] = now("sp")

        @sec("vector")
        def _(v):
            w(v, "sp", marks["consts"])
            v.memset(onesr[:], 1.0)
            v.memset(ones32_col[:], 1.0)
            v.memset(indu[0:1, 0:64], 1.0)
            v.memset(indu[0:1, 64:128], 0.0)
            v.memset(indl[0:1, 0:64], 0.0)
            v.memset(indl[0:1, 64:128], 1.0)
            v.memset(epsln[:], 1e-5)
            v.memset(epsrms[:], 1e-6)
            v.memset(vres[:].rearrange("p (h t c) -> p h t c", h=HPC, t=NT)[:, :, :, 128:129], 1.0)
            bar(v, v.tensor_scalar(negb[:], gains[:], -1.0, None, OP.mult), "dve")
            bar(v, v.tensor_tensor(negb[:], negb[:], gains[:], OP.max), "dve")
            inc(v.tensor_scalar(negb[:], negb[:], -BCOEF, None, OP.mult), "dve")
        marks["p0v"] = now("dve")

        @sec("tensor")
        def _(t):
            w(t, "dve", marks["p0v"])
            inc(t.matmul(pW[:, 0:4], onesr[:], gains[:], start=True, stop=True), "pe")
        marks["pG4"] = now("pe")

        @sec("scalar")
        def _(s_):
            w(s_, "pe", marks["pG4"])
            inc(s_.copy(gbc4[:], pW[:, 0:4]), "act")
        marks["gbc4"] = now("act")

        marks["wTevlast"] = 0
        marks["wTplast"] = 0
        for wi, wsrc in enumerate([wq_h, wk_h, wv_h]):
            for rc in range(4):
                @sec("sync")
                def _(sy, wi=wi, rc=rc, wsrc=wsrc):
                    if wi == 0 and rc == 0:
                        w(sy, "sp", marks["consts"])
                    else:
                        w(sy, "pe", marks["wTplast"])
                    inc(sy.dma_start(out=wsl[:], in_=wsrc[128 * rc:128 * (rc + 1), :]), "sp", 16)
                marks["wld"] = now("sp")

                for jb in range(4):
                    @sec("tensor")
                    def _(t, wi=wi, rc=rc, jb=jb):
                        w(t, "sp", marks["wld"])
                        w(t, "act", marks["wTevlast"] or marks["gbc4"])
                        for u in range(4):
                            j = 4 * jb + u
                            ins = t.transpose(pW[:, 128 * u:128 * (u + 1)],
                                              wsl[:, 128 * j:128 * (j + 1)], identf[:])
                        inc(ins, "pe")
                    marks["wTplast"] = now("pe")

                    @sec("scalar")
                    def _(s_, wi=wi, rc=rc, jb=jb):
                        w(s_, "pe", marks["wTplast"])
                        dst = wqkvT[:].rearrange("p (w ic c) -> p w ic c", w=3, ic=16)
                        inc(s_.copy(dst[:, wi, 4 * jb:4 * (jb + 1), 128 * rc:128 * (rc + 1)],
                                    pW[:, 0:512].rearrange("p (j c) -> p j c", j=4)), "act")
                    marks["wTevlast"] = now("act")
        marks["wqkvT"] = now("act")

        marks["wost"] = 0
        for L in range(4):
            @sec("sync")
            def _(sy, L=L):
                w(sy, "pe", marks["wTplast"])
                inc(sy.dma_start(out=wsl[:, 0:1024].rearrange("p (r c) -> p r c", r=4),
                                 in_=wo_s.rearrange("(l r p) c -> p l r c", p=128, r=4)[:, L, :, :]), "sp", 16)
            marks["wold"] = now("sp")
            for ih in range(2):
                @sec("tensor")
                def _(t, L=L, ih=ih):
                    w(t, "sp", marks["wold"])
                    w(t, "act", marks["wTevlast"])
                    for rc in range(4):
                        ins = t.transpose(pW[:, 128 * rc:128 * (rc + 1)],
                                          wsl[:, 0:1024].rearrange("p (r c) -> p r c", r=4)[:, rc, 128 * ih:128 * (ih + 1)],
                                          identf[:])
                    inc(ins, "pe")
                marks["wTplast"] = now("pe")

                @sec("scalar")
                def _(s_, L=L, ih=ih):
                    w(s_, "pe", marks["wTplast"])
                    par = (2 * L + ih) % 2
                    w(s_, "pld", marks["wost"])
                    inc(s_.copy(wstage[:, 512 * par:512 * (par + 1)], pW[:, 0:512]), "act")
                marks["wTevlast"] = now("act")

                @sec("gpsimd")
                def _(g_, L=L, ih=ih):
                    w(g_, "act", marks["wTevlast"])
                    par = (2 * L + ih) % 2
                    dst = bass.AP(woT_in, ih * 128 * D + 512 * L, [[D, 128], [1, 512]])
                    inc(g_.dma_start(out=dst, in_=wstage[:, 512 * par:512 * (par + 1)]), "pld", 16)
                marks["wost"] = now("pld")
        marks["wo_all_st"] = now("pld")

        for g in range(2):
            @sec("vector")
            def _(v, g=g):
                if g > 0:
                    w(v, "pe", marks["wTplast"])
                gv = g88[:].rearrange("t (g q) -> t g q", g=2)
                inc(v.tensor_copy(cscg[:], gv[:, g, :]), "dve")
            marks["scc"] = now("dve")

            @sec("tensor")
            def _(t, g=g):
                w(t, "act", marks["wTevlast"])
                w(t, "dve", marks["scc"])
                inc(t.transpose(pW[:, 0:88], cscg[:], identf[0:88, 0:88]), "pe")
            marks["wTplast"] = now("pe")

            @sec("scalar")
            def _(s_, g=g):
                w(s_, "pe", marks["wTplast"])
                dst = scg[:].rearrange("p (g t) -> p g t", g=2)
                inc(s_.copy(dst[:, g, :], pW[:, 0:88]), "act")
            marks["wTevlast"] = now("act")
        for sg in range(11):
            @sec("vector")
            def _(v, sg=sg):
                w(v, "pe", marks["wTplast"])
                dv_ = d16[:].rearrange("t (s q) -> t s q", s=11)
                inc(v.tensor_copy(cscd[:], dv_[:, sg, :]), "dve")
            marks["scc"] = now("dve")

            @sec("tensor")
            def _(t, sg=sg):
                w(t, "act", marks["wTevlast"])
                w(t, "dve", marks["scc"])
                inc(t.transpose(pW[:, 0:16], cscd[:], identf[0:16, 0:16]), "pe")
            marks["wTplast"] = now("pe")

            @sec("scalar")
            def _(s_, sg=sg):
                w(s_, "pe", marks["wTplast"])
                dst = scd[:].rearrange("p (s t) -> p s t", s=11)
                inc(s_.copy(dst[:, sg, :], pW[:, 0:16]), "act")
            marks["wTevlast"] = now("act")
        marks["scales"] = now("act")

        @sec("gpsimd")
        def _(g_):
            w(g_, "pld", marks["wo_all_st"])
            inc(g_.collective_compute(
                "AllGather", OP.bypass, replica_groups=[list(range(8))],
                ins=[woT_in.ap().opt()], outs=[woT_all.ap().opt()]), "cwo")

        p0t.close()
        ksb = sb1("ksb", [128, 2 * 512], F32)
        qtmp = sb1("qtmp", [128, 512], F32)
        ktmp = sb1("ktmp", [128, 512], F32)
        qr_s = sb1("qr_s", [128, 512], F32)
        kr_s = sb1("kr_s", [128, 512], F32)
        qn_r = sb1("qn_r", [128, 2 * 512], BF16)
        kn_r = sb1("kn_r", [128, 2 * 512], BF16)
        qcol = sb1("qcol", [128, 2 * 8], F32)
        kcol = sb1("kcol", [128, 2 * 8], F32)

        lnc = lncol[:].rearrange("p (r c) -> p r c", r=2)
        qc = qcol[:].rearrange("p (r c) -> p r c", r=2)
        kc = kcol[:].rearrange("p (r c) -> p r c", r=2)

        for i in range(NT):
            pi = i % 2

            if debug and i == 2:
                @sec("vector")
                def _(v):
                    w(v, "pld", marks["dbg_t0"])
                    v.engine_nop() if False else None
                    inc(v.memset(qtmp[0:1, 0:1], 0.0), "dve")
                @sec("scalar")
                def _(s_):
                    w(s_, "pld", marks["dbg_t0"])
                    inc(s_.copy(epsln[0:1, 0:1], epsln[0:1, 0:1]), "act")
                @sec("gpsimd")
                def _(g_):
                    w(g_, "pld", marks["dbg_t0"])
                    inc(g_.memset(ktmp[0:1, 0:1], 0.0), "pl")

            @sec("sync")
            def _(sy, i=i, pi=pi):
                if i == 0:
                    w(sy, "pld", marks["wo_all_st"])
                if i >= 2:
                    w(sy, "act", marks[("lnacc", i - 2)])
                    w(sy, "dve", marks[("qn", i - 2)])
                    w(sy, "pl", marks[("kn", i - 2)])
                inc(sy.dma_start(out=xld[:, D * pi:D * (pi + 1)],
                                 in_=x_b[128 * i:128 * (i + 1), :]), "sx%d" % (i % 2), 16)
                inc(sy.dma_start(out=cosld[:, 64 * pi:64 * (pi + 1)],
                                 in_=cos_t[128 * i:128 * (i + 1), :]), "sx%d" % (i % 2), 16)
                inc(sy.dma_start(out=sinld[:, 64 * pi:64 * (pi + 1)],
                                 in_=sin_t[128 * i:128 * (i + 1), :]), "sx%d" % (i % 2), 16)
            marks[("xld", i)] = now("sx%d" % (i % 2))

            @sec("scalar")
            def _(s_, i=i, pi=pi):
                w(s_, "sx%d" % (i % 2), marks[("xld", i)])
                if i >= 1:
                    w(s_, "act", marks[("lnacc", i - 1)])
                if i >= 2:
                    w(s_, "dve", marks[("lncols", i - 2)])
                    w(s_, "pe", marks[("x1Tp", i - 2)])
                xt = xld[:, D * pi:D * (pi + 1)]
                s_.activation(dump[:], xt, AF.Copy, accum_out=lnc[:, pi, 0:1])
                inc(s_.activation(x1r[:, D * pi:D * (pi + 1)], xt, AF.Square,
                                  accum_out=lnc[:, pi, 1:2]), "act")
            marks[("lnacc", i)] = now("act")

            @sec("vector")
            def _(v, i=i, pi=pi):
                w(v, "act", marks[("lnacc", i)])
                v.tensor_scalar(lnc[:, pi, 2:3], lnc[:, pi, 0:1], 1.0 / D, None, OP.mult)
                bar(v, v.scalar_tensor_tensor(lnc[:, pi, 5:6], lnc[:, pi, 0:1],
                                              1.0 / (D * D), lnc[:, pi, 0:1],
                                              OP.mult, OP.mult), "dve")
                inc(v.scalar_tensor_tensor(lnc[:, pi, 3:4], lnc[:, pi, 1:2], 1.0 / D,
                                           lnc[:, pi, 5:6], OP.mult, OP.subtract), "dve")
            marks[("lncols", i)] = now("dve")

            @sec("scalar")
            def _(s_, i=i, pi=pi):
                w(s_, "dve", marks[("lncols", i)])
                bar(s_, s_.activation(lnc[:, pi, 4:5], lnc[:, pi, 3:4], AF.Ln, bias=epsln[:, 0:1]), "act")
                inc(s_.activation(lnc[:, pi, 4:5], lnc[:, pi, 4:5], AF.Exp, scale=-0.5), "act")
            marks[("lnrstd", i)] = now("act")

            @sec("vector")
            def _(v, i=i, pi=pi):
                w(v, "act", marks[("lnrstd", i)])
                if i >= 2:
                    w(v, "pe", marks[("x1Tp", i - 2)])
                xt = xld[:, D * pi:D * (pi + 1)]
                inc(v.tensor_scalar(x1r[:, D * pi:D * (pi + 1)], xt,
                                    lnc[:, pi, 2:3], lnc[:, pi, 4:5],
                                    OP.subtract, OP.mult), "dve")
            marks[("x1", i)] = now("dve")

            @sec("tensor")
            def _(t, i=i, pi=pi):
                w(t, "dve", marks[("x1", i)])
                if i >= 1:
                    w(t, "dve", marks[("x1Tev", i - 1)])
                for j in range(16):
                    ins = t.transpose(pX1T[:, 128 * j:128 * (j + 1)],
                                      x1r[:, D * pi + 128 * j: D * pi + 128 * (j + 1)],
                                      identb[:])
                inc(ins, "pe")
            marks[("x1Tp", i)] = now("pe")

            @sec("vector")
            def _(v, i=i):
                w(v, "pe", marks[("x1Tp", i)])
                if i >= 2:
                    w(v, "pe", marks[("qkv", i - 2)])
                ri = i % 2
                v.tensor_copy(x1T[:, 2048 * ri:2048 * ri + 1024], pX1T[:, 0:1024])
                inc(v.tensor_copy(x1T[:, 2048 * ri + 1024:2048 * ri + 2048], pX1T[:, 1024:2048]), "dve")
            marks[("x1Tev", i)] = now("dve")

            @sec("tensor")
            def _(t, i=i):
                w(t, "dve", marks[("x1Tev", i)])
                w(t, "act", marks["wqkvT"])
                ri = i % 2
                wv_ = wqkvT[:].rearrange("p (w ic c) -> p w ic c", w=3, ic=16)
                qh = 512 * (i % 2)
                views = [pQ[:, qh:qh + 512], pK[:, 0:512], pV[:, 0:512]]
                for wi, pp in enumerate(views):
                    if wi == 0 and i >= 2:
                        w(t, "dve", marks[("qn", i - 2)])
                    elif wi == 1 and i >= 1:
                        w(t, "act", marks[("kev", i - 1)])
                    elif wi == 2 and i >= 1:
                        w(t, "act", marks[("vev", i - 1)])
                    for j in range(16):
                        ins = t.matmul(pp,
                                       x1T[:, 2048 * ri + 128 * j: 2048 * ri + 128 * (j + 1)],
                                       wv_[:, wi, j, :],
                                       start=(j == 0), stop=(j == 15))
                    inc(ins, "pe")
                    marks[("qkvg%d" % wi, i)] = now("pe")
            marks[("qkv", i)] = now("pe")

            @sec("scalar")
            def _(s_, i=i, pi=pi):
                w(s_, "pe", marks[("qkvg0", i)])
                if i >= 1:
                    w(s_, "dve", marks[("qn", i - 1)])
                inc(s_.activation(qr_s[:], pQ[:, 512 * (i % 2):512 * (i % 2) + 512], AF.Square), "act")
                marks[("qev", i)] = now("act")
                w(s_, "pe", marks[("qkvg1", i)])
                if i >= 2:
                    w(s_, "pl", marks[("kn", i - 2)])
                if i >= 1:
                    w(s_, "pl", marks[("knrope", i - 1)])
                s_.copy(ksb[:, 512 * pi:512 * (pi + 1)], pK[:, 0:512])
                inc(s_.activation(kr_s[:], pK[:, 0:512], AF.Square), "act")
                marks[("kev", i)] = now("act")
                w(s_, "pe", marks[("qkvg2", i)])
                inc(s_.copy(vview[:, :, i, 0:128], pV[:].rearrange("p (h c) -> p h c", h=4)), "act")
                marks[("vev", i)] = now("act")

            @sec("vector")
            def _(v, i=i, pi=pi):
                w(v, "act", marks[("qev", i)])
                bar(v, v.reduce_sum(qc[:, pi, 0:4],
                                    qr_s[:].rearrange("p (h c) -> p h c", h=4),
                                    axis=mybir.AxisListType.X), "dve")
                inc(v.tensor_scalar(qc[:, pi, 0:4], qc[:, pi, 0:4], 1.0 / DH, 1e-6,
                                    OP.mult, OP.add), "dve")
            marks[("qA", i)] = now("dve")

            @sec("vector")
            def _(v, i=i, pi=pi):
                w(v, "act", marks[("kev", i)])
                bar(v, v.reduce_sum(kc[:, pi, 0:4],
                                    kr_s[:].rearrange("p (h c) -> p h c", h=4),
                                    axis=mybir.AxisListType.X), "dve")
                inc(v.tensor_scalar(kc[:, pi, 0:4], kc[:, pi, 0:4], 1.0 / DH, 1e-6,
                                    OP.mult, OP.add), "dve")
            marks[("kA", i)] = now("dve")

            @sec("scalar")
            def _(s_, i=i, pi=pi):
                w(s_, "dve", marks[("qA", i)])
                w(s_, "act", marks["gbc4"])
                bar(s_, s_.activation(qc[:, pi, 0:4], qc[:, pi, 0:4], AF.Ln), "act")
                inc(s_.activation(qc[:, pi, 0:4], qc[:, pi, 0:4], AF.Exp, scale=-0.5), "act")
                marks[("qB", i)] = now("act")
                w(s_, "dve", marks[("kA", i)])
                bar(s_, s_.activation(kc[:, pi, 0:4], kc[:, pi, 0:4], AF.Ln), "act")
                inc(s_.activation(kc[:, pi, 0:4], kc[:, pi, 0:4], AF.Exp, scale=-0.5), "act")
                marks[("kB", i)] = now("act")

            @sec("vector")
            def _(v, i=i, pi=pi):
                w(v, "act", marks[("qB", i)])
                bar(v, v.tensor_tensor(qc[:, pi, 4:8], qc[:, pi, 0:4], gbc4[:, 0:4], OP.mult), "dve")
                qv = pQ[:, 512 * pi:512 * (pi + 1)].rearrange("p (h two d) -> p h two d", h=4, two=2)
                rv = qr_s[:].rearrange("p (h two d) -> p h two d", h=4, two=2)
                tmp = qtmp[:].rearrange("p (h two d) -> p h two d", h=4, two=2)
                cb = cosld[:, 64 * pi:64 * (pi + 1)].rearrange("p (o d) -> p o d", o=1).broadcast_to((128, 4, 64))
                sbr = sinld[:, 64 * pi:64 * (pi + 1)].rearrange("p (o d) -> p o d", o=1).broadcast_to((128, 4, 64))
                if i >= 2:
                    w(v, "pe", marks[("qkTp", i - 2)])
                v.tensor_tensor(rv[:, :, 0, :], qv[:, :, 0, :], cb, OP.mult)
                v.tensor_tensor(rv[:, :, 1, :], qv[:, :, 1, :], cb, OP.mult)
                v.tensor_tensor(tmp[:, :, 0, :], qv[:, :, 1, :], sbr, OP.mult)
                bar(v, v.tensor_tensor(tmp[:, :, 1, :], qv[:, :, 0, :], sbr, OP.mult), "dve")
                v.tensor_tensor(rv[:, :, 0, :], rv[:, :, 0, :], tmp[:, :, 0, :], OP.add)
                bar(v, v.tensor_tensor(rv[:, :, 1, :], rv[:, :, 1, :], tmp[:, :, 1, :], OP.subtract), "dve")
                qnv = qn_r[:, 512 * pi:512 * (pi + 1)]
                for hh in range(4):
                    ins = v.tensor_scalar(qnv[:, 128 * hh:128 * (hh + 1)],
                                          qr_s[:, 128 * hh:128 * (hh + 1)],
                                          qc[:, pi, 4 + hh:5 + hh], None, OP.mult)
                inc(ins, "dve")
            marks[("qn", i)] = now("dve")

            @sec("gpsimd")
            def _(g_, i=i, pi=pi):
                w(g_, "act", marks[("kB", i)])
                kv = ksb[:, 512 * pi:512 * (pi + 1)].rearrange("p (h two d) -> p h two d", h=4, two=2)
                rv = kr_s[:].rearrange("p (h two d) -> p h two d", h=4, two=2)
                tmp = ktmp[:].rearrange("p (h two d) -> p h two d", h=4, two=2)
                cb = cosld[:, 64 * pi:64 * (pi + 1)].rearrange("p (o d) -> p o d", o=1).broadcast_to((128, 4, 64))
                sbr = sinld[:, 64 * pi:64 * (pi + 1)].rearrange("p (o d) -> p o d", o=1).broadcast_to((128, 4, 64))
                if i >= 2:
                    w(g_, "pe", marks[("qkTp", i - 2)])
                g_.tensor_tensor(rv[:, :, 0, :], kv[:, :, 0, :], cb, OP.mult)
                g_.tensor_tensor(rv[:, :, 1, :], kv[:, :, 1, :], cb, OP.mult)
                g_.tensor_tensor(tmp[:, :, 0, :], kv[:, :, 1, :], sbr, OP.mult)
                bar(g_, g_.tensor_tensor(tmp[:, :, 1, :], kv[:, :, 0, :], sbr, OP.mult), "pl")
                g_.tensor_tensor(rv[:, :, 0, :], rv[:, :, 0, :], tmp[:, :, 0, :], OP.add)
                bar(g_, g_.tensor_tensor(rv[:, :, 1, :], rv[:, :, 1, :], tmp[:, :, 1, :], OP.subtract), "pl")
                knv = kn_r[:, 512 * pi:512 * (pi + 1)]
                for hh in range(4):
                    ins = g_.tensor_scalar(knv[:, 128 * hh:128 * (hh + 1)],
                                           kr_s[:, 128 * hh:128 * (hh + 1)],
                                           kc[:, pi, 0 + hh:1 + hh], None, OP.mult)
                inc(ins, "pl")
            marks[("kn", i)] = now("pl")
            marks[("knrope", i)] = marks[("kn", i)]

            def emit_qkt(I):
                PI = I % 2

                @sec("tensor")
                def _(t, I=I, PI=PI):
                    w(t, "dve", marks[("qn", I)])
                    w(t, "pl", marks[("kn", I)])
                    if I >= 1:
                        w(t, "dve", marks[("qkTev", I - 1)])
                    for hh in range(4):
                        t.transpose(pQKT[:, 128 * hh:128 * (hh + 1)],
                                    qn_r[:, 512 * PI + 128 * hh: 512 * PI + 128 * (hh + 1)], identb[:])
                    for hh in range(4):
                        ins = t.transpose(pQKT[:, 512 + 128 * hh:512 + 128 * (hh + 1)],
                                          kn_r[:, 512 * PI + 128 * hh: 512 * PI + 128 * (hh + 1)], identb[:])
                    inc(ins, "pe")
                marks[("qkTp", I)] = now("pe")

                @sec("vector")
                def _(v, I=I):
                    w(v, "pe", marks[("qkTp", I)])
                    v.tensor_copy(qfmv[:, :, 128 * I:128 * (I + 1)],
                                  pQKT[:, 0:512].rearrange("p (h c) -> p h c", h=4))
                    inc(v.tensor_copy(kfmv[:, :, 128 * I:128 * (I + 1)],
                                      pQKT[:, 512:1024].rearrange("p (h c) -> p h c", h=4)), "dve")
                marks[("qkTev", I)] = now("dve")

            if i >= 1:
                emit_qkt(i - 1)
            if i == NT - 1:
                emit_qkt(i)

            if debug and i == 0:
                @sec("gpsimd")
                def _(g_):
                    w(g_, "dve", marks[("qn", 0)])
                    w(g_, "pl", marks[("kn", 0)])
                    inc(g_.dma_start(out=dbg_t["d_x1t0"], in_=x1r[:, 0:D]), "pld", 16)
                    inc(g_.dma_start(out=dbg_t["d_lnc"], in_=lncol[:]), "pld", 16)
                    inc(g_.dma_start(out=dbg_t["d_qn0"], in_=qn_r[:, 0:512]), "pld", 16)
                    inc(g_.dma_start(out=dbg_t["d_kn0"], in_=kn_r[:, 0:512]), "pld", 16)
                    inc(g_.dma_start(out=dbg_t["d_ksb0"], in_=ksb[:, 0:512]), "pld", 16)
                    inc(g_.dma_start(out=dbg_t["d_qcol"], in_=qcol[:]), "pld", 16)
                marks["dbg_t0"] = now("pld")

        marks["p1done_act"] = now("act")
        marks["p1done_dve"] = now("dve")
        marks["p1done_pe"] = now("pe")

        if debug:
            @sec("gpsimd")
            def _(g_):
                w(g_, "act", marks["p1done_act"])
                inc(g_.dma_start(out=dbg_t["d_knlast"], in_=kn_r[:]), "pld", 16)
                inc(g_.dma_start(out=dbg_t["d_kclast"], in_=kcol[:]), "pld", 16)
                inc(g_.dma_start(out=dbg_t["d_ksblast"], in_=ksb[:]), "pld", 16)
                inc(g_.dma_start(out=dbg_t["d_qfm"], in_=qfm[:]), "pld", 16)
                inc(g_.dma_start(out=dbg_t["d_kfm"], in_=kfm[:]), "pld", 16)
                inc(g_.dma_start(out=dbg_t["d_vres"], in_=vres[:]), "pld", 16)
            marks["dbg_p1"] = now("pld")

        p01.close()

        p23 = ExitStack()
        sb2 = lambda n, s_, d: p23.enter_context(nc.sbuf_tensor(n, s_, d))
        ps2 = lambda n, s_, d: p23.enter_context(nc.psum_tensor(n, s_, d))
        kps = sb2("kps", [JL, S], F32)
        kpn = sb2("kpn", [JL, S], F32)
        rown = sb2("rown", [1, S], F32)
        cent = sb2("cent", [JL, NB], F32)
        centn = sb2("centn", [JL, NB], F32)
        qsum = sb2("qsum", [128, 1], F32)
        qsumb = sb2("qsumb", [128, 1], BF16)
        qp_s = sb2("qp_s", [JL, 1], F32)
        srow = sb2("srow", [1, 256], F32)
        nq_s = sb2("nq_s", [1, 4], F32)
        cmpm = sb2("cmpm", [64, 64], F32)
        cntc = sb2("cntc", [64, 2], F32)
        penr = sb2("penr", [1, 64], F32)
        nbrow = sb2("nbrow", [1, 32], F32)
        pexp = sb2("pexp", [128, 2 * 512], BF16)
        rcj = sb2("rcj", [128, 4], F32)
        ydiv = sb2("ydiv", [128, 512], BF16)
        yslab = sb2("yslab", [128, S], BF16)
        pkg = sb2("pkg", [128, 704], I32)
        dsc1 = sb2("dsc1", [128, 704], I32)
        dsc2 = sb2("dsc2", [128, 704], I32)
        dsc3 = sb2("dsc3", [128, 704], I32)
        dsc4 = sb2("dsc4", [128, 704], I32)
        dqg = sb2("dqg", [128, 5632], BF16)
        dstage = sb2("dstage", [128, 6144], BF16)
        pS0 = ps2("pS0", [128, 512], F32)
        pS1 = ps2("pS1", [128, 512], F32)
        pv_banks = [ps2(f"pv{j}", [128, 512], F32) for j in range(4)]
        pStat = ps2("pStat", [128, 512], F32)
        pYT = ps2("pYT", [128, 1024], BF16)

        marks["pYTfree"] = ("act", 0)

        dq_steps = []

        def dq_planes(eng, pkv, dqv, nib_ot, dep_marks, j0, j1):
            def fn():
                secname = "vector" if eng == "dve" else "gpsimd"

                @sec(secname)
                def _(e):
                    for (sname, mkey) in dep_marks:
                        w(e, sname, marks.get(mkey, 0))
                    half = slice(0, 704)
                    bsem = "dve" if eng == "dve" else "pl"
                    for j in range(j0, j1):
                        bar(e, e.tensor_scalar(dsc1[:, half], pkv, 4 * j, 15,
                                               OP.logical_shift_right, OP.bitwise_and), bsem)
                        bar(e, e.tensor_scalar(dsc2[:, half], dsc1[:, half], 7, 6,
                                               OP.bitwise_and, OP.logical_shift_left), bsem)
                        e.tensor_scalar(dsc3[:, half], dsc2[:, half], 2, 0x3E80, OP.mult, OP.add)
                        bar(e, e.tensor_scalar(dsc4[:, half], dsc2[:, half], 0x3F00, None, OP.add), bsem)
                        bar(e, e.tensor_tensor(dsc3[:, half], dsc3[:, half], dsc4[:, half], OP.min), bsem)
                        bar(e, e.scalar_tensor_tensor(dsc3[:, half], dsc2[:, half], 252, dsc3[:, half],
                                                      OP.mult, OP.min), bsem)
                        bar(e, e.tensor_scalar(dsc1[:, half], dsc1[:, half], 8, 12,
                                               OP.bitwise_and, OP.logical_shift_left), bsem)
                        bar(e, e.tensor_tensor(dsc4[:, half], dsc3[:, half], dsc1[:, half], OP.bitwise_or), bsem)
                        src = dsc4[:, half].bitcast(U16).rearrange(
                            "p (ot g two) -> p ot g two", ot=nib_ot, two=2)[:, :, :, 0:1]
                        ins = e.tensor_copy(dqv[:, :, :, j:j + 1], src)
                        if j < j1 - 1:
                            bar(e, ins, bsem)
                    inc(ins, "dve" if eng == "dve" else "pl")
            return fn

        def gu_chunk(k):
            def load():
                @sec("sync")
                def _(sy):
                    if k > 0:
                        w(sy, "dve", marks[("gu_pl", k - 1)])
                    else:
                        w(sy, "pe", marks["p1done_pe"])
                        w(sy, "act", marks["p1done_act"])
                        w(sy, "dve", marks[("qn", NT - 1)])
                        w(sy, "pl", marks[("kn", NT - 1)])
                        if debug:
                            w(sy, "pld", marks.get("dbg_p1", 0))
                    inc(sy.dma_start(out=pkg[:].rearrange("p (ot g) -> p ot g", ot=22),
                                     in_=gup_s.rearrange("(ot p) g -> p ot g", p=128)[:, 22 * k:22 * (k + 1), :]), "sp", 16)
                marks[("gu_ld", k)] = now("sp")
            dq_steps.append(load)

            dqv = dqg[:].bitcast(U16).rearrange("p (ot g j) -> p ot g j", ot=22, g=32)

            def dve_a():
                dq_planes("dve", pkg[:, 0:704], dqv, 22,
                          [("sp", ("gu_ld", k)), ("pe", ("gu_tp", k - 1))], 0, 3)()
                marks[("gu_dva", k)] = now("dve")

            def dve_b():
                dq_planes("dve", pkg[:, 0:704], dqv, 22, [], 3, 6)()
                marks[("gu_dvb", k)] = now("dve")
            dq_steps.append(dve_a)
            dq_steps.append(dve_b)

            def dve_c():
                dq_planes("dve", pkg[:, 0:704], dqv, 22, [], 6, 8)()
                marks[("gu_pl", k)] = now("dve")
            dq_steps.append(dve_c)

            def scales_T():
                @sec("vector")
                def _(v):
                    scgv = scg[:].rearrange("p (g t) -> p g t", g=2)
                    for ot in range(22):
                        for g in range(2):
                            ins = v.tensor_scalar(dqg[:, 256 * ot + 128 * g:256 * ot + 128 * (g + 1)],
                                                  dqg[:, 256 * ot + 128 * g:256 * ot + 128 * (g + 1)],
                                                  scgv[:, g, 22 * k + ot:22 * k + ot + 1], None, OP.mult)
                    inc(ins, "dve")
                marks[("gu_sc", k)] = now("dve")
                for ih in range(2):
                    o0 = 0
                    for nb_ in [4, 4, 4, 4, 4, 2]:
                        @sec("tensor")
                        def _(t, ih=ih, o0=o0, nb_=nb_):
                            w(t, "dve", marks[("gu_sc", k)])
                            fs, fc = marks["pYTfree"]
                            w(t, fs, fc)
                            for u in range(nb_):
                                ot = o0 + u
                                ins = t.transpose(pYT[:, 128 * u:128 * (u + 1)],
                                                  dqg[:, 256 * ot + 128 * ih:256 * ot + 128 * (ih + 1)],
                                                  identb[:])
                            inc(ins, "pe")
                        marks["pYTw"] = now("pe")

                        @sec("scalar")
                        def _(s_, ih=ih, o0=o0, nb_=nb_):
                            w(s_, "pe", marks["pYTw"])
                            w(s_, "sgu", marks.get(("gu_st", k - 1), 0))
                            dsv = dstage[:, 0:5632].rearrange("p (i ot c) -> p i ot c", i=2, ot=22)
                            inc(s_.copy(dsv[:, ih, o0:o0 + nb_, :], pYT[:, 0:128 * nb_]), "act")
                        marks["pYTfree"] = ("act", now("act"))
                        o0 += nb_
                marks[("gu_tp", k)] = now("pe")
                marks[("gu_tpev", k)] = now("act")
            dq_steps.append(scales_T)

            def store():
                @sec("gpsimd")
                def _(g_):
                    w(g_, "act", marks[("gu_tpev", k)])
                    dsv = dstage[:, 0:5632].rearrange("p (i c) -> p i c", i=2)
                    dst = bass.AP(wguT_in, 2816 * k,
                                  [[2 * IFFN, 128], [128 * 2 * IFFN, 2], [1, 2816]])
                    inc(g_.dma_start(out=dst, in_=dsv), "sgu", 16)
                marks[("gu_st", k)] = now("sgu")
            dq_steps.append(store)

        def dn_chunk(m):
            def load():
                @sec("sync")
                def _(sy):
                    if m == 0:
                        w(sy, "dve", marks[("gu_sc", 3)])
                    else:
                        w(sy, "dve", marks[("dn_pl", m - 1)])
                    inc(sy.dma_start(out=pkg[:].rearrange("p (ot g) -> p ot g", ot=8),
                                     in_=dnp_s.rearrange("(ot p) g -> p ot g", p=128)[:, 8 * m:8 * (m + 1), :]), "sp", 16)
                marks[("dn_ld", m)] = now("sp")
            dq_steps.append(load)

            dqv = dqg[:].bitcast(U16).rearrange("p (ot g j) -> p ot g j", ot=8, g=88)

            def dve_a():
                dq_planes("dve", pkg[:, 0:704], dqv, 8,
                          [("sp", ("dn_ld", m)),
                           ("pe", ("dn_tp", m - 1) if m > 0 else ("gu_tp", 3))], 0, 3)()
                marks[("dn_dva", m)] = now("dve")

            def dve_b():
                dq_planes("dve", pkg[:, 0:704], dqv, 8, [], 3, 6)()
                marks[("dn_dvb", m)] = now("dve")
            dq_steps.append(dve_a)
            dq_steps.append(dve_b)

            def dve_c():
                dq_planes("dve", pkg[:, 0:704], dqv, 8, [], 6, 8)()
                marks[("dn_pl", m)] = now("dve")
            dq_steps.append(dve_c)

            def scales_T():
                @sec("vector")
                def _(v):
                    scdv = scd[:].rearrange("p (s t) -> p s t", s=11)
                    for ot in range(8):
                        for sg_ in range(11):
                            ins = v.tensor_scalar(dqg[:, 704 * ot + 64 * sg_:704 * ot + 64 * (sg_ + 1)],
                                                  dqg[:, 704 * ot + 64 * sg_:704 * ot + 64 * (sg_ + 1)],
                                                  scdv[:, sg_, 8 * m + ot:8 * m + ot + 1], None, OP.mult)
                    inc(ins, "dve")
                marks[("dn_sc", m)] = now("dve")
                for ot in range(8):
                    @sec("tensor")
                    def _(t, ot=ot):
                        w(t, "dve", marks[("dn_sc", m)])
                        fs, fc = marks["pYTfree"]
                        w(t, fs, fc)
                        for ih in range(5):
                            t.transpose(pYT[:, 128 * ih:128 * (ih + 1)],
                                        dqg[:, 704 * ot + 128 * ih:704 * ot + 128 * (ih + 1)],
                                        identb[:])
                        ins = t.transpose(pYT[0:64, 640:768],
                                          dqg[:, 704 * ot + 640:704 * ot + 704], identb[:])
                        inc(ins, "pe")
                    marks["pYTw"] = now("pe")

                    @sec("scalar")
                    def _(s_, ot=ot):
                        w(s_, "pe", marks["pYTw"])
                        if ot == 0:
                            w(s_, "sgu", marks.get(("dn_st", m - 1), 0))
                        dsv = dstage[:].rearrange("p (i ot c) -> p i ot c", i=6, ot=8)
                        inc(s_.copy(dsv[:, 0:6, ot, :],
                                    pYT[:, 0:768].rearrange("p (i c) -> p i c", i=6)), "act")
                    marks["pYTfree"] = ("act", now("act"))
                marks[("dn_tp", m)] = now("pe")
                marks[("dn_tpev", m)] = now("act")
            dq_steps.append(scales_T)

            def store():
                @sec("gpsimd")
                def _(g_):
                    w(g_, "act", marks[("dn_tpev", m)])
                    dsv = dstage[:].rearrange("p (i ot c) -> p i ot c", i=6, ot=8)
                    dst1 = bass.AP(wdT_in, 1024 * m, [[D, 128], [128 * D, 5], [1, 1024]])
                    src1 = dsv[:, 0:5, :, :].rearrange("p i ot c -> p i (ot c)")
                    inc(g_.dma_start(out=dst1, in_=src1), "sgu", 16)
                    dst2 = bass.AP(wdT_in, 640 * D + 1024 * m, [[D, 64], [1, 1024]])
                    src2 = dsv[0:64, 5, :, :].rearrange("p ot c -> p (ot c)")
                    inc(g_.dma_start(out=dst2, in_=src2), "sgu", 16)
                marks[("dn_st", m)] = now("sgu")
            dq_steps.append(store)

        for k in range(4):
            gu_chunk(k)
        for m in range(2):
            dn_chunk(m)

        def dq_collectives():
            @sec("gpsimd")
            def _(g_):
                w(g_, "sgu", marks[("gu_st", 3)])
                inc(g_.collective_compute(
                    "AllGather", OP.bypass, replica_groups=[list(range(8))],
                    ins=[wguT_in.ap().opt()], outs=[wguT_all.ap().opt()]), "cgu")
                w(g_, "sgu", marks[("dn_st", 1)])
                inc(g_.collective_compute(
                    "AllGather", OP.bypass, replica_groups=[list(range(8))],
                    ins=[wdT_in.ap().opt()], outs=[wdT_all.ap().opt()]), "cwd")
        dq_steps.append(dq_collectives)

        def emit_stats(h):
            @sec("vector")
            def _(v):
                w(v, "act", marks["p1done_act"])
                w(v, "dve", marks["p1done_dve"])
                if debug:
                    w(v, "pld", marks.get("dbg_p1", 0))
                if h > 0:
                    w(v, "pe", marks[("qp_mm", h - 1)])
                bar(v, v.reduce_sum(qsum[:], qfmv[:, h, :], axis=mybir.AxisListType.X), "dve")
                inc(v.tensor_copy(qsumb[:], qsum[:]), "dve")
            marks[("qsum", h)] = now("dve")

            for cc_ in range(8):
                @sec("tensor")
                def _(t, cc_=cc_):
                    if cc_ == 0:
                        w(t, "dve", marks[("qsum", h)])
                        if debug:
                            w(t, "pld", marks.get("dbg_p1", 0))
                        if h > 0:
                            w(t, "dve", marks[("kpn_used", h - 1)])
                    w(t, "dve", marks.get(("kp_ev", h, cc_ - 1), 0))
                    inc(t.matmul(pStat[0:JL, 0:512], projb[:],
                                 kfmv[:, h, 512 * cc_:512 * (cc_ + 1)],
                                 start=True, stop=True), "pe")
                marks[("kp_mm", h, cc_)] = now("pe")

                @sec("vector")
                def _(v, cc_=cc_):
                    w(v, "pe", marks[("kp_mm", h, cc_)])
                    inc(v.tensor_copy(kps[:, 512 * cc_:512 * (cc_ + 1)], pStat[0:JL, 0:512]), "dve")
                marks[("kp_ev", h, cc_)] = now("dve")

            @sec("vector")
            def _(v):
                w(v, "dve", marks[("kp_ev", h, 7)])
                inc(v.tensor_tensor(kpn[:], kps[:], kps[:], OP.mult), "dve")
            marks[("kpsq", h)] = now("dve")

            for cc_ in range(8):
                @sec("tensor")
                def _(t, cc_=cc_):
                    w(t, "dve", marks[("kpsq", h)])
                    w(t, "dve", marks.get(("cs_ev", h, cc_ - 1), 0))
                    inc(t.matmul(pStat[0:1, 0:512], _ones32(),
                                 kpn[:, 512 * cc_:512 * (cc_ + 1)],
                                 start=True, stop=True), "pe")
                marks[("cs_mm", h, cc_)] = now("pe")

                @sec("vector")
                def _(v, cc_=cc_):
                    w(v, "pe", marks[("cs_mm", h, cc_)])
                    inc(v.tensor_copy(rown[0:1, 512 * cc_:512 * (cc_ + 1)], pStat[0:1, 0:512]), "dve")
                marks[("cs_ev", h, cc_)] = now("dve")

            @sec("scalar")
            def _(s_):
                w(s_, "dve", marks[("cs_ev", h, 7)])
                bar(s_, s_.activation(rown[0:1, :], rown[0:1, :], AF.Ln), "act")
                inc(s_.activation(rown[0:1, :], rown[0:1, :], AF.Exp, scale=0.5), "act")
            marks[("nrm_sq", h)] = now("act")

            @sec("vector")
            def _(v):
                w(v, "act", marks[("nrm_sq", h)])
                bar(v, v.tensor_scalar(rown[0:1, :], rown[0:1, :], 1e-12, None, OP.max), "dve")
                inc(v.reciprocal(rown[0:1, :], rown[0:1, :]), "dve")
            marks[("nrm_rr", h)] = now("dve")

            for cc_ in range(8):
                @sec("tensor")
                def _(t, cc_=cc_):
                    w(t, "dve", marks[("nrm_rr", h)])
                    w(t, "dve", marks.get(("kpn_mul", h, cc_ - 1), 0))
                    inc(t.matmul(pStat[0:JL, 0:512], onesr[0:1, 0:JL],
                                 rown[0:1, 512 * cc_:512 * (cc_ + 1)],
                                 start=True, stop=True), "pe")
                marks[("bc_mm", h, cc_)] = now("pe")

                @sec("vector")
                def _(v, cc_=cc_):
                    w(v, "pe", marks[("bc_mm", h, cc_)])
                    inc(v.tensor_tensor(kpn[:, 512 * cc_:512 * (cc_ + 1)],
                                        kps[:, 512 * cc_:512 * (cc_ + 1)],
                                        pStat[0:JL, 0:512], OP.mult), "dve")
                marks[("kpn_mul", h, cc_)] = now("dve")
            marks[("kpn", h)] = now("dve")

            @sec("vector")
            def _(v):
                w(v, "dve", marks[("kpn", h)])
                bar(v, v.reduce_sum(cent[:], kpn[:].rearrange("p (b c) -> p b c", b=NB),
                                    axis=mybir.AxisListType.X), "dve")
                inc(v.tensor_tensor(centn[:], cent[:], cent[:], OP.mult), "dve")
            marks[("cent", h)] = now("dve")

            @sec("tensor")
            def _(t):
                w(t, "dve", marks[("cent", h)])
                inc(t.matmul(pStat[0:1, 0:NB], _ones32(), centn[:], start=True, stop=True), "pe")
            marks[("cn_mm", h)] = now("pe")

            @sec("scalar")
            def _(s_):
                w(s_, "pe", marks[("cn_mm", h)])
                bar(s_, s_.copy(srow[0:1, 192:256], pStat[0:1, 0:NB]), "act")
                bar(s_, s_.activation(srow[0:1, 192:256], srow[0:1, 192:256], AF.Ln), "act")
                inc(s_.activation(srow[0:1, 192:256], srow[0:1, 192:256], AF.Exp, scale=0.5), "act")
            marks[("cn_sq", h)] = now("act")

            @sec("vector")
            def _(v):
                w(v, "act", marks[("cn_sq", h)])
                bar(v, v.tensor_scalar(srow[0:1, 192:256], srow[0:1, 192:256], 1e-12, None, OP.max), "dve")
                inc(v.reciprocal(srow[0:1, 192:256], srow[0:1, 192:256]), "dve")
            marks[("cn_rr", h)] = now("dve")

            @sec("tensor")
            def _(t):
                w(t, "dve", marks[("cn_rr", h)])
                inc(t.matmul(pStat[0:JL, 0:NB], onesr[0:1, 0:JL], srow[0:1, 192:256],
                             start=True, stop=True), "pe")
            marks[("cnbc_mm", h)] = now("pe")

            @sec("vector")
            def _(v):
                w(v, "pe", marks[("cnbc_mm", h)])
                inc(v.tensor_tensor(centn[:], cent[:], pStat[0:JL, 0:NB], OP.mult), "dve")
            marks[("centn", h)] = now("dve")

            @sec("vector")
            def _(v):
                w(v, "dve", marks[("centn", h)])
                ce = centn[:].rearrange("p (b o) -> p b o", o=1).broadcast_to((JL, NB, 64))
                inc(v.tensor_tensor(kps[:].rearrange("p (b c) -> p b c", b=NB),
                                    kpn[:].rearrange("p (b c) -> p b c", b=NB),
                                    ce, OP.mult), "dve")
            marks[("prod", h)] = now("dve")

            for cc_ in range(8):
                @sec("tensor")
                def _(t, cc_=cc_):
                    w(t, "dve", marks[("prod", h)])
                    w(t, "dve", marks.get(("dots_ev", h, cc_ - 1), 0))
                    inc(t.matmul(pStat[0:1, 0:512], _ones32(),
                                 kps[:, 512 * cc_:512 * (cc_ + 1)],
                                 start=True, stop=True), "pe")
                marks[("dots_mm", h, cc_)] = now("pe")

                @sec("vector")
                def _(v, cc_=cc_):
                    w(v, "pe", marks[("dots_mm", h, cc_)])
                    inc(v.tensor_copy(rown[0:1, 512 * cc_:512 * (cc_ + 1)], pStat[0:1, 0:512]), "dve")
                marks[("dots_ev", h, cc_)] = now("dve")

            @sec("vector")
            def _(v):
                w(v, "dve", marks[("dots_ev", h, 7)])
                bar(v, v.tensor_reduce(srow[0:1, 128:192], rown[0:1, :].rearrange("o (b c) -> o b c", b=NB),
                                       axis=mybir.AxisListType.X, op=OP.min), "dve")
                bar(v, v.tensor_scalar(srow[0:1, 64:128], srow[0:1, 128:192], -1.0, 1.0, OP.mult, OP.add), "dve")
                inc(v.tensor_scalar(srow[0:1, 64:128], srow[0:1, 64:128], 0.0, 1.0, OP.max, OP.min), "dve")
            marks[("radii", h)] = now("dve")

            @sec("tensor")
            def _(t):
                w(t, "dve", marks[("qsum", h)])
                w(t, "dve", marks[("dots_ev", h, 7)])
                inc(t.matmul(pStat[0:JL, 64:65], projb[:], qsumb[:], start=True, stop=True), "pe")
            marks[("qp_mm", h)] = now("pe")

            @sec("scalar")
            def _(s_):
                w(s_, "pe", marks[("qp_mm", h)])
                inc(s_.copy(qp_s[:], pStat[0:JL, 64:65]), "act")
            marks[("qp_ev", h)] = now("act")

            @sec("vector")
            def _(v):
                w(v, "act", marks[("qp_ev", h)])
                inc(v.tensor_tensor(kps[0:JL, 0:1], qp_s[:], qp_s[:], OP.mult), "dve")
            marks[("qpsq", h)] = now("dve")

            @sec("tensor")
            def _(t):
                w(t, "dve", marks[("qpsq", h)])
                inc(t.matmul(pStat[0:1, 65:66], _ones32(), kps[0:JL, 0:1], start=True, stop=True), "pe")
            marks[("nq_mm", h)] = now("pe")

            @sec("scalar")
            def _(s_):
                w(s_, "pe", marks[("nq_mm", h)])
                bar(s_, s_.copy(nq_s[0:1, 0:1], pStat[0:1, 65:66]), "act")
                bar(s_, s_.activation(nq_s[0:1, 0:1], nq_s[0:1, 0:1], AF.Ln), "act")
                inc(s_.activation(nq_s[0:1, 0:1], nq_s[0:1, 0:1], AF.Exp, scale=0.5), "act")
            marks[("nq_sq", h)] = now("act")

            @sec("vector")
            def _(v):
                w(v, "act", marks[("nq_sq", h)])
                bar(v, v.tensor_scalar(nq_s[0:1, 0:1], nq_s[0:1, 0:1], 1e-12, None, OP.max), "dve")
                inc(v.reciprocal(nq_s[0:1, 0:1], nq_s[0:1, 0:1]), "dve")
            marks[("nq_rr", h)] = now("dve")

            @sec("tensor")
            def _(t):
                w(t, "dve", marks[("nq_rr", h)])
                inc(t.matmul(pStat[0:1, 0:NB], qp_s[:], centn[:], start=True, stop=True), "pe")
            marks[("sc_mm", h)] = now("pe")

            @sec("vector")
            def _(v):
                w(v, "pe", marks[("sc_mm", h)])
                bar(v, v.tensor_scalar(srow[0:1, 0:64], pStat[0:1, 0:NB], nq_s[0:1, 0:1], None, OP.mult), "dve")
                bar(v, v.tensor_tensor(srow[0:1, 0:64], srow[0:1, 0:64], srow[0:1, 64:128], OP.add), "dve")
                inc(v.memset(srow[0:1, 62:64], 1e9), "dve")
            marks[("scores", h)] = now("dve")

            if debug and h == 0:
                @sec("gpsimd")
                def _(g_):
                    w(g_, "dve", marks[("scores", h)])
                    inc(g_.dma_start(out=dbg_t["d_kp"], in_=kps[:]), "pld", 16)
                    inc(g_.dma_start(out=dbg_t["d_kpn"], in_=kpn[:]), "pld", 16)
                    inc(g_.dma_start(out=dbg_t["d_cent"], in_=centn[:]), "pld", 16)
                    inc(g_.dma_start(out=dbg_t["d_scores"], in_=srow[0:1, 0:64]), "pld", 16)
                marks[("dbg_stats", h)] = now("pld")

            @sec("tensor")
            def _(t):
                w(t, "dve", marks[("scores", h)])
                if debug and h == 0:
                    w(t, "pld", marks[("dbg_stats", h)])
                t.matmul(pStat[0:64, 0:64], onesr[0:1, 0:64], srow[0:1, 0:64], start=True, stop=True)
                inc(t.matmul(pStat[0:64, 64:65], srow[0:1, 0:64], onesr[0:1, 0:1], start=True, stop=True), "pe")
            marks[("mask_mm", h)] = now("pe")

            @sec("vector")
            def _(v):
                w(v, "pe", marks[("mask_mm", h)])
                bar(v, v.tensor_scalar(cmpm[:], pStat[0:64, 0:64], pStat[0:64, 64:65], None, OP.is_gt), "dve")
                bar(v, v.reduce_sum(cntc[0:64, 0:1], cmpm[:], axis=mybir.AxisListType.X), "dve")
                bar(v, v.tensor_scalar(cntc[0:64, 0:1], cntc[0:64, 0:1], 33.5, None, OP.is_lt), "dve")
                inc(v.tensor_scalar(cntc[0:64, 1:2], cntc[0:64, 0:1], -1.0, 30000.0,
                                    OP.add, OP.mult), "dve")
            marks[("pen_col", h)] = now("dve")

            @sec("tensor")
            def _(t):
                w(t, "dve", marks[("pen_col", h)])
                inc(t.transpose(pStat[0:1, 66:130], cntc[0:64, 1:2], identf[0:64, 0:64]), "pe")
            marks[("penr_mm", h)] = now("pe")

            @sec("scalar")
            def _(s_):
                w(s_, "pe", marks[("penr_mm", h)])
                inc(s_.copy(penr[0:1, :], pStat[0:1, 66:130]), "act")
            marks[("penr", h)] = now("act")

            @sec("vector")
            def _(v):
                w(v, "act", marks[("penr", h)])
                inc(v.tensor_scalar(nbrow[0:1, :], onesr[0:1, 0:32], negb[0:1, h:h + 1], None, OP.mult), "dve")
            marks[("nbrow", h)] = now("dve")

            @sec("tensor")
            def _(t):
                w(t, "dve", marks[("nbrow", h)])
                pe_ = penr[0:1, :].rearrange("o (t two) -> o t two", two=2)
                t.matmul(pStat[0:128, 128:160], indu[0:1, :], pe_[:, :, 0:1].squeeze(2), start=True, stop=False)
                t.matmul(pStat[0:128, 128:160], indl[0:1, :], pe_[:, :, 1:2].squeeze(2), start=False, stop=False)
                inc(t.matmul(pStat[0:128, 128:160], onesr[0:1, :], nbrow[0:1, :], start=False, stop=True), "pe")
            marks[("ptab_mm", h)] = now("pe")

            @sec("scalar")
            def _(s_):
                w(s_, "pe", marks[("ptab_mm", h)])
                inc(s_.copy(pen_tab[:, 32 * h:32 * (h + 1)], pStat[0:128, 128:160]), "act")
            marks[("ptab", h)] = now("act")
            marks[("kpn_used", h)] = now("dve")

        def emit_attend(h):
            for ch in range(8):
                A0 = now("act")

                def expmark(hh, cc, tt):
                    if (hh, cc) == (h, ch):
                        return A0 + tt + 1
                    return marks[("exp", hh, cc, tt)]

                @sec("tensor")
                def _(t, h=h, ch=ch, A0=A0):
                    for t_i in range(34):
                        if t_i < 32:
                            if t_i == 0 and ch == 0:
                                w(t, "act", marks[("ptab", h)])
                            if t_i >= 2:
                                w(t, "act", expmark(h, ch, t_i - 2))
                            elif ch > 0:
                                w(t, "act", expmark(h, ch - 1, 30 + t_i))
                            elif h > 0:
                                w(t, "act", expmark(h - 1, 7, 30 + t_i))
                            pS = pS0 if t_i % 2 == 0 else pS1
                            inc(t.matmul(pS[:, 0:512],
                                         kfmv[:, h, 128 * t_i:128 * (t_i + 1)],
                                         qfmv[:, h, 512 * ch:512 * (ch + 1)],
                                         start=True, stop=True), "pe")
                            marks[("qk", h, ch, t_i)] = now("pe")
                        if t_i >= 2:
                            tp = t_i - 2
                            w(t, "act", expmark(h, ch, tp))
                            if tp == 0:
                                pm = marks.get(("yfin", h, ch - 1)) if ch > 0 else marks.get(("yfin", h - 1, 7))
                                if pm:
                                    w(t, "dve", pm)
                            for j in range(4):
                                ins = t.matmul(pv_banks[j][:, 0:129],
                                               pexp[:, 512 * (tp % 2) + 128 * j:512 * (tp % 2) + 128 * (j + 1)],
                                               vview[:, h, tp, 0:129],
                                               start=(tp == 0), stop=(tp == 31))
                            inc(ins, "pe")
                            marks[("pv", h, ch, tp)] = now("pe")

                @sec("scalar")
                def _(s_, h=h, ch=ch):
                    for t_i in range(32):
                        w(s_, "pe", marks[("qk", h, ch, t_i)])
                        prev = None
                        if t_i >= 2:
                            prev = ("pv", h, ch, t_i - 2)
                        elif ch > 0:
                            prev = ("pv", h, ch - 1, 30 + t_i)
                        elif h > 0:
                            prev = ("pv", h - 1, 7, 30 + t_i)
                        if prev is not None:
                            w(s_, "pe", marks[prev])
                        pS = pS0 if t_i % 2 == 0 else pS1
                        ins = s_.activation(pexp[:, 512 * (t_i % 2):512 * (t_i % 2 + 1)],
                                            pS[:, 0:512], AF.Exp,
                                            bias=pen_tab[:, 32 * h + t_i:32 * h + t_i + 1],
                                            scale=SC_QK)
                        inc(ins, "act")
                        marks[("exp", h, ch, t_i)] = now("act")
                assert now("act") == A0 + 32

                @sec("vector")
                def _(v, h=h, ch=ch):
                    w(v, "pe", marks[("pv", h, ch, 31)])
                    for j in range(4):
                        ins = v.reciprocal(rcj[:, j:j + 1], pv_banks[j][:, 128:129])
                    bar(v, ins, "dve")
                    for j in range(4):
                        ins = v.tensor_scalar(ydiv[:, 128 * j:128 * (j + 1)],
                                              pv_banks[j][:, 0:128], rcj[:, j:j + 1], None, OP.mult)
                    inc(ins, "dve")
                marks[("yfin", h, ch)] = now("dve")

                @sec("tensor")
                def _(t, h=h, ch=ch):
                    w(t, "dve", marks[("yfin", h, ch)])
                    fs, fc = marks["pYTfree"]
                    w(t, fs, fc)
                    for j in range(4):
                        ins = t.transpose(pYT[:, 128 * j:128 * (j + 1)],
                                          ydiv[:, 128 * j:128 * (j + 1)], identb[:])
                    inc(ins, "pe")
                marks["pYTw"] = now("pe")

                @sec("vector")
                def _(v, h=h, ch=ch):
                    w(v, "pe", marks["pYTw"])
                    w(v, "pld", marks.get(("yslab_st", h - 1), 0))
                    inc(v.tensor_copy(yslab[:, 512 * ch:512 * (ch + 1)], pYT[:, 0:512]), "dve")
                marks["pYTfree"] = ("dve", now("dve"))
                marks[("ychunk", h, ch)] = now("dve")

                for _pop in range(2):
                    if dq_steps:
                        dq_steps.pop(0)()

            @sec("gpsimd")
            def _(g_, h=h):
                w(g_, "dve", marks[("ychunk", h, 7)])
                dst = y_in.ap().rearrange("(h p) s -> p h s", p=128)[:, h, :]
                inc(g_.dma_start(out=dst, in_=yslab[:]), "pld", 16)
            marks[("yslab_st", h)] = now("pld")

        for h in range(HPC):
            emit_stats(h)
            emit_attend(h)

        while dq_steps:
            dq_steps.pop(0)()

        @sec("gpsimd")
        def _(g_):
            w(g_, "pld", marks[("yslab_st", HPC - 1)])
            inc(g_.collective_compute(
                "AllGather", OP.bypass,
                replica_groups=[[0, 1, 2, 3], [4, 5, 6, 7]],
                ins=[y_in.ap().opt()], outs=[y_all.ap().opt()]), "cy")

        if debug:
            @sec("gpsimd")
            def _(g_):
                w(g_, "cy", 1)
                inc(g_.dma_start(out=dbg_t["d_yin"], in_=y_in.ap()), "pld", 16)
                inc(g_.dma_start(out=dbg_t["d_yall"], in_=y_all.ap()), "pld", 16)
                inc(g_.dma_start(out=dbg_t["d_pen"], in_=pen_tab[:]), "pld", 16)

        marks["attn_done_pe"] = now("pe")
        p23.close()
        pA.close()

        pB = ExitStack()
        x2T = pB.enter_context(nc.sbuf_tensor("x2T", [128, 16 * QTOK], BF16))
        x2v = x2T[:].rearrange("p (dc s) -> p dc s", dc=16)

        p3b = ExitStack()
        sb3 = lambda n, s_, d: p3b.enter_context(nc.sbuf_tensor(n, s_, d))
        ps3 = lambda n, s_, d: p3b.enter_context(nc.psum_tensor(n, s_, d))
        yfmb = sb3("yfmb", [128, 16 * QTOK], BF16)
        woTs = sb3("woTs", [128, 2 * 16 * 512], BF16)
        atst = sb3("atst", [128, 2 * 512], F32)
        h1t = sb3("h1t", [128, 2 * D], F32)
        x2b = sb3("x2b", [128, 2 * D], BF16)
        ln2c = sb3("ln2c", [128, 2 * 8], F32)
        dmp2 = sb3("dmp2", [128, 2 * D], F32)
        pWo = [ps3("pWo0", [128, 512], F32), ps3("pWo1", [128, 512], F32)]
        pX2T = ps3("pX2T", [128, 2048], BF16)

        @sec("gpsimd")
        def _(g_):
            w(g_, "cy", 1)
            with g_.register("rg0") as r0, g_.register("rg1") as r1:
                g_.reg_load(r0, pidc[0:1, 0:1])
                g_.reg_mod(r1, r0, 4)
                g_.reg_mul(r1, r1, QTOK)
                src = bass.AP(y_all, r1, [[S, 128], [128 * S, 16], [1, QTOK]])
                inc(g_.dma_start(out=yfmb[:].rearrange("p (dc s) -> p dc s", dc=16), in_=src), "pld", 16)
        marks["yfm_ld"] = now("pld")

        for oc in range(4):
            @sec("sync")
            def _(sy, oc=oc):
                w(sy, "cwo", 1)
                if oc == 0:
                    w(sy, "pe", marks["attn_done_pe"])
                    w(sy, "act")
                    w(sy, "dve")
                    w(sy, "pld")
                if oc >= 2:
                    w(sy, "pe", marks[("wo_mm", oc - 2, 7)])
                src = woT_all.ap().rearrange("(dc p) c -> p dc c", p=128)[:, :, 512 * oc:512 * (oc + 1)]
                inc(sy.dma_start(out=woTs[:].rearrange("p (r dc c) -> p r dc c", r=2, dc=16)[:, oc % 2, :, :],
                                 in_=src), "so%d" % (oc % 2), 16)
            marks[("woT_ld", oc)] = now("so%d" % (oc % 2))

            for ss in range(8):
                @sec("tensor")
                def _(t, oc=oc, ss=ss):
                    w(t, "so%d" % (oc % 2), marks[("woT_ld", oc)])
                    if oc == 0 and ss == 0:
                        w(t, "pld", marks["yfm_ld"])
                    idx = oc * 8 + ss
                    if idx >= 2:
                        w(t, "dve", marks[("wo_ev", idx - 2)])
                    yv = yfmb[:].rearrange("p (dc s) -> p dc s", dc=16)
                    wv_ = woTs[:].rearrange("p (r dc c) -> p r dc c", r=2, dc=16)
                    for dc in range(16):
                        ins = t.matmul(pWo[idx % 2][:, 0:512],
                                       yv[:, dc, 128 * ss:128 * (ss + 1)],
                                       wv_[:, oc % 2, dc, :],
                                       start=(dc == 0), stop=(dc == 15))
                    inc(ins, "pe")
                marks[("wo_mm", oc, ss)] = now("pe")

                @sec("vector")
                def _(v, oc=oc, ss=ss):
                    idx = oc * 8 + ss
                    w(v, "pe", marks[("wo_mm", oc, ss)])
                    if idx >= 2:
                        w(v, "pld", marks[("at_st", idx - 2)])
                    inc(v.tensor_copy(atst[:, 512 * (idx % 2):512 * (idx % 2 + 1)], pWo[idx % 2][:, 0:512]), "dve")
                marks[("wo_ev", oc * 8 + ss)] = now("dve")

                @sec("gpsimd")
                def _(g_, oc=oc, ss=ss):
                    idx = oc * 8 + ss
                    w(g_, "dve", marks[("wo_ev", idx)])
                    inc(g_.dma_start(out=attn_d.ap()[128 * ss:128 * (ss + 1), 512 * oc:512 * (oc + 1)],
                                     in_=atst[:, 512 * (idx % 2):512 * (idx % 2 + 1)]), "pld", 16)
                marks[("at_st", oc * 8 + ss)] = now("pld")

        if debug:
            @sec("gpsimd")
            def _(g_):
                w(g_, "pld", marks[("at_st", 31)])
                inc(g_.dma_start(out=dbg_t["d_attn"], in_=attn_d.ap()), "pld", 16)
                inc(g_.dma_start(out=dbg_t["d_woT"], in_=woT_all.ap()), "pld", 16)

        for ss in range(8):
            pi = ss % 2

            @sec("sync")
            def _(sy, ss=ss, pi=pi):
                w(sy, "pld", marks[("at_st", 24 + ss)])
                if ss >= 2:
                    w(sy, "act", marks[("ln2acc", ss - 2)])
                    w(sy, "dve", marks[("x2", ss - 2)])
                inc(sy.dma_start(out=h1t[:, D * pi:D * (pi + 1)],
                                 in_=attn_d.ap()[128 * ss:128 * (ss + 1), :]), "sh%d" % (ss % 2), 16)
                inc(sy.dma_start(out=dmp2[:, D * pi:D * (pi + 1)],
                                 in_=x_q[128 * ss:128 * (ss + 1), :]), "sh%d" % (ss % 2), 16)
            marks[("h1_ld", ss)] = now("sh%d" % (ss % 2))

            @sec("vector")
            def _(v, ss=ss, pi=pi):
                w(v, "sh%d" % (ss % 2), marks[("h1_ld", ss)])
                inc(v.tensor_tensor(h1t[:, D * pi:D * (pi + 1)],
                                    h1t[:, D * pi:D * (pi + 1)], dmp2[:, D * pi:D * (pi + 1)], OP.add), "dve")
            marks[("h1", ss)] = now("dve")

            @sec("scalar")
            def _(s_, ss=ss, pi=pi):
                w(s_, "dve", marks[("h1", ss)])
                if ss >= 2:
                    w(s_, "pe", marks[("x2Tp", ss - 2)])
                xt = h1t[:, D * pi:D * (pi + 1)]
                s_.activation(dmp2[:, D * pi:D * (pi + 1)], xt, AF.Copy, accum_out=ln2c[:, 8 * pi + 0:8 * pi + 1])
                inc(s_.activation(x2b[:, D * pi:D * (pi + 1)], xt, AF.Square,
                                  accum_out=ln2c[:, 8 * pi + 1:8 * pi + 2]), "act")
            marks[("ln2acc", ss)] = now("act")

            @sec("vector")
            def _(v, ss=ss, pi=pi):
                w(v, "act", marks[("ln2acc", ss)])
                o = 8 * pi
                v.tensor_scalar(ln2c[:, o + 2:o + 3], ln2c[:, o + 0:o + 1], 1.0 / D, None, OP.mult)
                bar(v, v.scalar_tensor_tensor(ln2c[:, o + 5:o + 6], ln2c[:, o + 0:o + 1],
                                              1.0 / (D * D), ln2c[:, o + 0:o + 1],
                                              OP.mult, OP.mult), "dve")
                inc(v.scalar_tensor_tensor(ln2c[:, o + 3:o + 4], ln2c[:, o + 1:o + 2], 1.0 / D,
                                           ln2c[:, o + 5:o + 6], OP.mult, OP.subtract), "dve")
            marks[("ln2cols", ss)] = now("dve")

            @sec("scalar")
            def _(s_, ss=ss, pi=pi):
                w(s_, "dve", marks[("ln2cols", ss)])
                o = 8 * pi
                bar(s_, s_.activation(ln2c[:, o + 4:o + 5], ln2c[:, o + 3:o + 4], AF.Ln, bias=epsln[:, 0:1]), "act")
                inc(s_.activation(ln2c[:, o + 4:o + 5], ln2c[:, o + 4:o + 5], AF.Exp, scale=-0.5), "act")
            marks[("ln2rstd", ss)] = now("act")

            @sec("vector")
            def _(v, ss=ss, pi=pi):
                w(v, "act", marks[("ln2rstd", ss)])
                o = 8 * pi
                if ss >= 2:
                    w(v, "pe", marks[("x2Tp", ss - 2)])
                inc(v.tensor_scalar(x2b[:, D * pi:D * (pi + 1)], h1t[:, D * pi:D * (pi + 1)],
                                    ln2c[:, o + 2:o + 3], ln2c[:, o + 4:o + 5],
                                    OP.subtract, OP.mult), "dve")
            marks[("x2", ss)] = now("dve")

            @sec("tensor")
            def _(t, ss=ss, pi=pi):
                w(t, "dve", marks[("x2", ss)])
                if ss >= 1:
                    w(t, "dve", marks[("x2Tev", ss - 1)])
                for j in range(16):
                    ins = t.transpose(pX2T[:, 128 * j:128 * (j + 1)],
                                      x2b[:, D * pi + 128 * j:D * pi + 128 * (j + 1)], identb[:])
                inc(ins, "pe")
            marks[("x2Tp", ss)] = now("pe")

            @sec("vector")
            def _(v, ss=ss):
                w(v, "pe", marks[("x2Tp", ss)])
                v.tensor_copy(x2v[:, 0:8, 128 * ss:128 * (ss + 1)],
                              pX2T[:, 0:1024].rearrange("p (dc c) -> p dc c", dc=8))
                inc(v.tensor_copy(x2v[:, 8:16, 128 * ss:128 * (ss + 1)],
                                  pX2T[:, 1024:2048].rearrange("p (dc c) -> p dc c", dc=8)), "dve")
            marks[("x2Tev", ss)] = now("dve")
        marks["x2T"] = now("dve")

        if debug:
            @sec("gpsimd")
            def _(g_):
                w(g_, "dve", marks["x2T"])
                inc(g_.dma_start(out=dbg_t["d_x2t"], in_=x2T[:]), "pld", 16)
                w(g_, "cgu", 1)
                inc(g_.dma_start(out=dbg_t["d_wg"], in_=wguT_all.ap()), "pld", 16)
                w(g_, "cwd", 1)
                inc(g_.dma_start(out=dbg_t["d_wd"], in_=wdT_all.ap()), "pld", 16)

        p3b.close()

        p4base = ExitStack()
        hfm = p4base.enter_context(nc.sbuf_tensor("hfm", [128, 44 * QTOK], BF16))

        p4a = ExitStack()
        wgs = p4a.enter_context(nc.sbuf_tensor("wgs", [128, 2 * 2 * 16 * 128], BF16))
        gsil = p4a.enter_context(nc.sbuf_tensor("gsil", [128, 512], BF16))
        pGt = p4a.enter_context(nc.psum_tensor("pGt", [128, 512], F32))
        pUp = p4a.enter_context(nc.psum_tensor("pUp", [128, 512], F32))

        for ip in range(44):
            @sec("sync")
            def _(sy, ip=ip):
                w(sy, "cgu", 1)
                w(sy, "dve", marks["x2T"])
                if ip >= 2:
                    w(sy, "pe", marks[("gu_mm", ip - 2, 1)])
                wgv = wgs[:].rearrange("p (r gu dc c) -> p r gu dc c", r=2, gu=2, dc=16)
                src_g = wguT_all.ap().rearrange("(dc p) c -> p dc c", p=128)[:, :, 128 * ip:128 * (ip + 1)]
                src_u = wguT_all.ap().rearrange("(dc p) c -> p dc c", p=128)[:, :, IFFN + 128 * ip:IFFN + 128 * (ip + 1)]
                inc(sy.dma_start(out=wgv[:, ip % 2, 0, :, :], in_=src_g), "sg%d" % (ip % 2), 16)
                inc(sy.dma_start(out=wgv[:, ip % 2, 1, :, :], in_=src_u), "sg%d" % (ip % 2), 16)
            marks[("wg_ld", ip)] = now("sg%d" % (ip % 2))

            for sc_ in range(2):
                @sec("tensor")
                def _(t, ip=ip, sc_=sc_):
                    w(t, "sg%d" % (ip % 2), marks[("wg_ld", ip)])
                    if ip == 0 and sc_ == 0:
                        w(t, "dve", marks["x2T"])
                    if not (ip == 0 and sc_ == 0):
                        pidx = (ip, sc_ - 1) if sc_ > 0 else (ip - 1, 1)
                        w(t, "dve", marks[("h_mul", pidx[0], pidx[1])])
                        w(t, "act", marks[("gsil", pidx[0], pidx[1])])
                    wgv = wgs[:].rearrange("p (r gu dc c) -> p r gu dc c", r=2, gu=2, dc=16)
                    for dc in range(16):
                        t.matmul(pGt[:, 0:512], wgv[:, ip % 2, 0, dc, :],
                                 x2v[:, dc, 512 * sc_:512 * (sc_ + 1)],
                                 start=(dc == 0), stop=(dc == 15))
                    for dc in range(16):
                        ins = t.matmul(pUp[:, 0:512], wgv[:, ip % 2, 1, dc, :],
                                       x2v[:, dc, 512 * sc_:512 * (sc_ + 1)],
                                       start=(dc == 0), stop=(dc == 15))
                    inc(ins, "pe")
                marks[("gu_mm", ip, sc_)] = now("pe")

                @sec("scalar")
                def _(s_, ip=ip, sc_=sc_):
                    w(s_, "pe", marks[("gu_mm", ip, sc_)])
                    inc(s_.activation(gsil[:], pGt[:, 0:512], AF.Silu), "act")
                marks[("gsil", ip, sc_)] = now("act")

                @sec("vector")
                def _(v, ip=ip, sc_=sc_):
                    w(v, "act", marks[("gsil", ip, sc_)])
                    inc(v.tensor_tensor(hfm[:, QTOK * ip + 512 * sc_:QTOK * ip + 512 * (sc_ + 1)],
                                        pUp[:, 0:512], gsil[:], OP.mult), "dve")
                marks[("h_mul", ip, sc_)] = now("dve")
        marks["hfm"] = now("dve")

        if debug:
            @sec("gpsimd")
            def _(g_):
                w(g_, "dve", marks["hfm"])
                inc(g_.dma_start(out=dbg_t["d_hfm"], in_=hfm[:]), "pld", 16)

        p4a.close()

        p4b = ExitStack()
        sb4 = lambda n, s_, d: p4b.enter_context(nc.sbuf_tensor(n, s_, d))
        ps4 = lambda n, s_, d: p4b.enter_context(nc.psum_tensor(n, s_, d))
        wds = sb4("wds", [128, 2 * 44 * 128], BF16)
        ytev = sb4("ytev", [128, 512], F32)
        ysb4 = sb4("ysb4", [128, 4 * D], F32)
        rescol = sb4("rescol", [128, 8], F32)
        h1l = sb4("h1l", [128, D], F32)
        xql = sb4("xql", [128, D], F32)
        ynrm = sb4("ynrm", [128, D], F32)
        pYd = ps4("pYd", [128, 512], F32)
        pYd2 = ps4("pYd2", [128, 512], F32)
        pTr = ps4("pTr", [128, 512], F32)

        for sc_ in range(2):
            for ho in range(16):
                idx = sc_ * 16 + ho

                @sec("sync")
                def _(sy, sc_=sc_, ho=ho, idx=idx):
                    w(sy, "cwd", 1)
                    w(sy, "dve", marks["hfm"])
                    if idx >= 2:
                        w(sy, "pe", marks[("dn_mmB", idx - 2)])
                    wdv = wds[:].rearrange("p (r ic c) -> p r ic c", r=2, ic=44)
                    src = wdT_all.ap().rearrange("(ic p) c -> p ic c", p=128)[:, :, 128 * ho:128 * (ho + 1)]
                    inc(sy.dma_start(out=wdv[:, idx % 2, :, :], in_=src), "sd%d" % (idx % 2), 16)
                marks[("wd_ld", idx)] = now("sd%d" % (idx % 2))

                @sec("tensor")
                def _(t, sc_=sc_, ho=ho, idx=idx):
                    w(t, "sd%d" % (idx % 2), marks[("wd_ld", idx)])
                    if idx >= 2:
                        w(t, "dve", marks[("yt_ev", idx - 2)])
                    wdv = wds[:].rearrange("p (r ic c) -> p r ic c", r=2, ic=44)
                    pp = pYd if idx % 2 == 0 else pYd2
                    for ic in range(44):
                        ins = t.matmul(pp[:, 0:512], wdv[:, idx % 2, ic, :],
                                       hfm[:, QTOK * ic + 512 * sc_:QTOK * ic + 512 * (sc_ + 1)],
                                       start=(ic == 0), stop=(ic == 43))
                    inc(ins, "pe")
                marks[("dn_mmB", idx)] = now("pe")

                @sec("vector")
                def _(v, idx=idx):
                    w(v, "pe", marks[("dn_mmB", idx)])
                    if idx >= 1:
                        w(v, "pe", marks[("yt_tp", idx - 1)])
                    pp = pYd if idx % 2 == 0 else pYd2
                    inc(v.tensor_copy(ytev[:], pp[:, 0:512]), "dve")
                marks[("yt_ev", idx)] = now("dve")

                @sec("tensor")
                def _(t, idx=idx):
                    w(t, "dve", marks[("yt_ev", idx)])
                    if idx >= 1:
                        w(t, "dve", marks[("ytr_ev", idx - 1)])
                    for j in range(4):
                        ins = t.transpose(pTr[:, 128 * j:128 * (j + 1)],
                                          ytev[:, 128 * j:128 * (j + 1)], identf[:])
                    inc(ins, "pe")
                marks[("yt_tp", idx)] = now("pe")

                @sec("vector")
                def _(v, sc_=sc_, ho=ho, idx=idx):
                    w(v, "pe", marks[("yt_tp", idx)])
                    if sc_ > 0 and ho == 0:
                        w(v, "dve", marks[("fin", sc_ - 1, 3)])
                    yv = ysb4[:].rearrange("p (ssl c) -> p ssl c", ssl=4)
                    inc(v.tensor_copy(yv[:, :, 128 * ho:128 * (ho + 1)],
                                      pTr[:, 0:512].rearrange("p (ssl c) -> p ssl c", ssl=4)), "dve")
                marks[("ytr_ev", idx)] = now("dve")

            for ssl in range(4):
                ss = 4 * sc_ + ssl

                @sec("sync")
                def _(sy, ss=ss, sc_=sc_, ssl=ssl):
                    if not (sc_ == 0 and ssl == 0):
                        prevss = ss - 1
                        w(sy, "dve", marks[("fin", prevss // 4, prevss % 4)])
                    inc(sy.dma_start(out=h1l[:], in_=attn_d.ap()[128 * ss:128 * (ss + 1), :]), "sp", 16)
                    inc(sy.dma_start(out=xql[:], in_=x_q[128 * ss:128 * (ss + 1), :]), "sp", 16)
                marks[("fin_ld", sc_, ssl)] = now("sp")

                @sec("scalar")
                def _(s_, sc_=sc_, ssl=ssl):
                    w(s_, "dve", marks[("ytr_ev", sc_ * 16 + 15)])
                    if not (sc_ == 0 and ssl == 0):
                        prevss = 4 * sc_ + ssl - 1
                        w(s_, "dve", marks[("fin", prevss // 4, prevss % 4)])
                        w(s_, "pld", marks[("out_st", prevss)])
                    yv = ysb4[:].rearrange("p (s c) -> p s c", s=4)
                    bar(s_, s_.activation(ynrm[:], yv[:, ssl, :], AF.Square,
                                          accum_out=rescol[:, 2 * ssl:2 * ssl + 1]), "act")
                    bar(s_, s_.activation(rescol[:, 2 * ssl + 1:2 * ssl + 2], rescol[:, 2 * ssl:2 * ssl + 1],
                                          AF.Ln, scale=1.0 / D, bias=epsrms[:, 0:1]), "act")
                    inc(s_.activation(rescol[:, 2 * ssl + 1:2 * ssl + 2],
                                      rescol[:, 2 * ssl + 1:2 * ssl + 2], AF.Exp, scale=-0.5), "act")
                marks[("rms", sc_, ssl)] = now("act")

                @sec("vector")
                def _(v, sc_=sc_, ssl=ssl):
                    w(v, "act", marks[("rms", sc_, ssl)])
                    w(v, "sp", marks[("fin_ld", sc_, ssl)])
                    ss2 = 4 * sc_ + ssl
                    if ss2 >= 1:
                        w(v, "pld", marks[("out_st", ss2 - 1)])
                    yv = ysb4[:].rearrange("p (s c) -> p s c", s=4)
                    v.tensor_scalar(ynrm[:], yv[:, ssl, :], rescol[:, 2 * ssl + 1:2 * ssl + 2], None, OP.mult)
                    v.tensor_tensor(ynrm[:], ynrm[:], h1l[:], OP.add)
                    inc(v.tensor_tensor(ynrm[:], ynrm[:], xql[:], OP.add), "dve")
                marks[("fin", sc_, ssl)] = now("dve")

                @sec("gpsimd")
                def _(g_, ss=ss, sc_=sc_, ssl=ssl):
                    w(g_, "dve", marks[("fin", sc_, ssl)])
                    inc(g_.dma_start(out=out_q[128 * ss:128 * (ss + 1), :], in_=ynrm[:]), "pld", 16)
                marks[("out_st", ss)] = now("pld")

        @sec("sync")
        def _(sy):
            w(sy, "pld", now("pld"))

        p4b.close()
        p4base.close()
        pB.close()

    return nc


_NC_CACHE = {}


def _get_nc(debug=False):
    if debug not in _NC_CACHE:
        _NC_CACHE[debug] = build_nc(debug=debug)
    return _NC_CACHE[debug]


def _host_constants():
    import jax
    import jax.numpy as jnp
    try:
        cpu = jax.local_devices(backend="cpu")[0]
        dev_ctx = jax.default_device(cpu)
    except Exception:
        import contextlib
        dev_ctx = contextlib.nullcontext()
    with dev_ctx:
        key = jax.random.key(42, impl="threefry2x32")
        proj = np.asarray(jax.random.normal(key, (DH, JL), jnp.float32)
                          / np.sqrt(JL))
        inv_freq = 1.0 / (10000.0 ** (jnp.arange(0, DH, 2, dtype=jnp.float32) / DH))
        freqs = jnp.outer(jnp.arange(S, dtype=jnp.float32), inv_freq)
        cos = np.asarray(jnp.cos(freqs), np.float32)
        sin = np.asarray(jnp.sin(freqs), np.float32)
    return proj.astype(BF), cos, sin


def kernel(L0_attn_x, L0_attn_Wq, L0_attn_Wk, L0_attn_Wv, L0_attn_Wo,
           L0_attn_q_gain, L0_ffn_gate_up_packed, L0_ffn_gate_up_scales,
           L0_ffn_down_packed, L0_ffn_down_scales, _debug=False):
    x = np.asarray(L0_attn_x)
    Wq = np.asarray(L0_attn_Wq)
    Wk = np.asarray(L0_attn_Wk)
    Wv = np.asarray(L0_attn_Wv)
    Wo = np.asarray(L0_attn_Wo)
    gain = np.asarray(L0_attn_q_gain)
    gup = np.asarray(L0_ffn_gate_up_packed)
    gus = np.asarray(L0_ffn_gate_up_scales)
    dnp = np.asarray(L0_ffn_down_packed)
    dns = np.asarray(L0_ffn_down_scales)

    projb, cos, sin = _host_constants()
    identb = np.eye(128).astype(BF)
    identf = np.eye(128, dtype=np.float32)

    nc = _get_nc(debug=_debug)
    in_maps = []
    for c in range(8):
        b, g = c // 4, c % 4
        dns_rows = np.ascontiguousarray(
            np.stack([dns[(11 * c + s_) // 2] for s_ in range(11)], axis=0))
        in_maps.append(dict(
            x_b=np.ascontiguousarray(x[b]),
            x_q=np.ascontiguousarray(x[b, QTOK * g:QTOK * (g + 1)]),
            wq_h=np.ascontiguousarray(Wq[512 * g:512 * (g + 1)]),
            wk_h=np.ascontiguousarray(Wk[512 * g:512 * (g + 1)]),
            wv_h=np.ascontiguousarray(Wv[512 * g:512 * (g + 1)]),
            wo_s=np.ascontiguousarray(Wo[:, 256 * c:256 * (c + 1)]),
            gup_s=np.ascontiguousarray(gup[:, 32 * c:32 * (c + 1)]),
            gus_s=np.ascontiguousarray(gus[2 * c:2 * (c + 1)]),
            dnp_s=np.ascontiguousarray(dnp[:, 88 * c:88 * (c + 1)]),
            dns_s=dns_rows,
            gain_s=np.ascontiguousarray(gain[4 * g:4 * (g + 1)]).reshape(1, 4).astype(np.float32),
            cos_t=cos, sin_t=sin, proj_c=projb,
            id_b=identb, id_f=identf,
        ))

    res = run_bass_kernel_spmd(nc, in_maps, core_ids=list(range(8)))
    out = np.empty((B, S, D), np.float32)
    for c in range(8):
        b, g = c // 4, c % 4
        out[b, QTOK * g:QTOK * (g + 1)] = np.asarray(res.results[c]["out_q"])
    if _debug:
        kernel._dbg = res.results
    return out
